# version 08
# baseline (speedup 1.0000x reference)
import numpy as np
import ml_dtypes
import concourse.bass as bass
import concourse.mybir as mybir
from concourse.bass_utils import run_bass_kernel_spmd

F32 = mybir.dt.float32
BF16 = mybir.dt.bfloat16
AF = mybir.ActivationFunctionType
ALU = mybir.AluOpType
AXX = mybir.AxisListType.X
BF = ml_dtypes.bfloat16

S, D, DI, NS = 8192, 1024, 2048, 16
TT = 256
NT = S // TT
LARGE = 1.0e6
EPS = 1e-5
RD = 1.0 / D


def build_program(SL=S, dbg=False):
    NTL = SL // TT
    nc = bass.Bass("TRN2", target_bir_lowering=False, detect_race_conditions=False)

    dx = nc.dram_tensor("x", [SL, D], F32, kind="ExternalInput").ap()
    d_wi = nc.dram_tensor("w_inT", [D, 2 * DI], F32, kind="ExternalInput").ap()
    d_bin = nc.dram_tensor("b_in", [128, 32], F32, kind="ExternalInput").ap()
    d_cw = nc.dram_tensor("convw", [128, 64], F32, kind="ExternalInput").ap()
    d_cb = nc.dram_tensor("conv_b", [128, 16], F32, kind="ExternalInput").ap()
    d_xp = nc.dram_tensor("xprojT", [DI, 33], F32, kind="ExternalInput").ap()
    d_xpb = nc.dram_tensor("xproj_b", [33, 1], F32, kind="ExternalInput").ap()
    d_alog = nc.dram_tensor("alog_row", [1, NS], F32, kind="ExternalInput").ap()
    d_dvec = nc.dram_tensor("dvec", [128, 16], F32, kind="ExternalInput").ap()
    d_wo = nc.dram_tensor("w_o_omaj", [8, 16, 128, 128], F32, kind="ExternalInput").ap()
    d_bo = nc.dram_tensor("b_out", [128, 8], F32, kind="ExternalInput").ap()
    d_gb = nc.dram_tensor("gamma_bc", [128, D], F32, kind="ExternalInput").ap()
    d_bb = nc.dram_tensor("beta_bc", [128, D], F32, kind="ExternalInput").ap()
    d_u1 = nc.dram_tensor("u1", [D, 256], F32, kind="ExternalInput").ap()
    d_v1 = nc.dram_tensor("v1", [256, 4096], F32, kind="ExternalInput").ap()
    d_b1 = nc.dram_tensor("b1", [128, 32], F32, kind="ExternalInput").ap()
    d_u2 = nc.dram_tensor("u2", [4096, 256], F32, kind="ExternalInput").ap()
    d_v2 = nc.dram_tensor("v2", [256, D], F32, kind="ExternalInput").ap()
    d_b2 = nc.dram_tensor("b2", [128, 8], F32, kind="ExternalInput").ap()
    d_id32 = nc.dram_tensor("ident32", [128, 128], F32, kind="ExternalInput").ap()
    d_idbf = nc.dram_tensor("identbf", [128, 128], BF16, kind="ExternalInput").ap()
    d_tri = nc.dram_tensor("tri", [128, 128], F32, kind="ExternalInput").ap()
    d_tris = nc.dram_tensor("tris", [128, 128], F32, kind="ExternalInput").ap()
    d_um = nc.dram_tensor("umask", [128, 128], F32, kind="ExternalInput").ap()
    d_ones = nc.dram_tensor("onesrow", [1, 128], F32, kind="ExternalInput").ap()
    d_eps = nc.dram_tensor("epsv", [128, 1], F32, kind="ExternalInput").ap()
    d_onesbf = nc.dram_tensor("onesrowbf", [1, 128], BF16, kind="ExternalInput").ap()
    d_out = nc.dram_tensor("out", [SL, D], F32, kind="ExternalOutput").ap()
    d_x2 = nc.dram_tensor("x2scratch", [SL, D], F32,
                          kind="ExternalOutput" if dbg else "Internal").ap()
    if dbg:
        d_dab = nc.dram_tensor("dbg_ab", [128, 23040], BF16, kind="ExternalOutput").ap()
        d_daf = nc.dram_tensor("dbg_af", [128, 13056], F32, kind="ExternalOutput").ap()
    d_wos = nc.dram_tensor("wo_scr", [8, 16, 128, 128], BF16, kind="Internal").ap()

    wa = nc.alloc_sbuf_tensor("wa", [128, 33312], BF16)
    Wi = wa[:, 0:32768]
    XP = wa[:, 32768:33312]
    U1 = wa[:, 0:2048]
    V1 = wa[:, 2048:10240]
    U2 = wa[:, 10240:18432]
    V2 = wa[:, 18432:20480]

    wob = nc.alloc_sbuf_tensor("wob", [128, 4096], BF16)
    WoB = [wob[:, 0:2048], wob[:, 2048:4096]]

    ab = nc.alloc_sbuf_tensor("ab", [128, 29312], BF16)
    ln1x_tm2 = [ab[:, 0:2048], ab[:, 27264:29312]]
    ln1x_fm = ab[:, 2048:4096]
    ao = ln1x_fm
    x_main = ab[:, 4096:8240]
    zs = ab[:, 8240:12336]
    xc = ab[:, 12336:16432]
    Xt2 = [ab[:, 16432:18480], ab[:, 22848:24896]]
    Mt2 = [ab[:, 18480:20528], ab[:, 24896:26944]]
    h_bf = ab[0:16, 20528:22576]
    Qt2 = [ab[0:16, 22576:22704], ab[0:16, 26944:27072]]
    Kt2 = [ab[:, 27072:27088], ab[:, 27088:27104]]
    ln3_tm = ab[:, 0:2048]
    ln3_fm = ab[:, 2048:4096]
    Pf = ab[:, 4096:4608]
    Hf = ab[:, 4608:12800]
    Qf = ab[:, 12800:13312]
    fo = ab[:, 13312:15360]
    fo_tm = ab[:, 15360:17408]

    af = nc.alloc_sbuf_tensor("af", [128, 10916], F32)
    x_tm = [af[:, 0:2048], af[:, 2048:4096]]
    r_tm = af[:, 4096:6144]
    wst = [af[:, 6144:6656], af[:, 6656:7168]]
    h_T = af[0:16, 7168:9216]
    ssm = af[:, 9216:9284]
    scr = af[:, 9284:9828]
    stt = af[:, 9828:9892]
    cacc = af[:, 9892:10404]
    sc_y1b = af[:, 10404:10532]
    sc_dT1 = af[:, 10532:10660]
    ssm_fm = af[:, 10660:10916]
    sc_dt = scr[:, 0:2]
    sc_t0 = scr[:, 2:4]
    sc_t1 = scr[:, 4:6]
    sc_ctm = scr[:, 6:8]
    sc_r = scr[:, 8:10]
    sc_dT = scr[:, 16:144]
    sc_E = [scr[:, 144:272], scr[:, 272:400]]
    sc_y1 = scr[:, 400:528]
    sc_k0 = scr[:, 528:544]
    small = nc.alloc_sbuf_tensor("small", [128, 832], F32)
    sc_Cf2 = [small[0:16, 0:128], small[0:16, 128:256]]
    sc_E2s = [small[0:16, 256:384], small[0:16, 384:512]]
    sc_crow2 = [small[0:1, 512:640], small[0:1, 640:768]]
    a_col = small[0:16, 768:769]
    sc_dec2 = [small[0:16, 770:771], small[0:16, 771:772]]
    arow = small[0:1, 772:788]
    epsv = small[:, 800:801]
    nc.const_aps.register(F32, EPS, epsv)

    cf = nc.alloc_sbuf_tensor("cf", [128, 1664], F32)
    ID32 = cf[:, 0:128]
    TRI = cf[:, 128:256]
    TRIS = cf[:, 256:384]
    UM = cf[:, 384:512]
    ONES = cf[0:1, 512:640]
    GB = cf[:, 640:1664]
    cf2 = nc.alloc_sbuf_tensor("cf2", [128, 1248], F32)
    BB = cf2[:, 0:1024]
    B_IN = cf2[:, 1024:1056]
    C_B = cf2[:, 1056:1072]
    B_O = cf2[:, 1072:1080]
    DV = cf2[:, 1080:1096]
    B1 = cf2[:, 1096:1128]
    B2 = cf2[:, 1128:1136]
    XPB = cf2[0:33, 1136:1137]
    ARB = cf2[:, 1140:1156]
    CW = cf2[:, 1156:1220]
    cbf = nc.alloc_sbuf_tensor("cbf", [128, 256], BF16)
    IDBF = cbf[:, 0:128]
    ONESBF = cbf[0:1, 128:256]
    _cf0 = nc.alloc_sbuf_tensor("cflat", [1, 2048], F32)
    cflat2 = [_cf0[0:1, 0:2048], _cf0[0:1, 0:2048]]
    _cb0 = nc.alloc_sbuf_tensor("cbflat", [1, 4096], BF16)
    cbflat2 = [_cb0[0:1, 0:2048], _cb0[0:1, 2048:4096]]
    sqdump = nc.alloc_sbuf_tensor("sqdump", [128, 1024], BF16)

    ps_a = nc.alloc_psum_tensor("ps_a", [128, 512], F32)
    ps_b = nc.alloc_psum_tensor("ps_b", [128, 512], F32)
    ps_ab = [ps_a, ps_b]
    ps_t = [nc.alloc_psum_tensor("ps_t1", [128, 1024], BF16),
            nc.alloc_psum_tensor("ps_t2", [128, 1024], BF16)]
    ps_y = [nc.alloc_psum_tensor("ps_y1", [128, 512], F32),
            nc.alloc_psum_tensor("ps_y2", [128, 512], F32)]
    ps_c = [nc.alloc_psum_tensor("ps_c1", [128, 512], F32),
            nc.alloc_psum_tensor("ps_c2", [128, 512], F32)]

    import contextlib
    stx = contextlib.ExitStack()
    SEM, TOT = {}, {}
    for name in ("swc", "sw0", "sw1", "sx0", "sx1", "swo0", "swo1",
                 "sg0", "sg1", "sgw0", "sgw1", "sgc", "st", "sv", "ss"):
        SEM[name] = stx.enter_context(nc.semaphore(name))
        TOT[name] = 0
    DMA_SEMS = ("swc", "sw0", "sw1", "sx0", "sx1", "swo0", "swo1",
                "sg0", "sg1", "sgw0", "sgw1", "sgc")

    sites = []
    lm = {}
    bank = {}

    def site(eng, fn, waits=(), inc=None):
        ws = []
        for w in waits:
            if w is None:
                continue
            sname, v = w
            if v > 0:
                ws.append((SEM[sname], v))
        if inc is not None:
            TOT[inc] += 16 if inc in DMA_SEMS else 1
        sites.append((eng, fn, ws,
                      (SEM[inc], 16 if inc in DMA_SEMS else 1) if inc else None))

    def t(s):
        return TOT[s]

    def mark(key, sem):
        lm[key] = (sem, TOT[sem])

    def wlm(key):
        e = lm.get(key)
        return (e[0], e[1]) if e else None

    def bfree(bk):
        e = bank.get(bk)
        return (e[0], e[1]) if e else None

    def bset(bk, sem):
        bank[bk] = (sem, TOT[sem])

    for (dst, src) in ((ID32, d_id32), (TRI, d_tri), (TRIS, d_tris), (UM, d_um),
                       (ONES, d_ones), (GB, d_gb), (BB, d_bb), (B_IN, d_bin),
                       (C_B, d_cb), (B_O, d_bo), (DV, d_dvec), (B1, d_b1),
                       (B2, d_b2), (XPB, d_xpb), (IDBF, d_idbf), (ONESBF, d_onesbf),
                       (CW, d_cw), (arow, d_alog), (epsv, d_eps)):
        site("sync", lambda e, o=dst, i=src: e.dma_start(out=o, in_=i), inc="swc")
    consts_w = t("swc")

    site("scalar", lambda e: e.activation(arow, arow, AF.Exp),
         waits=[("swc", consts_w)], inc="ss")
    site("vector", lambda e: e.tensor_scalar(arow, arow, -1.0, None, ALU.mult),
         waits=[("ss", t("ss"))], inc="sv")
    av = t("sv")
    site("tensor", lambda e: e.matmul(ps_y[0][:, 0:16], ONES, arow, start=True, stop=True),
         waits=[("sv", av)], inc="st")
    site("tensor", lambda e: e.matmul(ps_y[1][0:16, 0:1], arow, ONES[0:1, 0:1], start=True, stop=True),
         inc="st")
    sv0 = t("st")
    site("vector", lambda e: e.tensor_copy(ARB, ps_y[0][:, 0:16]), waits=[("st", sv0)], inc="sv")
    site("vector", lambda e: e.tensor_copy(a_col, ps_y[1][0:16, 0:1]), inc="sv")
    bset("y0", "sv")
    bset("y1", "sv")
    site("vector", lambda e: e.memset(
        x_main.rearrange("p (c w) -> p c w", w=259)[:, :, 0:3], 0.0), inc="sv")
    site("vector", lambda e: e.memset(h_T, 0.0), inc="sv")
    site("vector", lambda e: e.memset(h_bf, 0.0), inc="sv")
    init_v = t("sv")

    cseq = [0]

    def loadcast(src_ap, dst_ap, ncols, extra_cast_waits=()):
        j = cseq[0]
        stg = wst[j % 2]
        sm = "sw%d" % (j % 2)

        def _dma(e, o=stg, i=src_ap, n=ncols):
            oo = o[:, 0:n]
            if len(i.ap) == 3:
                oo = oo.rearrange("p (c j) -> p c j", c=i.ap[1][1])
            return e.dma_start(out=oo, in_=i)
        site("sync", _dma, waits=[wlm(("stg", (j - 2) % 2))], inc=sm)
        wv = t(sm)
        site("vector", lambda e, o=dst_ap, i=stg, n=ncols: e.tensor_copy(o, i[:, 0:n]),
             waits=[(sm, wv)] + list(extra_cast_waits), inc="sv")
        mark(("stg", j % 2), "sv")
        cseq[0] += 1

    for kc in range(8):
        for q in range(8):
            loadcast(d_wi[kc * 128:(kc + 1) * 128, q * 512:(q + 1) * 512],
                     Wi[:, kc * 4096 + q * 512: kc * 4096 + (q + 1) * 512], 512)
    for hh in range(2):
        j = cseq[0]
        stg = wst[j % 2]
        _sm = "sw%d" % (j % 2)
        site("sync", lambda e, o=stg, hh=hh: e.dma_start(
            out=o[:, 0:264].rearrange("p (c j) -> p c j", j=33),
            in_=d_xp.rearrange("(c p) j -> p c j", p=128)[:, hh * 8:(hh + 1) * 8, :]),
            waits=[wlm(("stg", (j - 2) % 2))], inc=_sm)
        wv = t(_sm)
        site("vector", lambda e, i=stg, hh=hh: e.tensor_copy(
            XP.rearrange("p (c j) -> p c j", j=34)[:, hh * 8:(hh + 1) * 8, 0:33],
            i[:, 0:264].rearrange("p (c j) -> p c j", j=33)),
            waits=[(_sm, wv)], inc="sv")
        mark(("stg", j % 2), "sv")
        cseq[0] += 1
    for o in range(8):
        for hh in range(4):
            j = cseq[0]
            stg = wst[j % 2]
            _sm = "sw%d" % (j % 2)
            site("sync", lambda e, o_=stg, i=d_wo[o, hh * 4:(hh + 1) * 4].rearrange("c p j -> p c j"):
                 e.dma_start(out=o_[:, 0:512].rearrange("p (c j) -> p c j", j=128), in_=i),
                 waits=[wlm(("stg", (j - 2) % 2))], inc=_sm)
            wv = t(_sm)
            dstb = WoB[j % 2][:, 0:512]
            site("vector", lambda e, o_=dstb, i=stg: e.tensor_copy(o_, i[:, 0:512]),
                 waits=[(_sm, wv), wlm(("wob", j % 2))], inc="sv")
            mark(("stg", j % 2), "sv")
            vv = t("sv")
            _gm = "sgw%d" % (j % 2)
            site("gpsimd", lambda e, o_=d_wos[o, hh * 4:(hh + 1) * 4].rearrange("c p j -> p c j"),
                 i=dstb.rearrange("p (c j) -> p c j", j=128): e.dma_start(out=o_, in_=i),
                 waits=[("sv", vv)], inc=_gm)
            mark(("wob", j % 2), _gm)
            cseq[0] += 1
    weightsA_v = t("sv")
    sgw_tot = (t("sgw0"), t("sgw1"))

    def ln_pair_gen(srcs, norm_fns, wait_pre, sb=0):
        for j in (0, 1):
            b = 8 * (j + sb)
            site("vector", lambda e, j=j, b=b: e.reduce_sum(stt[:, b + 0:b + 1], srcs[j], axis=AXX),
                 waits=wait_pre if j == 0 else (), inc="sv")
        yield
        for j in (0, 1):
            b = 8 * (j + sb)
            site("vector", lambda e, j=j, b=b: e.scalar_tensor_tensor(
                sqdump[:, 0:1024], srcs[j], 1.0, srcs[j], ALU.bypass, ALU.mult,
                accum_out=stt[:, b + 1:b + 2]), inc="sv")
        yield
        for j in (0, 1):
            b = 8 * (j + sb)
            site("vector", lambda e, j=j, b=b: e.tensor_scalar(
                stt[:, b + 2:b + 3], stt[:, b + 0:b + 1], RD, None, ALU.mult), inc="sv")
        for j in (0, 1):
            b = 8 * (j + sb)
            site("vector", lambda e, j=j, b=b: e.tensor_tensor(
                stt[:, b + 5:b + 6], stt[:, b + 2:b + 3], stt[:, b + 2:b + 3], ALU.mult), inc="sv")
        yield
        vvs = {}
        for j in (0, 1):
            b = 8 * (j + sb)
            site("vector", lambda e, j=j, b=b: e.scalar_tensor_tensor(
                stt[:, b + 3:b + 4], stt[:, b + 1:b + 2], RD, stt[:, b + 5:b + 6],
                ALU.mult, ALU.subtract), inc="sv")
            vvs[j] = t("sv")
        for j in (0, 1):
            b = 8 * (j + sb)
            site("scalar", lambda e, j=j, b=b: e.activation(
                stt[:, b + 5:b + 6], stt[:, b + 3:b + 4], AF.Sqrt, bias=EPS),
                waits=[("sv", vvs[1])], inc="ss")
        yield
        sv_ = t("ss")
        for j in (0, 1):
            b = 8 * (j + sb)
            site("vector", lambda e, j=j, b=b: e.reciprocal(stt[:, b + 4:b + 5], stt[:, b + 5:b + 6]),
                 waits=[("ss", sv_)] if j == 0 else (), inc="sv")
        yield
        for j in (0, 1):
            b = 8 * (j + sb)
            site("vector", lambda e, j=j, b=b: norm_fns[j](e, stt[:, b + 2:b + 3], stt[:, b + 4:b + 5]), inc="sv")

    def ln_pair(srcs, norm_fns, wait_pre, sb=0):
        for _ in ln_pair_gen(srcs, norm_fns, wait_pre, sb=sb):
            pass

    xh = {}
    PUMPQ = []

    def pump(nsteps=1):
        for _ in range(nsteps):
            while PUMPQ:
                try:
                    next(PUMPQ[0])
                    return
                except StopIteration:
                    PUMPQ.pop(0)
            return

    def pump_rest():
        while PUMPQ:
            try:
                next(PUMPQ[0])
            except StopIteration:
                PUMPQ.pop(0)

    def emit_xload(i):
        if i >= NTL:
            return
        xb = x_tm[i % 2]
        _gp = "sg%d" % (i % 2)
        _xp_ = "sx%d" % (i % 2)
        w = [(_gp, 16 * ((i - 2) // 2 + 1))] if i >= 2 else []
        site("sync", lambda e, i=i, xb=xb: e.dma_start(
            out=xb.rearrange("p (j d) -> p j d", d=1024),
            in_=dx[i * TT:(i + 1) * TT, :].rearrange("(j p) d -> p j d", p=128)),
            waits=w, inc=_xp_)
        xh[i] = t(_xp_)

    def ln1_gen(i):
        xb = x_tm[i % 2]
        lt = ln1x_tm2[i % 2]
        srcs = [xb[:, 0:1024], xb[:, 1024:2048]]

        def mknorm(jj, src_):
            def f(e, mean, inv):
                return e.tensor_scalar(lt[:, jj * 1024:(jj + 1) * 1024], src_,
                                       mean, inv, ALU.subtract, ALU.mult)
            return f
        wpre = [("sx%d" % (i % 2), xh[i]), ("sv", init_v),
                wlm(("lnT", i - 2, 0)), wlm(("lnT", i - 2, 1))]
        yield from ln_pair_gen(srcs, [mknorm(0, srcs[0]), mknorm(1, srcs[1])], wpre, sb=2)
        mark(("ln1", i), "sv")

    def emit_lnT(i):
        lt = ln1x_tm2[i % 2]
        for jj in range(2):
            bk = "t%d" % jj
            wv = [wlm(("ln1", i)), bfree(bk), wlm(("aoT", i - 1))]
            for kc in range(8):
                site("tensor", lambda e, jj=jj, kc=kc, lt=lt: e.transpose(
                    ps_t[jj][:, kc * 128:(kc + 1) * 128],
                    lt[:, jj * 1024 + kc * 128: jj * 1024 + (kc + 1) * 128], IDBF),
                    waits=wv if kc == 0 else (), inc="st" if kc == 7 else None)
            mark(("lnT", i, jj), "st")
            stv = t("st")
            site("scalar", lambda e, jj=jj: e.copy(
                ab[:, 2048:4096].rearrange("p (k s) -> p k s", s=256)[:, :, jj * 128:(jj + 1) * 128],
                ps_t[jj][:, 0:1024].rearrange("p (k s) -> p k s", s=128)),
                waits=[("st", stv)], inc="ss")
            bset(bk, "ss")
        mark(("ln1fm", i), "ss")

    def make_epi(i):
        xb = x_tm[i % 2]
        rr = [r_tm[:, 0:1024], r_tm[:, 1024:2048]]

        def mkinp(rj_):
            def f(e, mean, inv):
                return e.tensor_scalar(rj_, rj_, mean, inv, ALU.subtract, ALU.mult)
            return f

        def gen():
            yield from ln_pair_gen(rr, [mkinp(rr[0]), mkinp(rr[1])], [("sv", t("sv"))], sb=0)
            yield
            for jj in (0, 1):
                site("vector", lambda e, jj=jj: e.tensor_tensor(rr[jj], rr[jj], GB, ALU.mult), inc="sv")
            yield
            for jj in (0, 1):
                site("vector", lambda e, jj=jj: e.tensor_tensor(rr[jj], rr[jj], BB, ALU.add), inc="sv")
            yield
            yield from ln_pair_gen(rr, [mkinp(rr[0]), mkinp(rr[1])], [("sv", t("sv"))], sb=0)
            yield
            for jj in (0, 1):
                site("vector", lambda e, jj=jj: e.tensor_tensor(
                    xb[:, jj * 1024:(jj + 1) * 1024], xb[:, jj * 1024:(jj + 1) * 1024],
                    rr[jj], ALU.add), inc="sv")
            mark(("x2", i), "sv")
            yield
            vv = t("sv")
            site("gpsimd", lambda e, i=i: e.dma_start(
                out=d_x2[i * TT:(i + 1) * TT, :].rearrange("(j p) d -> p j d", p=128),
                in_=xb.rearrange("p (j d) -> p j d", d=1024)),
                waits=[("sv", vv)], inc="sg%d" % (i % 2))
            yield
        return gen()

    emit_xload(0)
    emit_xload(1)
    for _ in ln1_gen(0):
        pass
    mark(("aoT", -1), "st")
    emit_lnT(0)

    for i in range(NTL):
        xb = x_tm[i % 2]
        if i + 1 < NTL:
            PUMPQ.append(ln1_gen(i + 1))

        for o in range(32):
            bk = "ab%d" % (o % 2)
            wv = [wlm(("ln1fm", i)), ("sv", weightsA_v), bfree(bk)]
            if i > 0 and o < 16:
                wv.append(wlm(("outproj", i - 1)))
            if i > 0 and o >= 16:
                wv.append(wlm(("gate", i - 1)))
            pb = ps_ab[o % 2]
            for kc in range(8):
                site("tensor", lambda e, o=o, kc=kc, pb=pb: e.matmul(
                    pb[:, 0:256],
                    Wi[:, kc * 4096 + o * 128: kc * 4096 + (o + 1) * 128],
                    ab[:, 2048 + kc * 256: 2048 + (kc + 1) * 256],
                    start=(kc == 0), stop=(kc == 7)),
                    waits=wv if kc == 0 else (), inc="st" if kc == 7 else None)
            if o == 31:
                mark(("ipg", i), "st")
            stv = t("st")
            if o < 16:
                c = o
                wv2 = [("st", stv), wlm(("halo", i - 1, c))]
                site("scalar", lambda e, c=c, pb=pb: e.activation(
                    x_main[:, c * 259 + 3: c * 259 + 259], pb[:, 0:256],
                    AF.Identity, bias=B_IN[:, c:c + 1]), waits=wv2, inc="ss")
                mark(("xme", i, c), "ss")
            else:
                c = o - 16
                site("scalar", lambda e, c=c, o=o, pb=pb: e.activation(
                    zs[:, c * 256:(c + 1) * 256], pb[:, 0:256],
                    AF.Silu, bias=B_IN[:, o:o + 1]), waits=[("st", stv)], inc="ss")
            mark(("ipe", i, o), "ss")
            if o == 31:
                mark(("zse", i), "ss")
            bset(bk, "ss")

        for cp in range(8):
            c0, c1 = 2 * cp, 2 * cp + 1
            accs = {c0: cacc[:, 0:256], c1: cacc[:, 256:512]}
            wv = [wlm(("xme", i, c0)), wlm(("xme", i, c1)),
                  wlm(("xce", i, c0 - 2)), wlm(("xce", i, c1 - 2))]
            for c in (c0, c1):
                site("vector", lambda e, c=c, acc=accs[c]: e.tensor_scalar(
                    acc, x_main[:, c * 259 + 0: c * 259 + 256],
                    CW[:, c * 4 + 0:c * 4 + 1], None, ALU.mult),
                    waits=wv if c == c0 else (), inc="sv")
            for k in range(1, 4):
                for c in (c0, c1):
                    site("vector", lambda e, c=c, k=k, acc=accs[c]: e.scalar_tensor_tensor(
                        acc, x_main[:, c * 259 + k: c * 259 + k + 256],
                        CW[:, c * 4 + k:c * 4 + k + 1], acc, ALU.mult, ALU.add), inc="sv")
            for c in (c0, c1):
                site("vector", lambda e, c=c: e.tensor_copy(
                    x_main[:, c * 259: c * 259 + 3],
                    x_main[:, c * 259 + 256: c * 259 + 259]), inc="sv")
                mark(("halo", i, c), "sv")
            vv = t("sv")
            for c in (c0, c1):
                site("scalar", lambda e, c=c, acc=accs[c]: e.activation(
                    xc[:, c * 256:(c + 1) * 256], acc, AF.Silu, bias=C_B[:, c:c + 1]),
                    waits=[("sv", vv)], inc="ss")
                mark(("xce", i, c), "ss")

        for kc in range(16):
            wv = [wlm(("xce", i, kc))]
            if kc == 0:
                wv.append(bfree("ab0"))
            site("tensor", lambda e, kc=kc: e.matmul(
                ps_a[0:33, 0:256], XP[:, kc * 34: kc * 34 + 33],
                xc[:, kc * 256:(kc + 1) * 256],
                start=(kc == 0), stop=(kc == 15)),
                waits=wv, inc="st" if kc == 15 else None)
        stv = t("st")
        site("scalar", lambda e: e.activation(ssm_fm[0:33, 0:256], ps_a[0:33, 0:256],
                                              AF.Identity, bias=XPB), waits=[("st", stv)], inc="ss")
        ssmfm_s = t("ss")
        bset("ab0", "ss")

        dt_v = {}
        for sc in range(2):
            site("tensor", lambda e, sc=sc: e.transpose(
                ps_y[sc][:, 0:33], ssm_fm[0:33, sc * 128:(sc + 1) * 128], ID32[0:33, 0:33]),
                waits=[("ss", ssmfm_s), bfree("y%d" % sc)], inc="st")
            stv = t("st")
            site("vector", lambda e, sc=sc: e.tensor_copy(ssm[:, sc * 34: sc * 34 + 33], ps_y[sc][:, 0:33]),
                 waits=[("st", stv)], inc="sv")
            bset("y%d" % sc, "sv")
            site("scalar", lambda e, sc=sc: e.dma_start(
                out=cflat2[sc], in_=ssm_fm[16:32, sc * 128:(sc + 1) * 128]),
                waits=[("ss", ssmfm_s), wlm(("cbfu",))], inc="sgc")
            cfl_g = t("sgc")
            site("scalar", lambda e, sc=sc: e.copy(cbflat2[sc], cflat2[sc]),
                 waits=[("sgc", cfl_g), wlm(("cbc", sc))], inc="ss")
            mark(("cfl", sc), "ss")
            mark(("cbfu",), "ss")
            vv = t("sv")
            site("scalar", lambda e, sc=sc: e.activation(sc_t0[:, sc:sc + 1], ssm[:, sc * 34 + 32:sc * 34 + 33], AF.Abs),
                 waits=[("sv", vv)], inc="ss")
            site("scalar", lambda e, sc=sc: e.activation(sc_t1[:, sc:sc + 1], ssm[:, sc * 34 + 32:sc * 34 + 33], AF.Relu), inc="ss")
            site("scalar", lambda e, sc=sc: e.activation(sc_t0[:, sc:sc + 1], sc_t0[:, sc:sc + 1], AF.Exp, scale=-1.0),
                 waits=[("ss", t("ss"))], inc="ss")
            site("scalar", lambda e, sc=sc: e.activation(sc_t0[:, sc:sc + 1], sc_t0[:, sc:sc + 1], AF.Ln, bias=1.0),
                 waits=[("ss", t("ss"))], inc="ss")
            sv_ = t("ss")
            site("vector", lambda e, sc=sc: e.tensor_tensor(sc_dt[:, sc:sc + 1], sc_t0[:, sc:sc + 1], sc_t1[:, sc:sc + 1], ALU.add),
                 waits=[("ss", sv_)], inc="sv")
            dt_v[sc] = t("sv")
            pump()

        xt_v = {}
        for sc in range(2):
            for half in range(2):
                bk = "t%d" % half
                for q in range(8):
                    kc = half * 8 + q
                    wv = [wlm(("xce", i, kc))]
                    if q == 0:
                        wv.append(bfree(bk))
                    site("tensor", lambda e, kc=kc, half=half, sc=sc, q=q: e.transpose(
                        ps_t[half][:, q * 128:(q + 1) * 128],
                        xc[:, kc * 256 + sc * 128: kc * 256 + sc * 128 + 128], IDBF),
                        waits=wv, inc="st" if q == 7 else None)
                stv = t("st")
                site("vector", lambda e, half=half, sc=sc: e.tensor_scalar(
                    Xt2[sc][:, half * 1024:(half + 1) * 1024], ps_t[half][:, 0:1024],
                    sc_dt[:, sc:sc + 1], None, ALU.mult),
                    waits=[("st", stv), ("sv", dt_v[sc])], inc="sv")
                bset(bk, "sv")
            xt_v[sc] = t("sv")
            pump()

        dT_v = {}
        kt_v = {}
        dec_s = {}
        for sc in range(2):
            dtap = sc_dt[:, sc:sc + 1]
            site("tensor", lambda e, dtap=dtap: e.matmul(ps_a[:, 0:1], TRI, dtap, start=True, stop=True),
                 waits=[("sv", dt_v[sc]), bfree("ab0")], inc="st")
            site("tensor", lambda e, dtap=dtap: e.matmul(ps_b[0:1, 0:128], dtap, TRI, start=True, stop=True),
                 waits=[bfree("ab1")], inc="st")
            site("tensor", lambda e, dtap=dtap: e.matmul(ps_c[0][:, 0:1], TRIS, dtap, start=True, stop=True),
                 waits=[bfree("c0")], inc="st")
            site("tensor", lambda e, sc=sc: e.transpose(ps_c[1][0:16, 0:128],
                                                        ssm[:, sc * 34 + 16: sc * 34 + 32], ID32),
                 waits=[bfree("c1")], inc="st")
            stv = t("st")
            site("vector", lambda e, sc=sc: e.tensor_copy(sc_ctm[:, sc:sc + 1], ps_a[:, 0:1]),
                 waits=[("st", stv)], inc="sv")
            bset("ab0", "sv")
            site("vector", lambda e, sc=sc: e.tensor_copy(sc_crow2[sc], ps_b[0:1, 0:128]), inc="sv")
            bset("ab1", "sv")
            site("vector", lambda e, sc=sc: e.tensor_copy(sc_r[:, sc:sc + 1], ps_c[0][:, 0:1]), inc="sv")
            bset("c0", "sv")
            site("vector", lambda e, sc=sc: e.tensor_copy(sc_Cf2[sc], ps_c[1][0:16, 0:128]), inc="sv")
            bset("c1", "sv")
            ev1 = t("sv")
            site("tensor", lambda e, sc=sc: e.matmul(ps_a[:, 0:128], ONES, sc_crow2[sc], start=True, stop=True),
                 waits=[("sv", ev1), bfree("ab0")], inc="st")
            site("tensor", lambda e, sc=sc: e.matmul(ps_b[0:16, 0:128], ONES[0:1, 0:16], sc_crow2[sc], start=True, stop=True),
                 waits=[bfree("ab1")], inc="st")
            site("tensor", lambda e, sc=sc: e.matmul(ps_b[0:16, 128:129], ONES[0:1, 0:16], sc_crow2[sc][0:1, 127:128],
                                                     start=True, stop=True), inc="st")
            stv = t("st")
            dTt = sc_dT if sc == 0 else sc_dT1
            site("vector", lambda e, sc=sc, dTt=dTt: e.scalar_tensor_tensor(
                dTt, ps_a[:, 0:128], sc_ctm[:, sc:sc + 1], UM, ALU.subtract, ALU.add),
                waits=[("st", stv)], inc="sv")
            bset("ab0", "sv")
            dT_v[sc] = t("sv")
            site("scalar", lambda e, sc=sc: e.activation(sc_E2s[sc], ps_b[0:16, 0:128], AF.Exp, scale=a_col),
                 waits=[("st", stv)], inc="ss")
            site("scalar", lambda e, sc=sc: e.activation(sc_dec2[sc], ps_b[0:16, 128:129], AF.Exp, scale=a_col), inc="ss")
            dec_s[sc] = t("ss")
            bset("ab1", "ss")
            sv_ = t("ss")
            site("vector", lambda e, sc=sc: e.tensor_tensor(Qt2[sc], sc_E2s[sc], sc_Cf2[sc], ALU.mult),
                 waits=[("ss", sv_)], inc="sv")
            site("vector", lambda e, sc=sc: e.tensor_scalar(sc_k0, ARB, sc_r[:, sc:sc + 1], None, ALU.mult),
                 waits=[("sv", t("sv"))], inc="sv")
            vv = t("sv")
            site("scalar", lambda e: e.activation(sc_k0, sc_k0, AF.Exp), waits=[("sv", vv)], inc="ss")
            sv_ = t("ss")
            site("vector", lambda e, sc=sc: e.tensor_tensor(Kt2[sc], sc_k0, ssm[:, sc * 34: sc * 34 + 16], ALU.mult),
                 waits=[("ss", sv_), ("sv", t("sv"))], inc="sv")
            kt_v[sc] = t("sv")
            pump()

        CBANKS = [("ab0", ps_a), ("ab1", ps_b), ("c0", ps_c[0]), ("c1", ps_c[1])]

        def emit_mblock(sc):
            dTt = sc_dT if sc == 0 else sc_dT1
            for n in range(16):
                bk, pb = CBANKS[n % 4]
                col = (n // 4) * 128
                site("tensor", lambda e, sc=sc, n=n, pb=pb, col=col: e.matmul(
                    pb[:, col:col + 128], ONESBF, cbflat2[sc][0:1, n * 128:(n + 1) * 128],
                    start=True, stop=True),
                    waits=[bfree(bk) if n < 4 else None, wlm(("cfl", sc))], inc="st")
                mark(("cbst", sc, n), "st")
            mark(("cbc", sc), "st")
            for n in range(16):
                ns = sc * 16 + n
                site("scalar", lambda e, sc=sc, n=n, dTt=dTt: e.activation(
                    sc_E[n % 2], dTt, AF.Exp, scale=ARB[:, n:n + 1]),
                    waits=[("sv", dT_v[sc]), wlm(("mteg", i, ns - 2))], inc="ss")
                mark(("exps", sc, n), "ss")

        def emit_mstt(sc, n):
            ns = sc * 16 + n
            bk, pb = CBANKS[n % 4]
            col = (n // 4) * 128
            site("vector", lambda e, sc=sc, n=n, pb=pb, col=col: e.scalar_tensor_tensor(
                Mt2[sc][:, n * 128:(n + 1) * 128], sc_E[n % 2],
                ssm[:, sc * 34 + n: sc * 34 + n + 1], pb[:, col:col + 128], ALU.mult, ALU.mult),
                waits=[wlm(("exps", sc, n)), wlm(("cbst", sc, n))], inc="sv")
            mark(("mteg", i, ns), "sv")
            mark(("mte", i, sc, n), "sv")
            if n >= 12:
                bset(CBANKS[n % 4][0], "sv")

        emit_mblock(0)
        for n in range(16):
            emit_mstt(0, n)
            if n % 4 == 3:
                pump()
        emit_mblock(1)

        def emit_intra_chunk(sc, interleave_next):
            hbf_w = wlm(("hbf",))
            for dc in range(16):
                bk = "y%d" % (dc % 2)
                pb = ps_y[dc % 2]
                for n in range(16):
                    wv = []
                    if n == 0:
                        wv = [hbf_w, bfree(bk), ("sv", xt_v[sc])]
                    wv.append(wlm(("mte", i, sc, n)))
                    site("tensor", lambda e, sc=sc, dc=dc, n=n, pb=pb: e.matmul(
                        pb[:, 0:128], Xt2[sc][:, dc * 128:(dc + 1) * 128],
                        Mt2[sc][:, n * 128:(n + 1) * 128], start=(n == 0), stop=False),
                        waits=wv, inc=None)
                site("tensor", lambda e, sc=sc, dc=dc, pb=pb: e.matmul(
                    pb[:, 0:128], h_bf[0:16, dc * 128:(dc + 1) * 128], Qt2[sc],
                    start=False, stop=True), inc="st")
                if dc == 15:
                    mark(("inter", i, sc), "st")
                stv = t("st")
                y1b = sc_y1 if dc % 2 == 0 else sc_y1b
                site("vector", lambda e, dc=dc, sc=sc, pb=pb, y1b=y1b: e.scalar_tensor_tensor(
                    y1b, xc[:, dc * 256 + sc * 128: dc * 256 + sc * 128 + 128],
                    DV[:, dc:dc + 1], pb[:, 0:128], ALU.mult, ALU.add),
                    waits=[("st", stv), wlm(("zse", i))], inc="sv")
                site("vector", lambda e, dc=dc, sc=sc, y1b=y1b: e.tensor_tensor(
                    x_main[:, dc * 259 + 3 + sc * 128: dc * 259 + 3 + sc * 128 + 128],
                    y1b, zs[:, dc * 256 + sc * 128: dc * 256 + sc * 128 + 128], ALU.mult),
                    waits=[("sv", t("sv"))], inc="sv")
                bset(bk, "sv")
                if interleave_next:
                    emit_mstt(1, dc)
                else:
                    pump()
            if sc == 1:
                mark(("gate", i), "sv")

        def emit_state(sc):
            for q in range(4):
                bk = "c%d" % (q % 2)
                pb = ps_c[q % 2]
                site("tensor", lambda e, sc=sc, q=q, pb=pb: e.matmul(
                    pb[0:16, 0:512], Kt2[sc], Xt2[sc][:, q * 512:(q + 1) * 512], start=True, stop=True),
                    waits=[("sv", kt_v[sc]), ("sv", xt_v[sc]), bfree(bk)], inc="st")
                stv = t("st")
                site("vector", lambda e, sc=sc, q=q, pb=pb: e.scalar_tensor_tensor(
                    h_T[0:16, q * 512:(q + 1) * 512], h_T[0:16, q * 512:(q + 1) * 512],
                    sc_dec2[sc], pb[0:16, 0:512], ALU.mult, ALU.add),
                    waits=[("st", stv), ("ss", dec_s[sc])], inc="sv")
                bset(bk, "sv")
            site("vector", lambda e: e.tensor_copy(h_bf, h_T),
                 waits=[wlm(("inter", i, sc)), ("sv", t("sv"))], inc="sv")
            mark(("hbf",), "sv")

        emit_intra_chunk(0, True)
        emit_state(0)
        emit_intra_chunk(1, False)
        emit_state(1)

        for o in range(8):
            _wm = "swo%d" % (o % 2)
            site("sync", lambda e, o=o: e.dma_start(
                out=WoB[o % 2].rearrange("p (c j) -> p c j", j=128),
                in_=d_wos[o].rearrange("c p j -> p c j")),
                waits=[wlm(("opg", i, o - 2)) or wlm(("opg", i - 1, o + 6)),
                       ("sgw0", sgw_tot[0]), ("sgw1", sgw_tot[1])], inc=_wm)
            wo_here = t(_wm)
            bk = "ab%d" % (o % 2)
            pb = ps_ab[o % 2]
            wv = [(_wm, wo_here), wlm(("gate", i)), bfree(bk)]
            for kc in range(16):
                site("tensor", lambda e, o=o, kc=kc, pb=pb: e.matmul(
                    pb[:, 0:256], WoB[o % 2][:, kc * 128:(kc + 1) * 128],
                    x_main[:, kc * 259 + 3: kc * 259 + 259],
                    start=(kc == 0), stop=(kc == 15)),
                    waits=wv if kc == 0 else (), inc="st" if kc == 15 else None)
            mark(("opg", i, o), "st")
            stv = t("st")
            site("scalar", lambda e, o=o, pb=pb: e.activation(
                ao[:, o * 256:(o + 1) * 256], pb[:, 0:256], AF.Identity, bias=B_O[:, o:o + 1]),
                waits=[("st", stv)], inc="ss")
            mark(("aoe", i, o), "ss")
            bset(bk, "ss")
            pump()
        mark(("outproj", i), "st")

        stv0 = stv1 = None
        for jj in range(2):
            bk = "t%d" % jj
            wv = [wlm(("aoe", i, 7)), bfree(bk)]
            for o in range(8):
                site("tensor", lambda e, jj=jj, o=o: e.transpose(
                    ps_t[jj][:, o * 128:(o + 1) * 128],
                    ao[:, o * 256 + jj * 128: o * 256 + jj * 128 + 128], IDBF),
                    waits=wv if o == 0 else (), inc="st" if o == 7 else None)
            if jj == 0:
                stv0 = t("st")
            else:
                mark(("aoT", i), "st")
                stv1 = t("st")

        pump_rest()
        lt = ln1x_tm2[i % 2]
        rr = [r_tm[:, 0:1024], r_tm[:, 1024:2048]]
        site("vector", lambda e, lt=lt, rr=rr: e.tensor_tensor(
            rr[0], ps_t[0][:, 0:1024], lt[:, 0:1024], ALU.add),
            waits=[("st", stv0)], inc="sv")
        site("vector", lambda e, lt=lt, rr=rr: e.tensor_tensor(
            rr[1], ps_t[1][:, 0:1024], lt[:, 1024:2048], ALU.add),
            waits=[("st", stv1)], inc="sv")
        bset("t0", "sv")
        bset("t1", "sv")
        if i + 1 < NTL:
            emit_lnT(i + 1)
        PUMPQ.append(make_epi(i))
        emit_xload(i + 2)
    pump_rest()
    lastA_ip = lm[("ipg", NTL - 1)]
    gA = (t("sg0"), t("sg1"))

    jobsB = []
    for kc in range(8):
        jobsB.append((d_u1[kc * 128:(kc + 1) * 128, :], U1[:, kc * 256:(kc + 1) * 256], 256))
    for r in range(2):
        for q in range(8):
            jobsB.append((d_v1[r * 128:(r + 1) * 128, q * 512:(q + 1) * 512],
                          V1[:, r * 4096 + q * 512: r * 4096 + (q + 1) * 512], 512))
    for g in range(16):
        jobsB.append((d_u2[g * 256:(g + 1) * 256, :].rearrange("(c p) j -> p c j", p=128),
                      U2[:, g * 512:(g + 1) * 512], 512))
    for r in range(2):
        for q in range(2):
            jobsB.append((d_v2[r * 128:(r + 1) * 128, q * 512:(q + 1) * 512],
                          V2[:, r * 1024 + q * 512: r * 1024 + (q + 1) * 512], 512))
    for (srcap, dstap, ncols) in jobsB:
        loadcast(srcap, dstap, ncols, extra_cast_waits=[("st", lastA_ip[1])])
    weightsB_v = t("sv")

    xhB = {}

    def emit_x2load(i):
        if i >= NTL:
            return
        xb = x_tm[i % 2]
        _gp = "sg%d" % (i % 2)
        _xp_ = "sx%d" % (i % 2)
        w = [("sg0", gA[0]), ("sg1", gA[1])]
        if i >= 2:
            w.append((_gp, t(_gp)))
        site("sync", lambda e, i=i, xb=xb: e.dma_start(
            out=xb.rearrange("p (j d) -> p j d", d=1024),
            in_=d_x2[i * TT:(i + 1) * TT, :].rearrange("(j p) d -> p j d", p=128)),
            waits=w, inc=_xp_)
        xhB[i] = t(_xp_)

    def emit_ln3(i):
        xb = x_tm[i % 2]
        srcs = [xb[:, 0:1024], xb[:, 1024:2048]]

        def mknorm3(jj, src_):
            def f(e, mean, inv):
                return e.tensor_scalar(ln3_tm[:, jj * 1024:(jj + 1) * 1024], src_,
                                       mean, inv, ALU.subtract, ALU.mult)
            return f
        wpre = [("sx%d" % (i % 2), xhB[i]), ("sv", weightsB_v), wlm(("ln3u", i - 1))]
        ln_pair(srcs, [mknorm3(0, srcs[0]), mknorm3(1, srcs[1])], wpre, sb=2)
        mark(("ln3", i), "sv")

    def emit_lnT3(i):
        for jj in range(2):
            bk = "t%d" % jj
            wv = [wlm(("ln3", i)), bfree(bk)]
            for kc in range(8):
                site("tensor", lambda e, jj=jj, kc=kc: e.transpose(
                    ps_t[jj][:, kc * 128:(kc + 1) * 128],
                    ln3_tm[:, jj * 1024 + kc * 128: jj * 1024 + (kc + 1) * 128], IDBF),
                    waits=wv if kc == 0 else (), inc="st" if kc == 7 else None)
            stv = t("st")
            site("scalar", lambda e, jj=jj: e.copy(
                ab[:, 2048:4096].rearrange("p (k s) -> p k s", s=256)[:, :, jj * 128:(jj + 1) * 128],
                ps_t[jj][:, 0:1024].rearrange("p (k s) -> p k s", s=128)),
                waits=[("st", stv)], inc="ss")
            bset(bk, "ss")
        mark(("ln3fm", i), "ss")

    emit_x2load(0)
    emit_x2load(1)
    emit_ln3(0)
    emit_lnT3(0)

    for i in range(NTL):
        xb = x_tm[i % 2]

        for r in range(2):
            bk = "ab%d" % (r % 2)
            pb = ps_ab[r % 2]
            wv = [wlm(("ln3fm", i)), bfree(bk)]
            for kc in range(8):
                site("tensor", lambda e, r=r, kc=kc, pb=pb: e.matmul(
                    pb[:, 0:256], U1[:, kc * 256 + r * 128: kc * 256 + r * 128 + 128],
                    ab[:, 2048 + kc * 256: 2048 + (kc + 1) * 256],
                    start=(kc == 0), stop=(kc == 7)),
                    waits=wv if kc == 0 else (), inc="st" if kc == 7 else None)
            stv = t("st")
            site("scalar", lambda e, r=r, pb=pb: e.copy(
                Pf[:, r * 256:(r + 1) * 256], pb[:, 0:256]), waits=[("st", stv)], inc="ss")
            bset(bk, "ss")
        pf_s = t("ss")

        for f in range(32):
            bk = "ab%d" % (f % 2)
            pb = ps_ab[f % 2]
            wv = [("ss", pf_s), bfree(bk)]
            for r in range(2):
                site("tensor", lambda e, f=f, r=r, pb=pb: e.matmul(
                    pb[:, 0:256], V1[:, r * 4096 + f * 128: r * 4096 + (f + 1) * 128],
                    Pf[:, r * 256:(r + 1) * 256], start=(r == 0), stop=(r == 1)),
                    waits=wv if r == 0 else (), inc="st" if r == 1 else None)
            stv = t("st")
            site("scalar", lambda e, f=f, pb=pb: e.activation(
                Hf[:, f * 256:(f + 1) * 256], pb[:, 0:256], AF.Gelu_apprx_tanh,
                bias=B1[:, f:f + 1]), waits=[("st", stv)], inc="ss")
            mark(("hfe", i, f), "ss")
            bset(bk, "ss")

        for r in range(2):
            bk = "ab%d" % (r % 2)
            pb = ps_ab[r % 2]
            wv = [wlm(("hfe", i, 31)), bfree(bk)]
            for kc in range(32):
                site("tensor", lambda e, r=r, kc=kc, pb=pb: e.matmul(
                    pb[:, 0:256], U2[:, kc * 256 + r * 128: kc * 256 + r * 128 + 128],
                    Hf[:, kc * 256:(kc + 1) * 256], start=(kc == 0), stop=(kc == 31)),
                    waits=wv if kc == 0 else (), inc="st" if kc == 31 else None)
            stv = t("st")
            site("scalar", lambda e, r=r, pb=pb: e.copy(
                Qf[:, r * 256:(r + 1) * 256], pb[:, 0:256]), waits=[("st", stv)], inc="ss")
            bset(bk, "ss")
        qf_s = t("ss")

        for o in range(8):
            bk = "ab%d" % (o % 2)
            pb = ps_ab[o % 2]
            wv = [("ss", qf_s), bfree(bk)]
            for r in range(2):
                site("tensor", lambda e, o=o, r=r, pb=pb: e.matmul(
                    pb[:, 0:256], V2[:, r * 1024 + o * 128: r * 1024 + (o + 1) * 128],
                    Qf[:, r * 256:(r + 1) * 256], start=(r == 0), stop=(r == 1)),
                    waits=wv if r == 0 else (), inc="st" if r == 1 else None)
            stv = t("st")
            site("scalar", lambda e, o=o, pb=pb: e.activation(
                fo[:, o * 256:(o + 1) * 256], pb[:, 0:256], AF.Identity, bias=B2[:, o:o + 1]),
                waits=[("st", stv)], inc="ss")
            mark(("foe", i, o), "ss")
            bset(bk, "ss")

        stv0 = stv1 = None
        for jj in range(2):
            bk = "t%d" % jj
            wv = [wlm(("foe", i, 7)), bfree(bk)]
            for o in range(8):
                site("tensor", lambda e, jj=jj, o=o: e.transpose(
                    ps_t[jj][:, o * 128:(o + 1) * 128],
                    fo[:, o * 256 + jj * 128: o * 256 + jj * 128 + 128], IDBF),
                    waits=wv if o == 0 else (), inc="st" if o == 7 else None)
            if jj == 0:
                stv0 = t("st")
            else:
                stv1 = t("st")
        site("vector", lambda e: e.tensor_copy(fo_tm[:, 0:1024], ps_t[0][:, 0:1024]),
             waits=[("st", stv0)], inc="sv")
        site("vector", lambda e: e.tensor_copy(fo_tm[:, 1024:2048], ps_t[1][:, 0:1024]),
             waits=[("st", stv1)], inc="sv")
        bset("t0", "sv")
        bset("t1", "sv")

        if i + 1 < NTL:
            emit_ln3(i + 1)
            emit_lnT3(i + 1)

        fos = [fo_tm[:, 0:1024], fo_tm[:, 1024:2048]]
        NRM = (r_tm[:, 0:1024], r_tm[:, 1024:2048])

        def mknorm4(jj, src_):
            def f(e, mean, inv):
                return e.tensor_scalar(NRM[jj][:, 0:1024], src_, mean, inv, ALU.subtract, ALU.mult)
            return f
        ln_pair(fos, [mknorm4(0, fos[0]), mknorm4(1, fos[1])], [("sv", t("sv"))], sb=0)
        for jj in (0, 1):
            site("vector", lambda e, jj=jj, xb=xb: e.tensor_tensor(
                xb[:, jj * 1024:(jj + 1) * 1024], xb[:, jj * 1024:(jj + 1) * 1024],
                NRM[jj][:, 0:1024], ALU.add), inc="sv")
        mark(("ln3u", i), "sv")
        vv = t("sv")
        site("gpsimd", lambda e, i=i, xb=xb: e.dma_start(
            out=d_out[i * TT:(i + 1) * TT, :].rearrange("(j p) d -> p j d", p=128),
            in_=xb.rearrange("p (j d) -> p j d", d=1024)),
            waits=[("sv", vv)], inc="sg%d" % (i % 2))
        emit_x2load(i + 2)

    site("gpsimd", lambda e: e.engine_nop(),
         waits=[("sg0", t("sg0")), ("sg1", t("sg1")), ("sgc", t("sgc"))])

    CONSTW = consts_w
    with nc.Block() as block:
        def mk(name):
            def f(e):
                seen = {}
                if name in ("tensor", "vector", "scalar", "gpsimd"):
                    e.wait_ge(SEM["swc"], CONSTW)
                    seen[id(SEM["swc"])] = CONSTW
                for (eng, fn, ws, inc) in sites:
                    if eng != name:
                        continue
                    for (sh, v) in ws:
                        if seen.get(id(sh), -1) < v:
                            e.wait_ge(sh, v)
                            seen[id(sh)] = v
                    inst = fn(e)
                    if inc is not None:
                        inst.then_inc(inc[0], inc[1])
            return f
        block.sync(mk("sync"))
        block.tensor(mk("tensor"))
        block.vector(mk("vector"))
        block.scalar(mk("scalar"))
        block.gpsimd(mk("gpsimd"))
    return nc


def _host_inputs(inputs):
    f32 = np.float32
    A = np.asarray(inputs["L0_attn_A_log"], f32)
    if not (A == A[0]).all():
        raise RuntimeError("A_log rows differ; general-A fallback not implemented")
    conv_w = np.asarray(inputs["L0_attn_conv_w"], f32)
    ident = np.eye(128, dtype=f32)
    u = np.arange(128)
    return {
        "w_inT": np.ascontiguousarray(np.asarray(inputs["L0_attn_in_proj_w"], f32).T),
        "b_in": np.ascontiguousarray(np.asarray(inputs["L0_attn_in_proj_b"], f32).reshape(32, 128).T),
        "convw": np.ascontiguousarray(conv_w[:, 0, :].reshape(16, 128, 4).transpose(1, 0, 2).reshape(128, 64)),
        "conv_b": np.ascontiguousarray(np.asarray(inputs["L0_attn_conv_b"], f32).reshape(16, 128).T),
        "xprojT": np.ascontiguousarray(np.asarray(inputs["L0_attn_x_proj_w"], f32).T),
        "xproj_b": np.ascontiguousarray(np.asarray(inputs["L0_attn_x_proj_b"], f32).reshape(33, 1)),
        "alog_row": np.ascontiguousarray(A[0:1, :]),
        "dvec": np.ascontiguousarray(np.asarray(inputs["L0_attn_D"], f32).reshape(16, 128).T),
        "w_o_omaj": np.ascontiguousarray(
            np.asarray(inputs["L0_attn_out_proj_w"], f32).reshape(8, 128, 16, 128).transpose(0, 2, 3, 1)),
        "b_out": np.ascontiguousarray(np.asarray(inputs["L0_attn_out_proj_b"], f32).reshape(8, 128).T),
        "gamma_bc": np.ascontiguousarray(np.tile(np.asarray(inputs["L0_attn_ln_gamma"], f32)[None, :], (128, 1))),
        "beta_bc": np.ascontiguousarray(np.tile(np.asarray(inputs["L0_attn_ln_beta"], f32)[None, :], (128, 1))),
        "u1": np.ascontiguousarray(np.asarray(inputs["L0_ffn_U1"], f32)),
        "v1": np.ascontiguousarray(np.asarray(inputs["L0_ffn_V1"], f32)),
        "b1": np.ascontiguousarray(np.asarray(inputs["L0_ffn_b1"], f32).reshape(32, 128).T),
        "u2": np.ascontiguousarray(np.asarray(inputs["L0_ffn_U2"], f32)),
        "v2": np.ascontiguousarray(np.asarray(inputs["L0_ffn_V2"], f32)),
        "b2": np.ascontiguousarray(np.asarray(inputs["L0_ffn_b2"], f32).reshape(8, 128).T),
        "ident32": ident,
        "identbf": ident.astype(BF),
        "tri": np.ascontiguousarray((u[:, None] <= u[None, :]).astype(f32)),
        "tris": np.ascontiguousarray((u[:, None] > u[None, :]).astype(f32)),
        "umask": np.ascontiguousarray((LARGE * (u[:, None] > u[None, :])).astype(f32)),
        "onesrow": np.ones((1, 128), f32),
        "epsv": np.full((128, 1), EPS, f32),
        "onesrowbf": np.ones((1, 128), BF),
    }


def kernel(**inputs):
    x = np.asarray(inputs["L0_attn_x"], np.float32)
    base = _host_inputs(inputs)
    nc = build_program()
    in_maps = [dict(base, x=np.ascontiguousarray(x[b])) for b in range(8)]
    res = run_bass_kernel_spmd(nc, in_maps, core_ids=list(range(8)))
    out = np.stack([np.asarray(res.results[b]["out"]) for b in range(8)], axis=0)
    return out.astype(np.float32)


# version 09
# speedup vs baseline: 1.0292x; 1.0292x over previous
import numpy as np
import ml_dtypes
import concourse.bass as bass
import concourse.mybir as mybir
from concourse.bass_utils import run_bass_kernel_spmd

F32 = mybir.dt.float32
BF16 = mybir.dt.bfloat16
AF = mybir.ActivationFunctionType
ALU = mybir.AluOpType
AXX = mybir.AxisListType.X
BF = ml_dtypes.bfloat16

S, D, DI, NS = 8192, 1024, 2048, 16
TT = 256
NT = S // TT
LARGE = 1.0e6
EPS = 1e-5
RD = 1.0 / D


def build_program(SL=S, dbg=False):
    NTL = SL // TT
    nc = bass.Bass("TRN2", target_bir_lowering=False, detect_race_conditions=False)

    dx = nc.dram_tensor("x", [SL, D], F32, kind="ExternalInput").ap()
    d_wi = nc.dram_tensor("w_inT", [D, 2 * DI], F32, kind="ExternalInput").ap()
    d_bin = nc.dram_tensor("b_in", [128, 32], F32, kind="ExternalInput").ap()
    d_cw = nc.dram_tensor("convw", [128, 64], F32, kind="ExternalInput").ap()
    d_cb = nc.dram_tensor("conv_b", [128, 16], F32, kind="ExternalInput").ap()
    d_xp = nc.dram_tensor("xprojT", [DI, 33], F32, kind="ExternalInput").ap()
    d_xpb = nc.dram_tensor("xproj_b", [33, 1], F32, kind="ExternalInput").ap()
    d_alog = nc.dram_tensor("alog_row", [1, NS], F32, kind="ExternalInput").ap()
    d_dvec = nc.dram_tensor("dvec", [128, 16], F32, kind="ExternalInput").ap()
    d_wo = nc.dram_tensor("w_o_omaj", [8, 16, 128, 128], F32, kind="ExternalInput").ap()
    d_bo = nc.dram_tensor("b_out", [128, 8], F32, kind="ExternalInput").ap()
    d_gb = nc.dram_tensor("gamma_bc", [128, D], F32, kind="ExternalInput").ap()
    d_bb = nc.dram_tensor("beta_bc", [128, D], F32, kind="ExternalInput").ap()
    d_u1 = nc.dram_tensor("u1", [D, 256], F32, kind="ExternalInput").ap()
    d_v1 = nc.dram_tensor("v1", [256, 4096], F32, kind="ExternalInput").ap()
    d_b1 = nc.dram_tensor("b1", [128, 32], F32, kind="ExternalInput").ap()
    d_u2 = nc.dram_tensor("u2", [4096, 256], F32, kind="ExternalInput").ap()
    d_v2 = nc.dram_tensor("v2", [256, D], F32, kind="ExternalInput").ap()
    d_b2 = nc.dram_tensor("b2", [128, 8], F32, kind="ExternalInput").ap()
    d_id32 = nc.dram_tensor("ident32", [128, 128], F32, kind="ExternalInput").ap()
    d_idbf = nc.dram_tensor("identbf", [128, 128], BF16, kind="ExternalInput").ap()
    d_tri = nc.dram_tensor("tri", [128, 128], F32, kind="ExternalInput").ap()
    d_tris = nc.dram_tensor("tris", [128, 128], F32, kind="ExternalInput").ap()
    d_um = nc.dram_tensor("umask", [128, 128], F32, kind="ExternalInput").ap()
    d_ones = nc.dram_tensor("onesrow", [1, 128], F32, kind="ExternalInput").ap()
    d_eps = nc.dram_tensor("epsv", [128, 1], F32, kind="ExternalInput").ap()
    d_onesbf = nc.dram_tensor("onesrowbf", [1, 128], BF16, kind="ExternalInput").ap()
    d_out = nc.dram_tensor("out", [SL, D], F32, kind="ExternalOutput").ap()
    d_x2 = nc.dram_tensor("x2scratch", [SL, D], F32,
                          kind="ExternalOutput" if dbg else "Internal").ap()
    if dbg:
        d_dab = nc.dram_tensor("dbg_ab", [128, 23040], BF16, kind="ExternalOutput").ap()
        d_daf = nc.dram_tensor("dbg_af", [128, 13056], F32, kind="ExternalOutput").ap()
    d_wos = nc.dram_tensor("wo_scr", [8, 16, 128, 128], BF16, kind="Internal").ap()

    wa = nc.alloc_sbuf_tensor("wa", [128, 33312], BF16)
    Wi = wa[:, 0:32768]
    XP = wa[:, 32768:33312]
    U1 = wa[:, 0:2048]
    V1 = wa[:, 2048:10240]
    U2 = wa[:, 10240:18432]
    V2 = wa[:, 18432:20480]

    wob = nc.alloc_sbuf_tensor("wob", [128, 4096], BF16)
    WoB = [wob[:, 0:2048], wob[:, 2048:4096]]

    ab = nc.alloc_sbuf_tensor("ab", [128, 29312], BF16)
    ln1x_tm2 = [ab[:, 0:2048], ab[:, 27264:29312]]
    ln1x_fm = ab[:, 2048:4096]
    ao = ln1x_fm
    x_main = ab[:, 4096:8240]
    zs = ab[:, 8240:12336]
    xc = ab[:, 12336:16432]
    Xt2 = [ab[:, 16432:18480], ab[:, 22848:24896]]
    Mt2 = [ab[:, 18480:20528], ab[:, 24896:26944]]
    h_bf = ab[0:16, 20528:22576]
    Qt2 = [ab[0:16, 22576:22704], ab[0:16, 26944:27072]]
    Kt2 = [ab[:, 27072:27088], ab[:, 27088:27104]]
    ln3_tm = ab[:, 0:2048]
    ln3_fm = ab[:, 2048:4096]
    Pf = ab[:, 4096:4608]
    Hf = ab[:, 4608:12800]
    Qf = ab[:, 12800:13312]
    fo = ab[:, 13312:15360]
    fo_tm = ab[:, 15360:17408]

    af = nc.alloc_sbuf_tensor("af", [128, 10916], F32)
    x_tm = [af[:, 0:2048], af[:, 2048:4096]]
    r_tm = af[:, 4096:6144]
    wst = [af[:, 6144:6656], af[:, 6656:7168]]
    h_T = af[0:16, 7168:9216]
    ssm = af[:, 9216:9284]
    scr = af[:, 9284:9828]
    stt = af[:, 9828:9892]
    cacc = af[:, 9892:10404]
    sc_y1b = af[:, 10404:10532]
    sc_dT1 = af[:, 10532:10660]
    ssm_fm = af[:, 10660:10916]
    sc_dt = scr[:, 0:2]
    sc_t0 = scr[:, 2:4]
    sc_t1 = scr[:, 4:6]
    sc_ctm = scr[:, 6:8]
    sc_r = scr[:, 8:10]
    sc_dT = scr[:, 16:144]
    sc_E = [scr[:, 144:272], scr[:, 272:400]]
    sc_y1 = scr[:, 400:528]
    sc_k0 = scr[:, 528:544]
    small = nc.alloc_sbuf_tensor("small", [128, 832], F32)
    sc_Cf2 = [small[0:16, 0:128], small[0:16, 128:256]]
    sc_E2s = [small[0:16, 256:384], small[0:16, 384:512]]
    sc_crow2 = [small[0:1, 512:640], small[0:1, 640:768]]
    a_col = small[0:16, 768:769]
    sc_dec2 = [small[0:16, 770:771], small[0:16, 771:772]]
    arow = small[0:1, 772:788]
    epsv = small[:, 800:801]
    nc.const_aps.register(F32, EPS, epsv)

    cf = nc.alloc_sbuf_tensor("cf", [128, 1664], F32)
    ID32 = cf[:, 0:128]
    TRI = cf[:, 128:256]
    TRIS = cf[:, 256:384]
    UM = cf[:, 384:512]
    ONES = cf[0:1, 512:640]
    GB = cf[:, 640:1664]
    cf2 = nc.alloc_sbuf_tensor("cf2", [128, 1248], F32)
    BB = cf2[:, 0:1024]
    B_IN = cf2[:, 1024:1056]
    C_B = cf2[:, 1056:1072]
    B_O = cf2[:, 1072:1080]
    DV = cf2[:, 1080:1096]
    B1 = cf2[:, 1096:1128]
    B2 = cf2[:, 1128:1136]
    XPB = cf2[0:33, 1136:1137]
    ARB = cf2[:, 1140:1156]
    CW = cf2[:, 1156:1220]
    cbf = nc.alloc_sbuf_tensor("cbf", [128, 256], BF16)
    IDBF = cbf[:, 0:128]
    ONESBF = cbf[0:1, 128:256]
    _cf0 = nc.alloc_sbuf_tensor("cflat", [1, 2048], F32)
    cflat2 = [_cf0[0:1, 0:2048], _cf0[0:1, 0:2048]]
    _cb0 = nc.alloc_sbuf_tensor("cbflat", [1, 4096], BF16)
    cbflat2 = [_cb0[0:1, 0:2048], _cb0[0:1, 2048:4096]]
    sqdump = nc.alloc_sbuf_tensor("sqdump", [128, 1024], BF16)

    ps_a = nc.alloc_psum_tensor("ps_a", [128, 512], F32)
    ps_b = nc.alloc_psum_tensor("ps_b", [128, 512], F32)
    ps_ab = [ps_a, ps_b]
    ps_t = [nc.alloc_psum_tensor("ps_t1", [128, 1024], BF16),
            nc.alloc_psum_tensor("ps_t2", [128, 1024], BF16)]
    ps_y = [nc.alloc_psum_tensor("ps_y1", [128, 512], F32),
            nc.alloc_psum_tensor("ps_y2", [128, 512], F32)]
    ps_c = [nc.alloc_psum_tensor("ps_c1", [128, 512], F32),
            nc.alloc_psum_tensor("ps_c2", [128, 512], F32)]

    import contextlib
    stx = contextlib.ExitStack()
    SEM, TOT = {}, {}
    for name in ("swc", "sw0", "sw1", "sx0", "sx1", "swo0", "swo1",
                 "sg0", "sg1", "sgw0", "sgw1", "sgc", "st", "sv", "ss"):
        SEM[name] = stx.enter_context(nc.semaphore(name))
        TOT[name] = 0
    DMA_SEMS = ("swc", "sw0", "sw1", "sx0", "sx1", "swo0", "swo1",
                "sg0", "sg1", "sgw0", "sgw1", "sgc")

    sites = []
    lm = {}
    bank = {}

    def site(eng, fn, waits=(), inc=None):
        ws = []
        for w in waits:
            if w is None:
                continue
            sname, v = w
            if v > 0:
                ws.append((SEM[sname], v))
        if inc is not None:
            TOT[inc] += 16 if inc in DMA_SEMS else 1
        sites.append((eng, fn, ws,
                      (SEM[inc], 16 if inc in DMA_SEMS else 1) if inc else None))

    def t(s):
        return TOT[s]

    def mark(key, sem):
        lm[key] = (sem, TOT[sem])

    def wlm(key):
        e = lm.get(key)
        return (e[0], e[1]) if e else None

    def bfree(bk):
        e = bank.get(bk)
        return (e[0], e[1]) if e else None

    def bset(bk, sem):
        bank[bk] = (sem, TOT[sem])

    for (dst, src) in ((ID32, d_id32), (TRI, d_tri), (TRIS, d_tris), (UM, d_um),
                       (ONES, d_ones), (GB, d_gb), (BB, d_bb), (B_IN, d_bin),
                       (C_B, d_cb), (B_O, d_bo), (DV, d_dvec), (B1, d_b1),
                       (B2, d_b2), (XPB, d_xpb), (IDBF, d_idbf), (ONESBF, d_onesbf),
                       (CW, d_cw), (arow, d_alog), (epsv, d_eps)):
        site("sync", lambda e, o=dst, i=src: e.dma_start(out=o, in_=i), inc="swc")
    consts_w = t("swc")

    site("scalar", lambda e: e.activation(arow, arow, AF.Exp),
         waits=[("swc", consts_w)], inc="ss")
    site("vector", lambda e: e.tensor_scalar(arow, arow, -1.0, None, ALU.mult),
         waits=[("ss", t("ss"))], inc="sv")
    av = t("sv")
    site("tensor", lambda e: e.matmul(ps_y[0][:, 0:16], ONES, arow, start=True, stop=True),
         waits=[("sv", av)], inc="st")
    site("tensor", lambda e: e.matmul(ps_y[1][0:16, 0:1], arow, ONES[0:1, 0:1], start=True, stop=True),
         inc="st")
    sv0 = t("st")
    site("vector", lambda e: e.tensor_copy(ARB, ps_y[0][:, 0:16]), waits=[("st", sv0)], inc="sv")
    site("vector", lambda e: e.tensor_copy(a_col, ps_y[1][0:16, 0:1]), inc="sv")
    bset("y0", "sv")
    bset("y1", "sv")
    site("vector", lambda e: e.memset(
        x_main.rearrange("p (c w) -> p c w", w=259)[:, :, 0:3], 0.0), inc="sv")
    site("vector", lambda e: e.memset(h_T, 0.0), inc="sv")
    site("vector", lambda e: e.memset(h_bf, 0.0), inc="sv")
    init_v = t("sv")

    cseq = [0]

    def loadcast(src_ap, dst_ap, ncols, extra_cast_waits=()):
        j = cseq[0]
        stg = wst[j % 2]
        sm = "sw%d" % (j % 2)

        def _dma(e, o=stg, i=src_ap, n=ncols):
            oo = o[:, 0:n]
            if len(i.ap) == 3:
                oo = oo.rearrange("p (c j) -> p c j", c=i.ap[1][1])
            return e.dma_start(out=oo, in_=i)
        site("sync", _dma, waits=[wlm(("stg", (j - 2) % 2))], inc=sm)
        wv = t(sm)
        site("vector", lambda e, o=dst_ap, i=stg, n=ncols: e.tensor_copy(o, i[:, 0:n]),
             waits=[(sm, wv)] + list(extra_cast_waits), inc="sv")
        mark(("stg", j % 2), "sv")
        cseq[0] += 1

    for kc in range(8):
        for q in range(8):
            loadcast(d_wi[kc * 128:(kc + 1) * 128, q * 512:(q + 1) * 512],
                     Wi[:, kc * 4096 + q * 512: kc * 4096 + (q + 1) * 512], 512)
    for hh in range(2):
        j = cseq[0]
        stg = wst[j % 2]
        _sm = "sw%d" % (j % 2)
        site("sync", lambda e, o=stg, hh=hh: e.dma_start(
            out=o[:, 0:264].rearrange("p (c j) -> p c j", j=33),
            in_=d_xp.rearrange("(c p) j -> p c j", p=128)[:, hh * 8:(hh + 1) * 8, :]),
            waits=[wlm(("stg", (j - 2) % 2))], inc=_sm)
        wv = t(_sm)
        site("vector", lambda e, i=stg, hh=hh: e.tensor_copy(
            XP.rearrange("p (c j) -> p c j", j=34)[:, hh * 8:(hh + 1) * 8, 0:33],
            i[:, 0:264].rearrange("p (c j) -> p c j", j=33)),
            waits=[(_sm, wv)], inc="sv")
        mark(("stg", j % 2), "sv")
        cseq[0] += 1
    weightsA_v = t("sv")
    for o in range(8):
        for hh in range(4):
            j = cseq[0]
            stg = wst[j % 2]
            _sm = "sw%d" % (j % 2)
            site("sync", lambda e, o_=stg, i=d_wo[o, hh * 4:(hh + 1) * 4].rearrange("c p j -> p c j"):
                 e.dma_start(out=o_[:, 0:512].rearrange("p (c j) -> p c j", j=128), in_=i),
                 waits=[wlm(("stg", (j - 2) % 2))], inc=_sm)
            wv = t(_sm)
            dstb = WoB[j % 2][:, 0:512]
            site("vector", lambda e, o_=dstb, i=stg: e.tensor_copy(o_, i[:, 0:512]),
                 waits=[(_sm, wv), wlm(("wob", j % 2))], inc="sv")
            mark(("stg", j % 2), "sv")
            vv = t("sv")
            _gm = "sgw%d" % (j % 2)
            site("gpsimd", lambda e, o_=d_wos[o, hh * 4:(hh + 1) * 4].rearrange("c p j -> p c j"),
                 i=dstb.rearrange("p (c j) -> p c j", j=128): e.dma_start(out=o_, in_=i),
                 waits=[("sv", vv)], inc=_gm)
            mark(("wob", j % 2), _gm)
            cseq[0] += 1
    sgw_tot = (t("sgw0"), t("sgw1"))

    def ln_pair_gen(srcs, norm_fns, wait_pre, sb=0):
        for j in (0, 1):
            b = 8 * (j + sb)
            site("vector", lambda e, j=j, b=b: e.reduce_sum(stt[:, b + 0:b + 1], srcs[j], axis=AXX),
                 waits=wait_pre if j == 0 else (), inc="sv")
        yield
        for j in (0, 1):
            b = 8 * (j + sb)
            site("vector", lambda e, j=j, b=b: e.scalar_tensor_tensor(
                sqdump[:, 0:1024], srcs[j], 1.0, srcs[j], ALU.bypass, ALU.mult,
                accum_out=stt[:, b + 1:b + 2]), inc="sv")
        yield
        for j in (0, 1):
            b = 8 * (j + sb)
            site("vector", lambda e, j=j, b=b: e.tensor_scalar(
                stt[:, b + 2:b + 3], stt[:, b + 0:b + 1], RD, None, ALU.mult), inc="sv")
        for j in (0, 1):
            b = 8 * (j + sb)
            site("vector", lambda e, j=j, b=b: e.tensor_tensor(
                stt[:, b + 5:b + 6], stt[:, b + 2:b + 3], stt[:, b + 2:b + 3], ALU.mult), inc="sv")
        yield
        vvs = {}
        for j in (0, 1):
            b = 8 * (j + sb)
            site("vector", lambda e, j=j, b=b: e.scalar_tensor_tensor(
                stt[:, b + 3:b + 4], stt[:, b + 1:b + 2], RD, stt[:, b + 5:b + 6],
                ALU.mult, ALU.subtract), inc="sv")
            vvs[j] = t("sv")
        for j in (0, 1):
            b = 8 * (j + sb)
            site("scalar", lambda e, j=j, b=b: e.activation(
                stt[:, b + 5:b + 6], stt[:, b + 3:b + 4], AF.Sqrt, bias=EPS),
                waits=[("sv", vvs[1])], inc="ss")
        yield
        sv_ = t("ss")
        for j in (0, 1):
            b = 8 * (j + sb)
            site("vector", lambda e, j=j, b=b: e.reciprocal(stt[:, b + 4:b + 5], stt[:, b + 5:b + 6]),
                 waits=[("ss", sv_)] if j == 0 else (), inc="sv")
        yield
        for j in (0, 1):
            b = 8 * (j + sb)
            site("vector", lambda e, j=j, b=b: norm_fns[j](e, stt[:, b + 2:b + 3], stt[:, b + 4:b + 5]), inc="sv")

    def ln_pair(srcs, norm_fns, wait_pre, sb=0):
        for _ in ln_pair_gen(srcs, norm_fns, wait_pre, sb=sb):
            pass

    xh = {}
    PUMPQ = []

    def pump(nsteps=1):
        for _ in range(nsteps):
            while PUMPQ:
                try:
                    next(PUMPQ[0])
                    return
                except StopIteration:
                    PUMPQ.pop(0)
            return

    def pump_rest():
        while PUMPQ:
            try:
                next(PUMPQ[0])
            except StopIteration:
                PUMPQ.pop(0)

    def emit_xload(i):
        if i >= NTL:
            return
        xb = x_tm[i % 2]
        _gp = "sg%d" % (i % 2)
        _xp_ = "sx%d" % (i % 2)
        w = [(_gp, 16 * ((i - 2) // 2 + 1))] if i >= 2 else []
        site("sync", lambda e, i=i, xb=xb: e.dma_start(
            out=xb.rearrange("p (j d) -> p j d", d=1024),
            in_=dx[i * TT:(i + 1) * TT, :].rearrange("(j p) d -> p j d", p=128)),
            waits=w, inc=_xp_)
        xh[i] = t(_xp_)

    def ln1_gen(i):
        xb = x_tm[i % 2]
        lt = ln1x_tm2[i % 2]
        srcs = [xb[:, 0:1024], xb[:, 1024:2048]]

        def mknorm(jj, src_):
            def f(e, mean, inv):
                return e.tensor_scalar(lt[:, jj * 1024:(jj + 1) * 1024], src_,
                                       mean, inv, ALU.subtract, ALU.mult)
            return f
        wpre = [("sx%d" % (i % 2), xh[i]), ("sv", init_v),
                wlm(("lnT", i - 2, 0)), wlm(("lnT", i - 2, 1))]
        yield from ln_pair_gen(srcs, [mknorm(0, srcs[0]), mknorm(1, srcs[1])], wpre, sb=2)
        mark(("ln1", i), "sv")

    def emit_lnT(i):
        lt = ln1x_tm2[i % 2]
        for jj in range(2):
            bk = "t%d" % jj
            wv = [wlm(("ln1", i)), bfree(bk), wlm(("aoT", i - 1))]
            for kc in range(8):
                site("tensor", lambda e, jj=jj, kc=kc, lt=lt: e.transpose(
                    ps_t[jj][:, kc * 128:(kc + 1) * 128],
                    lt[:, jj * 1024 + kc * 128: jj * 1024 + (kc + 1) * 128], IDBF),
                    waits=wv if kc == 0 else (), inc="st" if kc == 7 else None)
            mark(("lnT", i, jj), "st")
            stv = t("st")
            site("scalar", lambda e, jj=jj: e.copy(
                ab[:, 2048:4096].rearrange("p (k s) -> p k s", s=256)[:, :, jj * 128:(jj + 1) * 128],
                ps_t[jj][:, 0:1024].rearrange("p (k s) -> p k s", s=128)),
                waits=[("st", stv)], inc="ss")
            bset(bk, "ss")
        mark(("ln1fm", i), "ss")

    def make_epi(i):
        xb = x_tm[i % 2]
        rr = [r_tm[:, 0:1024], r_tm[:, 1024:2048]]

        def mkinp(rj_):
            def f(e, mean, inv):
                return e.tensor_scalar(rj_, rj_, mean, inv, ALU.subtract, ALU.mult)
            return f

        def gen():
            yield from ln_pair_gen(rr, [mkinp(rr[0]), mkinp(rr[1])], [("sv", t("sv"))], sb=0)
            yield
            for jj in (0, 1):
                site("vector", lambda e, jj=jj: e.tensor_tensor(rr[jj], rr[jj], GB, ALU.mult), inc="sv")
            yield
            for jj in (0, 1):
                site("vector", lambda e, jj=jj: e.tensor_tensor(rr[jj], rr[jj], BB, ALU.add), inc="sv")
            yield
            yield from ln_pair_gen(rr, [mkinp(rr[0]), mkinp(rr[1])], [("sv", t("sv"))], sb=0)
            yield
            for jj in (0, 1):
                site("vector", lambda e, jj=jj: e.tensor_tensor(
                    xb[:, jj * 1024:(jj + 1) * 1024], xb[:, jj * 1024:(jj + 1) * 1024],
                    rr[jj], ALU.add), inc="sv")
            mark(("x2", i), "sv")
            yield
            vv = t("sv")
            site("gpsimd", lambda e, i=i: e.dma_start(
                out=d_x2[i * TT:(i + 1) * TT, :].rearrange("(j p) d -> p j d", p=128),
                in_=xb.rearrange("p (j d) -> p j d", d=1024)),
                waits=[("sv", vv)], inc="sg%d" % (i % 2))
            yield
        return gen()

    emit_xload(0)
    emit_xload(1)
    for _ in ln1_gen(0):
        pass
    mark(("aoT", -1), "st")
    emit_lnT(0)

    for i in range(NTL):
        xb = x_tm[i % 2]
        if i + 1 < NTL:
            PUMPQ.append(ln1_gen(i + 1))

        for o in range(32):
            bk = "ab%d" % (o % 2)
            wv = [wlm(("ln1fm", i)), ("sv", weightsA_v), bfree(bk)]
            if i > 0 and o < 16:
                wv.append(wlm(("outproj", i - 1)))
            if i > 0 and o >= 16:
                wv.append(wlm(("gate", i - 1)))
            pb = ps_ab[o % 2]
            for kc in range(8):
                site("tensor", lambda e, o=o, kc=kc, pb=pb: e.matmul(
                    pb[:, 0:256],
                    Wi[:, kc * 4096 + o * 128: kc * 4096 + (o + 1) * 128],
                    ab[:, 2048 + kc * 256: 2048 + (kc + 1) * 256],
                    start=(kc == 0), stop=(kc == 7)),
                    waits=wv if kc == 0 else (), inc="st" if kc == 7 else None)
            if o == 31:
                mark(("ipg", i), "st")
            stv = t("st")
            if o < 16:
                c = o
                wv2 = [("st", stv), wlm(("halo", i - 1, c))]
                site("scalar", lambda e, c=c, pb=pb: e.activation(
                    x_main[:, c * 259 + 3: c * 259 + 259], pb[:, 0:256],
                    AF.Identity, bias=B_IN[:, c:c + 1]), waits=wv2, inc="ss")
                mark(("xme", i, c), "ss")
            else:
                c = o - 16
                site("scalar", lambda e, c=c, o=o, pb=pb: e.activation(
                    zs[:, c * 256:(c + 1) * 256], pb[:, 0:256],
                    AF.Silu, bias=B_IN[:, o:o + 1]), waits=[("st", stv)], inc="ss")
            mark(("ipe", i, o), "ss")
            if o == 31:
                mark(("zse", i), "ss")
            bset(bk, "ss")

        for cp in range(8):
            c0, c1 = 2 * cp, 2 * cp + 1
            accs = {c0: sqdump[:, 0:256], c1: sqdump[:, 256:512]}
            wv = [wlm(("xme", i, c0)), wlm(("xme", i, c1)),
                  wlm(("xce", i, c0 - 2)), wlm(("xce", i, c1 - 2))]
            for c in (c0, c1):
                site("vector", lambda e, c=c, acc=accs[c]: e.tensor_scalar(
                    acc, x_main[:, c * 259 + 0: c * 259 + 256],
                    CW[:, c * 4 + 0:c * 4 + 1], None, ALU.mult),
                    waits=wv if c == c0 else (), inc="sv")
            for k in range(1, 4):
                for c in (c0, c1):
                    site("vector", lambda e, c=c, k=k, acc=accs[c]: e.scalar_tensor_tensor(
                        acc, x_main[:, c * 259 + k: c * 259 + k + 256],
                        CW[:, c * 4 + k:c * 4 + k + 1], acc, ALU.mult, ALU.add), inc="sv")
            for c in (c0, c1):
                site("vector", lambda e, c=c: e.tensor_copy(
                    x_main[:, c * 259: c * 259 + 3],
                    x_main[:, c * 259 + 256: c * 259 + 259]), inc="sv")
                mark(("halo", i, c), "sv")
            vv = t("sv")
            for c in (c0, c1):
                site("scalar", lambda e, c=c, acc=accs[c]: e.activation(
                    xc[:, c * 256:(c + 1) * 256], acc, AF.Silu, bias=C_B[:, c:c + 1]),
                    waits=[("sv", vv)], inc="ss")
                mark(("xce", i, c), "ss")

        for kc in range(16):
            wv = [wlm(("xce", i, kc))]
            if kc == 0:
                wv.append(bfree("ab0"))
            site("tensor", lambda e, kc=kc: e.matmul(
                ps_a[0:33, 0:256], XP[:, kc * 34: kc * 34 + 33],
                xc[:, kc * 256:(kc + 1) * 256],
                start=(kc == 0), stop=(kc == 15)),
                waits=wv, inc="st" if kc == 15 else None)
        stv = t("st")
        site("scalar", lambda e: e.activation(ssm_fm[0:33, 0:256], ps_a[0:33, 0:256],
                                              AF.Identity, bias=XPB), waits=[("st", stv)], inc="ss")
        ssmfm_s = t("ss")
        bset("ab0", "ss")

        dt_v = {}
        for sc in range(2):
            site("tensor", lambda e, sc=sc: e.transpose(
                ps_y[sc][:, 0:33], ssm_fm[0:33, sc * 128:(sc + 1) * 128], ID32[0:33, 0:33]),
                waits=[("ss", ssmfm_s), bfree("y%d" % sc)], inc="st")
            stv = t("st")
            site("vector", lambda e, sc=sc: e.tensor_copy(ssm[:, sc * 34: sc * 34 + 33], ps_y[sc][:, 0:33]),
                 waits=[("st", stv)], inc="sv")
            bset("y%d" % sc, "sv")
            site("scalar", lambda e, sc=sc: e.dma_start(
                out=cflat2[sc], in_=ssm_fm[16:32, sc * 128:(sc + 1) * 128]),
                waits=[("ss", ssmfm_s), wlm(("cbfu",))], inc="sgc")
            cfl_g = t("sgc")
            site("scalar", lambda e, sc=sc: e.copy(cbflat2[sc], cflat2[sc]),
                 waits=[("sgc", cfl_g), wlm(("cbc", sc))], inc="ss")
            mark(("cfl", sc), "ss")
            mark(("cbfu",), "ss")
            vv = t("sv")
            site("scalar", lambda e, sc=sc: e.activation(sc_t0[:, sc:sc + 1], ssm[:, sc * 34 + 32:sc * 34 + 33], AF.Abs),
                 waits=[("sv", vv)], inc="ss")
            site("scalar", lambda e, sc=sc: e.activation(sc_t1[:, sc:sc + 1], ssm[:, sc * 34 + 32:sc * 34 + 33], AF.Relu), inc="ss")
            site("scalar", lambda e, sc=sc: e.activation(sc_t0[:, sc:sc + 1], sc_t0[:, sc:sc + 1], AF.Exp, scale=-1.0),
                 waits=[("ss", t("ss"))], inc="ss")
            site("scalar", lambda e, sc=sc: e.activation(sc_t0[:, sc:sc + 1], sc_t0[:, sc:sc + 1], AF.Ln, bias=1.0),
                 waits=[("ss", t("ss"))], inc="ss")
            sv_ = t("ss")
            site("vector", lambda e, sc=sc: e.tensor_tensor(sc_dt[:, sc:sc + 1], sc_t0[:, sc:sc + 1], sc_t1[:, sc:sc + 1], ALU.add),
                 waits=[("ss", sv_)], inc="sv")
            dt_v[sc] = t("sv")
            pump()

        xt_v = {}
        for sc in range(2):
            for half in range(2):
                bk = "t%d" % half
                for q in range(8):
                    kc = half * 8 + q
                    wv = [wlm(("xce", i, kc))]
                    if q == 0:
                        wv.append(bfree(bk))
                    site("tensor", lambda e, kc=kc, half=half, sc=sc, q=q: e.transpose(
                        ps_t[half][:, q * 128:(q + 1) * 128],
                        xc[:, kc * 256 + sc * 128: kc * 256 + sc * 128 + 128], IDBF),
                        waits=wv, inc="st" if q == 7 else None)
                stv = t("st")
                site("vector", lambda e, half=half, sc=sc: e.tensor_scalar(
                    Xt2[sc][:, half * 1024:(half + 1) * 1024], ps_t[half][:, 0:1024],
                    sc_dt[:, sc:sc + 1], None, ALU.mult),
                    waits=[("st", stv), ("sv", dt_v[sc])], inc="sv")
                bset(bk, "sv")
            xt_v[sc] = t("sv")
            pump()

        dT_v = {}
        kt_v = {}
        dec_s = {}
        for sc in range(2):
            dtap = sc_dt[:, sc:sc + 1]
            site("tensor", lambda e, dtap=dtap: e.matmul(ps_a[:, 0:1], TRI, dtap, start=True, stop=True),
                 waits=[("sv", dt_v[sc]), bfree("ab0")], inc="st")
            site("tensor", lambda e, dtap=dtap: e.matmul(ps_b[0:1, 0:128], dtap, TRI, start=True, stop=True),
                 waits=[bfree("ab1")], inc="st")
            site("tensor", lambda e, dtap=dtap: e.matmul(ps_c[0][:, 0:1], TRIS, dtap, start=True, stop=True),
                 waits=[bfree("c0")], inc="st")
            site("tensor", lambda e, sc=sc: e.transpose(ps_c[1][0:16, 0:128],
                                                        ssm[:, sc * 34 + 16: sc * 34 + 32], ID32),
                 waits=[bfree("c1")], inc="st")
            stv = t("st")
            site("vector", lambda e, sc=sc: e.tensor_copy(sc_ctm[:, sc:sc + 1], ps_a[:, 0:1]),
                 waits=[("st", stv)], inc="sv")
            bset("ab0", "sv")
            site("vector", lambda e, sc=sc: e.tensor_copy(sc_crow2[sc], ps_b[0:1, 0:128]), inc="sv")
            bset("ab1", "sv")
            site("vector", lambda e, sc=sc: e.tensor_copy(sc_r[:, sc:sc + 1], ps_c[0][:, 0:1]), inc="sv")
            bset("c0", "sv")
            site("vector", lambda e, sc=sc: e.tensor_copy(sc_Cf2[sc], ps_c[1][0:16, 0:128]), inc="sv")
            bset("c1", "sv")
            ev1 = t("sv")
            site("tensor", lambda e, sc=sc: e.matmul(ps_a[:, 0:128], ONES, sc_crow2[sc], start=True, stop=True),
                 waits=[("sv", ev1), bfree("ab0")], inc="st")
            site("tensor", lambda e, sc=sc: e.matmul(ps_b[0:16, 0:128], ONES[0:1, 0:16], sc_crow2[sc], start=True, stop=True),
                 waits=[bfree("ab1")], inc="st")
            site("tensor", lambda e, sc=sc: e.matmul(ps_b[0:16, 128:129], ONES[0:1, 0:16], sc_crow2[sc][0:1, 127:128],
                                                     start=True, stop=True), inc="st")
            stv = t("st")
            dTt = sc_dT if sc == 0 else sc_dT1
            site("vector", lambda e, sc=sc, dTt=dTt: e.scalar_tensor_tensor(
                dTt, ps_a[:, 0:128], sc_ctm[:, sc:sc + 1], UM, ALU.subtract, ALU.add),
                waits=[("st", stv)], inc="sv")
            bset("ab0", "sv")
            dT_v[sc] = t("sv")
            site("scalar", lambda e, sc=sc: e.activation(sc_E2s[sc], ps_b[0:16, 0:128], AF.Exp, scale=a_col),
                 waits=[("st", stv)], inc="ss")
            site("scalar", lambda e, sc=sc: e.activation(sc_dec2[sc], ps_b[0:16, 128:129], AF.Exp, scale=a_col), inc="ss")
            dec_s[sc] = t("ss")
            bset("ab1", "ss")
            sv_ = t("ss")
            site("vector", lambda e, sc=sc: e.tensor_tensor(Qt2[sc], sc_E2s[sc], sc_Cf2[sc], ALU.mult),
                 waits=[("ss", sv_)], inc="sv")
            site("vector", lambda e, sc=sc: e.tensor_scalar(sc_k0, ARB, sc_r[:, sc:sc + 1], None, ALU.mult),
                 waits=[("sv", t("sv"))], inc="sv")
            vv = t("sv")
            site("scalar", lambda e: e.activation(sc_k0, sc_k0, AF.Exp), waits=[("sv", vv)], inc="ss")
            sv_ = t("ss")
            site("vector", lambda e, sc=sc: e.tensor_tensor(Kt2[sc], sc_k0, ssm[:, sc * 34: sc * 34 + 16], ALU.mult),
                 waits=[("ss", sv_), ("sv", t("sv"))], inc="sv")
            kt_v[sc] = t("sv")
            pump()

        CBANKS = [("ab0", ps_a), ("ab1", ps_b), ("c0", ps_c[0]), ("c1", ps_c[1])]

        def emit_mblock(sc):
            dTt = sc_dT if sc == 0 else sc_dT1
            for n in range(16):
                bk, pb = CBANKS[n % 4]
                col = (n // 4) * 128
                site("tensor", lambda e, sc=sc, n=n, pb=pb, col=col: e.matmul(
                    pb[:, col:col + 128], ONESBF, cbflat2[sc][0:1, n * 128:(n + 1) * 128],
                    start=True, stop=True),
                    waits=[bfree(bk) if n < 4 else None, wlm(("cfl", sc))], inc="st")
                mark(("cbst", sc, n), "st")
            mark(("cbc", sc), "st")
            for n in range(16):
                ns = sc * 16 + n
                site("scalar", lambda e, sc=sc, n=n, dTt=dTt: e.activation(
                    sc_E[n % 2], dTt, AF.Exp, scale=ARB[:, n:n + 1]),
                    waits=[("sv", dT_v[sc]), wlm(("mteg", i, ns - 2))], inc="ss")
                mark(("exps", sc, n), "ss")

        def emit_mstt(sc, n):
            ns = sc * 16 + n
            bk, pb = CBANKS[n % 4]
            col = (n // 4) * 128
            site("vector", lambda e, sc=sc, n=n, pb=pb, col=col: e.scalar_tensor_tensor(
                Mt2[sc][:, n * 128:(n + 1) * 128], sc_E[n % 2],
                ssm[:, sc * 34 + n: sc * 34 + n + 1], pb[:, col:col + 128], ALU.mult, ALU.mult),
                waits=[wlm(("exps", sc, n)), wlm(("cbst", sc, n))], inc="sv")
            mark(("mteg", i, ns), "sv")
            mark(("mte", i, sc, n), "sv")
            if n >= 12:
                bset(CBANKS[n % 4][0], "sv")

        emit_mblock(0)
        for n in range(16):
            emit_mstt(0, n)
            if n % 4 == 3:
                pump()
        emit_mblock(1)

        def emit_intra_chunk(sc, interleave_next):
            hbf_w = wlm(("hbf",))
            for dc in range(16):
                bk = "y%d" % (dc % 2)
                pb = ps_y[dc % 2]
                for n in range(16):
                    wv = []
                    if n == 0:
                        wv = [hbf_w, bfree(bk), ("sv", xt_v[sc])]
                    wv.append(wlm(("mte", i, sc, n)))
                    site("tensor", lambda e, sc=sc, dc=dc, n=n, pb=pb: e.matmul(
                        pb[:, 0:128], Xt2[sc][:, dc * 128:(dc + 1) * 128],
                        Mt2[sc][:, n * 128:(n + 1) * 128], start=(n == 0), stop=False),
                        waits=wv, inc=None)
                site("tensor", lambda e, sc=sc, dc=dc, pb=pb: e.matmul(
                    pb[:, 0:128], h_bf[0:16, dc * 128:(dc + 1) * 128], Qt2[sc],
                    start=False, stop=True), inc="st")
                if dc == 15:
                    mark(("inter", i, sc), "st")
                stv = t("st")
                y1b = sc_y1 if dc % 2 == 0 else sc_y1b
                site("vector", lambda e, dc=dc, sc=sc, pb=pb, y1b=y1b: e.scalar_tensor_tensor(
                    y1b, xc[:, dc * 256 + sc * 128: dc * 256 + sc * 128 + 128],
                    DV[:, dc:dc + 1], pb[:, 0:128], ALU.mult, ALU.add),
                    waits=[("st", stv), wlm(("zse", i))], inc="sv")
                site("vector", lambda e, dc=dc, sc=sc, y1b=y1b: e.tensor_tensor(
                    x_main[:, dc * 259 + 3 + sc * 128: dc * 259 + 3 + sc * 128 + 128],
                    y1b, zs[:, dc * 256 + sc * 128: dc * 256 + sc * 128 + 128], ALU.mult),
                    waits=[("sv", t("sv"))], inc="sv")
                bset(bk, "sv")
                if interleave_next:
                    emit_mstt(1, dc)
                else:
                    pump()
            if sc == 1:
                mark(("gate", i), "sv")

        def emit_state(sc):
            for q in range(4):
                bk = "c%d" % (q % 2)
                pb = ps_c[q % 2]
                site("tensor", lambda e, sc=sc, q=q, pb=pb: e.matmul(
                    pb[0:16, 0:512], Kt2[sc], Xt2[sc][:, q * 512:(q + 1) * 512], start=True, stop=True),
                    waits=[("sv", kt_v[sc]), ("sv", xt_v[sc]), bfree(bk)], inc="st")
                stv = t("st")
                site("vector", lambda e, sc=sc, q=q, pb=pb: e.scalar_tensor_tensor(
                    h_T[0:16, q * 512:(q + 1) * 512], h_T[0:16, q * 512:(q + 1) * 512],
                    sc_dec2[sc], pb[0:16, 0:512], ALU.mult, ALU.add),
                    waits=[("st", stv), ("ss", dec_s[sc])], inc="sv")
                bset(bk, "sv")
            site("vector", lambda e: e.tensor_copy(h_bf, h_T),
                 waits=[wlm(("inter", i, sc)), ("sv", t("sv"))], inc="sv")
            mark(("hbf",), "sv")

        emit_intra_chunk(0, True)
        emit_state(0)
        emit_intra_chunk(1, False)
        emit_state(1)

        for o in range(8):
            _wm = "swo%d" % (o % 2)
            site("sync", lambda e, o=o: e.dma_start(
                out=WoB[o % 2].rearrange("p (c j) -> p c j", j=128),
                in_=d_wos[o].rearrange("c p j -> p c j")),
                waits=[wlm(("opg", i, o - 2)) or wlm(("opg", i - 1, o + 6)),
                       ("sgw0", sgw_tot[0]), ("sgw1", sgw_tot[1])], inc=_wm)
            wo_here = t(_wm)
            bk = "ab%d" % (o % 2)
            pb = ps_ab[o % 2]
            wv = [(_wm, wo_here), wlm(("gate", i)), bfree(bk)]
            for kc in range(16):
                site("tensor", lambda e, o=o, kc=kc, pb=pb: e.matmul(
                    pb[:, 0:256], WoB[o % 2][:, kc * 128:(kc + 1) * 128],
                    x_main[:, kc * 259 + 3: kc * 259 + 259],
                    start=(kc == 0), stop=(kc == 15)),
                    waits=wv if kc == 0 else (), inc="st" if kc == 15 else None)
            mark(("opg", i, o), "st")
            stv = t("st")
            site("scalar", lambda e, o=o, pb=pb: e.activation(
                ao[:, o * 256:(o + 1) * 256], pb[:, 0:256], AF.Identity, bias=B_O[:, o:o + 1]),
                waits=[("st", stv)], inc="ss")
            mark(("aoe", i, o), "ss")
            bset(bk, "ss")
            pump()
        mark(("outproj", i), "st")

        stv0 = stv1 = None
        for jj in range(2):
            bk = "t%d" % jj
            wv = [wlm(("aoe", i, 7)), bfree(bk)]
            for o in range(8):
                site("tensor", lambda e, jj=jj, o=o: e.transpose(
                    ps_t[jj][:, o * 128:(o + 1) * 128],
                    ao[:, o * 256 + jj * 128: o * 256 + jj * 128 + 128], IDBF),
                    waits=wv if o == 0 else (), inc="st" if o == 7 else None)
            if jj == 0:
                stv0 = t("st")
            else:
                mark(("aoT", i), "st")
                stv1 = t("st")

        pump_rest()
        lt = ln1x_tm2[i % 2]
        rr = [r_tm[:, 0:1024], r_tm[:, 1024:2048]]
        site("vector", lambda e, lt=lt, rr=rr: e.tensor_tensor(
            rr[0], ps_t[0][:, 0:1024], lt[:, 0:1024], ALU.add),
            waits=[("st", stv0)], inc="sv")
        site("vector", lambda e, lt=lt, rr=rr: e.tensor_tensor(
            rr[1], ps_t[1][:, 0:1024], lt[:, 1024:2048], ALU.add),
            waits=[("st", stv1)], inc="sv")
        bset("t0", "sv")
        bset("t1", "sv")
        if i + 1 < NTL:
            emit_lnT(i + 1)
        PUMPQ.append(make_epi(i))
        emit_xload(i + 2)
    pump_rest()
    lastA_ip = lm[("ipg", NTL - 1)]
    gA = (t("sg0"), t("sg1"))

    jobsB = []
    for kc in range(8):
        jobsB.append((d_u1[kc * 128:(kc + 1) * 128, :], U1[:, kc * 256:(kc + 1) * 256], 256))
    for r in range(2):
        for q in range(8):
            jobsB.append((d_v1[r * 128:(r + 1) * 128, q * 512:(q + 1) * 512],
                          V1[:, r * 4096 + q * 512: r * 4096 + (q + 1) * 512], 512))
    for g in range(16):
        jobsB.append((d_u2[g * 256:(g + 1) * 256, :].rearrange("(c p) j -> p c j", p=128),
                      U2[:, g * 512:(g + 1) * 512], 512))
    for r in range(2):
        for q in range(2):
            jobsB.append((d_v2[r * 128:(r + 1) * 128, q * 512:(q + 1) * 512],
                          V2[:, r * 1024 + q * 512: r * 1024 + (q + 1) * 512], 512))
    for (srcap, dstap, ncols) in jobsB:
        loadcast(srcap, dstap, ncols, extra_cast_waits=[("st", lastA_ip[1])])
    weightsB_v = t("sv")

    xhB = {}

    def emit_x2load(i):
        if i >= NTL:
            return
        xb = x_tm[i % 2]
        _gp = "sg%d" % (i % 2)
        _xp_ = "sx%d" % (i % 2)
        w = [("sg0", gA[0]), ("sg1", gA[1])]
        if i >= 2:
            w.append((_gp, t(_gp)))
        site("sync", lambda e, i=i, xb=xb: e.dma_start(
            out=xb.rearrange("p (j d) -> p j d", d=1024),
            in_=d_x2[i * TT:(i + 1) * TT, :].rearrange("(j p) d -> p j d", p=128)),
            waits=w, inc=_xp_)
        xhB[i] = t(_xp_)

    def emit_ln3(i):
        xb = x_tm[i % 2]
        srcs = [xb[:, 0:1024], xb[:, 1024:2048]]

        def mknorm3(jj, src_):
            def f(e, mean, inv):
                return e.tensor_scalar(ln3_tm[:, jj * 1024:(jj + 1) * 1024], src_,
                                       mean, inv, ALU.subtract, ALU.mult)
            return f
        wpre = [("sx%d" % (i % 2), xhB[i]), ("sv", weightsB_v), wlm(("ln3u", i - 1))]
        ln_pair(srcs, [mknorm3(0, srcs[0]), mknorm3(1, srcs[1])], wpre, sb=2)
        mark(("ln3", i), "sv")

    def emit_lnT3(i):
        for jj in range(2):
            bk = "t%d" % jj
            wv = [wlm(("ln3", i)), bfree(bk)]
            for kc in range(8):
                site("tensor", lambda e, jj=jj, kc=kc: e.transpose(
                    ps_t[jj][:, kc * 128:(kc + 1) * 128],
                    ln3_tm[:, jj * 1024 + kc * 128: jj * 1024 + (kc + 1) * 128], IDBF),
                    waits=wv if kc == 0 else (), inc="st" if kc == 7 else None)
            stv = t("st")
            site("scalar", lambda e, jj=jj: e.copy(
                ab[:, 2048:4096].rearrange("p (k s) -> p k s", s=256)[:, :, jj * 128:(jj + 1) * 128],
                ps_t[jj][:, 0:1024].rearrange("p (k s) -> p k s", s=128)),
                waits=[("st", stv)], inc="ss")
            bset(bk, "ss")
        mark(("ln3fm", i), "ss")

    emit_x2load(0)
    emit_x2load(1)
    emit_ln3(0)
    emit_lnT3(0)

    for i in range(NTL):
        xb = x_tm[i % 2]

        BB4 = [("ab0", ps_a), ("ab1", ps_b), ("y0", ps_y[0]), ("y1", ps_y[1])]

        for r in range(2):
            bk, pb = BB4[r % 4]
            wv = [wlm(("ln3fm", i)), bfree(bk)]
            for kc in range(8):
                site("tensor", lambda e, r=r, kc=kc, pb=pb: e.matmul(
                    pb[:, 0:256], U1[:, kc * 256 + r * 128: kc * 256 + r * 128 + 128],
                    ab[:, 2048 + kc * 256: 2048 + (kc + 1) * 256],
                    start=(kc == 0), stop=(kc == 7)),
                    waits=wv if kc == 0 else (), inc="st" if kc == 7 else None)
            stv = t("st")
            site("scalar", lambda e, r=r, pb=pb: e.copy(
                Pf[:, r * 256:(r + 1) * 256], pb[:, 0:256]), waits=[("st", stv)], inc="ss")
            bset(bk, "ss")
        pf_s = t("ss")

        for f in range(32):
            bk, pb = BB4[f % 4]
            wv = [("ss", pf_s), bfree(bk)]
            for r in range(2):
                site("tensor", lambda e, f=f, r=r, pb=pb: e.matmul(
                    pb[:, 0:256], V1[:, r * 4096 + f * 128: r * 4096 + (f + 1) * 128],
                    Pf[:, r * 256:(r + 1) * 256], start=(r == 0), stop=(r == 1)),
                    waits=wv if r == 0 else (), inc="st" if r == 1 else None)
            stv = t("st")
            site("scalar", lambda e, f=f, pb=pb: e.activation(
                Hf[:, f * 256:(f + 1) * 256], pb[:, 0:256], AF.Gelu_apprx_tanh,
                bias=B1[:, f:f + 1]), waits=[("st", stv)], inc="ss")
            mark(("hfe", i, f), "ss")
            bset(bk, "ss")

        for r in range(2):
            bk, pb = BB4[r % 4]
            wv = [wlm(("hfe", i, 31)), bfree(bk)]
            for kc in range(32):
                site("tensor", lambda e, r=r, kc=kc, pb=pb: e.matmul(
                    pb[:, 0:256], U2[:, kc * 256 + r * 128: kc * 256 + r * 128 + 128],
                    Hf[:, kc * 256:(kc + 1) * 256], start=(kc == 0), stop=(kc == 31)),
                    waits=wv if kc == 0 else (), inc="st" if kc == 31 else None)
            stv = t("st")
            site("scalar", lambda e, r=r, pb=pb: e.copy(
                Qf[:, r * 256:(r + 1) * 256], pb[:, 0:256]), waits=[("st", stv)], inc="ss")
            bset(bk, "ss")
        qf_s = t("ss")

        for o in range(8):
            bk, pb = BB4[o % 4]
            wv = [("ss", qf_s), bfree(bk)]
            for r in range(2):
                site("tensor", lambda e, o=o, r=r, pb=pb: e.matmul(
                    pb[:, 0:256], V2[:, r * 1024 + o * 128: r * 1024 + (o + 1) * 128],
                    Qf[:, r * 256:(r + 1) * 256], start=(r == 0), stop=(r == 1)),
                    waits=wv if r == 0 else (), inc="st" if r == 1 else None)
            stv = t("st")
            site("scalar", lambda e, o=o, pb=pb: e.activation(
                fo[:, o * 256:(o + 1) * 256], pb[:, 0:256], AF.Identity, bias=B2[:, o:o + 1]),
                waits=[("st", stv)], inc="ss")
            mark(("foe", i, o), "ss")
            bset(bk, "ss")

        stv0 = stv1 = None
        for jj in range(2):
            bk = "t%d" % jj
            wv = [wlm(("foe", i, 7)), bfree(bk)]
            for o in range(8):
                site("tensor", lambda e, jj=jj, o=o: e.transpose(
                    ps_t[jj][:, o * 128:(o + 1) * 128],
                    fo[:, o * 256 + jj * 128: o * 256 + jj * 128 + 128], IDBF),
                    waits=wv if o == 0 else (), inc="st" if o == 7 else None)
            if jj == 0:
                stv0 = t("st")
            else:
                stv1 = t("st")
        site("vector", lambda e: e.tensor_copy(fo_tm[:, 0:1024], ps_t[0][:, 0:1024]),
             waits=[("st", stv0)], inc="sv")
        site("vector", lambda e: e.tensor_copy(fo_tm[:, 1024:2048], ps_t[1][:, 0:1024]),
             waits=[("st", stv1)], inc="sv")
        bset("t0", "sv")
        bset("t1", "sv")

        if i + 1 < NTL:
            emit_ln3(i + 1)
            emit_lnT3(i + 1)

        fos = [fo_tm[:, 0:1024], fo_tm[:, 1024:2048]]
        NRM = (r_tm[:, 0:1024], r_tm[:, 1024:2048])

        def mknorm4(jj, src_):
            def f(e, mean, inv):
                return e.tensor_scalar(NRM[jj][:, 0:1024], src_, mean, inv, ALU.subtract, ALU.mult)
            return f
        ln_pair(fos, [mknorm4(0, fos[0]), mknorm4(1, fos[1])], [("sv", t("sv"))], sb=0)
        for jj in (0, 1):
            site("vector", lambda e, jj=jj, xb=xb: e.tensor_tensor(
                xb[:, jj * 1024:(jj + 1) * 1024], xb[:, jj * 1024:(jj + 1) * 1024],
                NRM[jj][:, 0:1024], ALU.add), inc="sv")
        mark(("ln3u", i), "sv")
        vv = t("sv")
        site("gpsimd", lambda e, i=i, xb=xb: e.dma_start(
            out=d_out[i * TT:(i + 1) * TT, :].rearrange("(j p) d -> p j d", p=128),
            in_=xb.rearrange("p (j d) -> p j d", d=1024)),
            waits=[("sv", vv)], inc="sg%d" % (i % 2))
        emit_x2load(i + 2)

    site("gpsimd", lambda e: e.engine_nop(),
         waits=[("sg0", t("sg0")), ("sg1", t("sg1")), ("sgc", t("sgc"))])

    CONSTW = consts_w
    with nc.Block() as block:
        def mk(name):
            def f(e):
                seen = {}
                if name in ("tensor", "vector", "scalar", "gpsimd"):
                    e.wait_ge(SEM["swc"], CONSTW)
                    seen[id(SEM["swc"])] = CONSTW
                for (eng, fn, ws, inc) in sites:
                    if eng != name:
                        continue
                    for (sh, v) in ws:
                        if seen.get(id(sh), -1) < v:
                            e.wait_ge(sh, v)
                            seen[id(sh)] = v
                    inst = fn(e)
                    if inc is not None:
                        inst.then_inc(inc[0], inc[1])
            return f
        block.sync(mk("sync"))
        block.tensor(mk("tensor"))
        block.vector(mk("vector"))
        block.scalar(mk("scalar"))
        block.gpsimd(mk("gpsimd"))
    return nc


def _host_inputs(inputs):
    f32 = np.float32
    A = np.asarray(inputs["L0_attn_A_log"], f32)
    if not (A == A[0]).all():
        raise RuntimeError("A_log rows differ; general-A fallback not implemented")
    conv_w = np.asarray(inputs["L0_attn_conv_w"], f32)
    ident = np.eye(128, dtype=f32)
    u = np.arange(128)
    return {
        "w_inT": np.ascontiguousarray(np.asarray(inputs["L0_attn_in_proj_w"], f32).T),
        "b_in": np.ascontiguousarray(np.asarray(inputs["L0_attn_in_proj_b"], f32).reshape(32, 128).T),
        "convw": np.ascontiguousarray(conv_w[:, 0, :].reshape(16, 128, 4).transpose(1, 0, 2).reshape(128, 64)),
        "conv_b": np.ascontiguousarray(np.asarray(inputs["L0_attn_conv_b"], f32).reshape(16, 128).T),
        "xprojT": np.ascontiguousarray(np.asarray(inputs["L0_attn_x_proj_w"], f32).T),
        "xproj_b": np.ascontiguousarray(np.asarray(inputs["L0_attn_x_proj_b"], f32).reshape(33, 1)),
        "alog_row": np.ascontiguousarray(A[0:1, :]),
        "dvec": np.ascontiguousarray(np.asarray(inputs["L0_attn_D"], f32).reshape(16, 128).T),
        "w_o_omaj": np.ascontiguousarray(
            np.asarray(inputs["L0_attn_out_proj_w"], f32).reshape(8, 128, 16, 128).transpose(0, 2, 3, 1)),
        "b_out": np.ascontiguousarray(np.asarray(inputs["L0_attn_out_proj_b"], f32).reshape(8, 128).T),
        "gamma_bc": np.ascontiguousarray(np.tile(np.asarray(inputs["L0_attn_ln_gamma"], f32)[None, :], (128, 1))),
        "beta_bc": np.ascontiguousarray(np.tile(np.asarray(inputs["L0_attn_ln_beta"], f32)[None, :], (128, 1))),
        "u1": np.ascontiguousarray(np.asarray(inputs["L0_ffn_U1"], f32)),
        "v1": np.ascontiguousarray(np.asarray(inputs["L0_ffn_V1"], f32)),
        "b1": np.ascontiguousarray(np.asarray(inputs["L0_ffn_b1"], f32).reshape(32, 128).T),
        "u2": np.ascontiguousarray(np.asarray(inputs["L0_ffn_U2"], f32)),
        "v2": np.ascontiguousarray(np.asarray(inputs["L0_ffn_V2"], f32)),
        "b2": np.ascontiguousarray(np.asarray(inputs["L0_ffn_b2"], f32).reshape(8, 128).T),
        "ident32": ident,
        "identbf": ident.astype(BF),
        "tri": np.ascontiguousarray((u[:, None] <= u[None, :]).astype(f32)),
        "tris": np.ascontiguousarray((u[:, None] > u[None, :]).astype(f32)),
        "umask": np.ascontiguousarray((LARGE * (u[:, None] > u[None, :])).astype(f32)),
        "onesrow": np.ones((1, 128), f32),
        "epsv": np.full((128, 1), EPS, f32),
        "onesrowbf": np.ones((1, 128), BF),
    }


def kernel(**inputs):
    x = np.asarray(inputs["L0_attn_x"], np.float32)
    base = _host_inputs(inputs)
    nc = build_program()
    in_maps = [dict(base, x=np.ascontiguousarray(x[b])) for b in range(8)]
    res = run_bass_kernel_spmd(nc, in_maps, core_ids=list(range(8)))
    out = np.stack([np.asarray(res.results[b]["out"]) for b in range(8)], axis=0)
    return out.astype(np.float32)


# version 12
# speedup vs baseline: 1.0826x; 1.0519x over previous
import numpy as np
import ml_dtypes
import concourse.bass as bass
import concourse.mybir as mybir
from concourse.bass_utils import run_bass_kernel_spmd

F32 = mybir.dt.float32
BF16 = mybir.dt.bfloat16
AF = mybir.ActivationFunctionType
ALU = mybir.AluOpType
AXX = mybir.AxisListType.X
BF = ml_dtypes.bfloat16

S, D, DI, NS = 8192, 1024, 2048, 16
TT = 256
NT = S // TT
LARGE = 1.0e6
EPS = 1e-5
RD = 1.0 / D


def build_program(SL=S, dbg=False):
    NTL = SL // TT
    nc = bass.Bass("TRN2", target_bir_lowering=False, detect_race_conditions=False)

    dx = nc.dram_tensor("x", [SL, D], F32, kind="ExternalInput").ap()
    d_wi = nc.dram_tensor("w_inT", [D, 2 * DI], F32, kind="ExternalInput").ap()
    d_bin = nc.dram_tensor("b_in", [128, 32], F32, kind="ExternalInput").ap()
    d_cw = nc.dram_tensor("convw", [128, 64], F32, kind="ExternalInput").ap()
    d_cb = nc.dram_tensor("conv_b", [128, 16], F32, kind="ExternalInput").ap()
    d_xp = nc.dram_tensor("xprojT", [DI, 33], F32, kind="ExternalInput").ap()
    d_xpb = nc.dram_tensor("xproj_b", [33, 1], F32, kind="ExternalInput").ap()
    d_alog = nc.dram_tensor("alog_row", [1, NS], F32, kind="ExternalInput").ap()
    d_dvec = nc.dram_tensor("dvec", [128, 16], F32, kind="ExternalInput").ap()
    d_wo = nc.dram_tensor("w_o_omaj", [8, 16, 128, 128], F32, kind="ExternalInput").ap()
    d_bo = nc.dram_tensor("b_out", [128, 8], F32, kind="ExternalInput").ap()
    d_gb = nc.dram_tensor("gamma_bc", [128, D], F32, kind="ExternalInput").ap()
    d_bb = nc.dram_tensor("beta_bc", [128, D], F32, kind="ExternalInput").ap()
    d_u1 = nc.dram_tensor("u1", [D, 256], F32, kind="ExternalInput").ap()
    d_v1 = nc.dram_tensor("v1", [256, 4096], F32, kind="ExternalInput").ap()
    d_b1 = nc.dram_tensor("b1", [128, 32], F32, kind="ExternalInput").ap()
    d_u2 = nc.dram_tensor("u2", [4096, 256], F32, kind="ExternalInput").ap()
    d_v2 = nc.dram_tensor("v2", [256, D], F32, kind="ExternalInput").ap()
    d_b2 = nc.dram_tensor("b2", [128, 8], F32, kind="ExternalInput").ap()
    d_id32 = nc.dram_tensor("ident32", [128, 128], F32, kind="ExternalInput").ap()
    d_idbf = nc.dram_tensor("identbf", [128, 128], BF16, kind="ExternalInput").ap()
    d_tri = nc.dram_tensor("tri", [128, 128], F32, kind="ExternalInput").ap()
    d_tris = nc.dram_tensor("tris", [128, 128], F32, kind="ExternalInput").ap()
    d_um = nc.dram_tensor("umask", [128, 128], F32, kind="ExternalInput").ap()
    d_ones = nc.dram_tensor("onesrow", [1, 128], F32, kind="ExternalInput").ap()
    d_eps = nc.dram_tensor("epsv", [128, 1], F32, kind="ExternalInput").ap()
    d_onesbf = nc.dram_tensor("onesrowbf", [1, 128], BF16, kind="ExternalInput").ap()
    d_out = nc.dram_tensor("out", [SL, D], F32, kind="ExternalOutput").ap()
    d_x2 = nc.dram_tensor("x2scratch", [SL, D], F32,
                          kind="ExternalOutput" if dbg else "Internal").ap()
    if dbg:
        d_dab = nc.dram_tensor("dbg_ab", [128, 23040], BF16, kind="ExternalOutput").ap()
        d_daf = nc.dram_tensor("dbg_af", [128, 13056], F32, kind="ExternalOutput").ap()
    d_wos = nc.dram_tensor("wo_scr", [8, 16, 128, 128], BF16, kind="Internal").ap()
    d_wbs = nc.dram_tensor("wb_scr", [128, 20480], BF16, kind="Internal").ap()

    wa = nc.alloc_sbuf_tensor("wa", [128, 33312], BF16)
    Wi = wa[:, 0:32768]
    XP = wa[:, 32768:33312]
    U1 = wa[:, 0:2048]
    V1 = wa[:, 2048:10240]
    U2 = wa[:, 10240:18432]
    V2 = wa[:, 18432:20480]

    wob = nc.alloc_sbuf_tensor("wob", [128, 4096], BF16)
    WoB = [wob[:, 0:2048], wob[:, 2048:4096]]

    ab = nc.alloc_sbuf_tensor("ab", [128, 29312], BF16)
    ln1x_tm2 = [ab[:, 0:2048], ab[:, 27264:29312]]
    ln1x_fm = ab[:, 2048:4096]
    ao = ln1x_fm
    x_main = ab[:, 4096:8240]
    zs = ab[:, 8240:12336]
    xc = ab[:, 12336:16432]
    Xt2 = [ab[:, 16432:18480], ab[:, 22848:24896]]
    Mt2 = [ab[:, 18480:20528], ab[:, 24896:26944]]
    h_bf = ab[0:16, 20528:22576]
    Qt2 = [ab[0:16, 22576:22704], ab[0:16, 26944:27072]]
    Kt2 = [ab[:, 27072:27088], ab[:, 27088:27104]]
    ln3_tm = ab[:, 0:2048]
    ln3_fm = ab[:, 2048:4096]
    Pf = ab[:, 4096:4608]
    Hf = ab[:, 4608:12800]
    Qf = ab[:, 12800:13312]
    fo = ab[:, 13312:15360]
    fo_tm = ab[:, 15360:17408]

    af = nc.alloc_sbuf_tensor("af", [128, 10916], F32)
    x_tm = [af[:, 0:2048], af[:, 2048:4096]]
    r_tm = af[:, 4096:6144]
    wst = [af[:, 6144:6656], af[:, 6656:7168]]
    h_T = af[0:16, 7168:9216]
    ssm = af[:, 9216:9284]
    scr = af[:, 9284:9828]
    stt = af[:, 9828:9892]
    cacc = af[:, 9892:10404]
    sc_y1b = af[:, 10404:10532]
    sc_dT1 = af[:, 10532:10660]
    ssm_fm = af[:, 10660:10916]
    sc_dt = scr[:, 0:2]
    sc_t0 = scr[:, 2:4]
    sc_t1 = scr[:, 4:6]
    sc_ctm = scr[:, 6:8]
    sc_r = scr[:, 8:10]
    sc_dT = scr[:, 16:144]
    sc_E = [scr[:, 144:272], scr[:, 272:400]]
    sc_y1 = scr[:, 400:528]
    sc_k0 = scr[:, 528:544]
    small = nc.alloc_sbuf_tensor("small", [128, 832], F32)
    sc_Cf2 = [small[0:16, 0:128], small[0:16, 128:256]]
    sc_E2s = [small[0:16, 256:384], small[0:16, 384:512]]
    sc_crow2 = [small[0:1, 512:640], small[0:1, 640:768]]
    a_col = small[0:16, 768:769]
    sc_dec2 = [small[0:16, 770:771], small[0:16, 771:772]]
    arow = small[0:1, 772:788]
    epsv = small[:, 800:801]
    nc.const_aps.register(F32, EPS, epsv)

    cf = nc.alloc_sbuf_tensor("cf", [128, 1664], F32)
    ID32 = cf[:, 0:128]
    TRI = cf[:, 128:256]
    TRIS = cf[:, 256:384]
    UM = cf[:, 384:512]
    ONES = cf[0:1, 512:640]
    GB = cf[:, 640:1664]
    cf2 = nc.alloc_sbuf_tensor("cf2", [128, 1248], F32)
    BB = cf2[:, 0:1024]
    B_IN = cf2[:, 1024:1056]
    C_B = cf2[:, 1056:1072]
    B_O = cf2[:, 1072:1080]
    DV = cf2[:, 1080:1096]
    B1 = cf2[:, 1096:1128]
    B2 = cf2[:, 1128:1136]
    XPB = cf2[0:33, 1136:1137]
    ARB = cf2[:, 1140:1156]
    CW = cf2[:, 1156:1220]
    cbf = nc.alloc_sbuf_tensor("cbf", [128, 256], BF16)
    IDBF = cbf[:, 0:128]
    ONESBF = cbf[0:1, 128:256]
    _cf0 = nc.alloc_sbuf_tensor("cflat", [1, 2048], F32)
    cflat2 = [_cf0[0:1, 0:2048], _cf0[0:1, 0:2048]]
    _cb0 = nc.alloc_sbuf_tensor("cbflat", [1, 4096], BF16)
    cbflat2 = [_cb0[0:1, 0:2048], _cb0[0:1, 2048:4096]]
    sqdump = nc.alloc_sbuf_tensor("sqdump", [128, 1024], BF16)
    convacc = nc.alloc_sbuf_tensor("convacc", [128, 512], BF16)

    ps_a = nc.alloc_psum_tensor("ps_a", [128, 512], F32)
    ps_b = nc.alloc_psum_tensor("ps_b", [128, 512], F32)
    ps_ab = [ps_a, ps_b]
    ps_t = [nc.alloc_psum_tensor("ps_t1", [128, 1024], BF16),
            nc.alloc_psum_tensor("ps_t2", [128, 1024], BF16)]
    ps_y = [nc.alloc_psum_tensor("ps_y1", [128, 512], F32),
            nc.alloc_psum_tensor("ps_y2", [128, 512], F32)]
    ps_c = [nc.alloc_psum_tensor("ps_c1", [128, 512], F32),
            nc.alloc_psum_tensor("ps_c2", [128, 512], F32)]

    import contextlib
    stx = contextlib.ExitStack()
    SEM, TOT = {}, {}
    for name in ("swc", "sw0", "sw1", "sx0", "sx1", "swo0", "swo1",
                 "sg0", "sg1", "sgw0", "sgw1", "sgc", "st", "sv", "ss"):
        SEM[name] = stx.enter_context(nc.semaphore(name))
        TOT[name] = 0
    DMA_SEMS = ("swc", "sw0", "sw1", "sx0", "sx1", "swo0", "swo1",
                "sg0", "sg1", "sgw0", "sgw1", "sgc")

    sites = []
    lm = {}
    bank = {}

    def site(eng, fn, waits=(), inc=None):
        ws = []
        for w in waits:
            if w is None:
                continue
            sname, v = w
            if v > 0:
                ws.append((SEM[sname], v))
        if inc is not None:
            TOT[inc] += 16 if inc in DMA_SEMS else 1
        sites.append((eng, fn, ws,
                      (SEM[inc], 16 if inc in DMA_SEMS else 1) if inc else None))

    def t(s):
        return TOT[s]

    def mark(key, sem):
        lm[key] = (sem, TOT[sem])

    def wlm(key):
        e = lm.get(key)
        return (e[0], e[1]) if e else None

    def bfree(bk):
        e = bank.get(bk)
        return (e[0], e[1]) if e else None

    def bset(bk, sem):
        bank[bk] = (sem, TOT[sem])

    for (dst, src) in ((ID32, d_id32), (TRI, d_tri), (TRIS, d_tris), (UM, d_um),
                       (ONES, d_ones), (GB, d_gb), (BB, d_bb), (B_IN, d_bin),
                       (C_B, d_cb), (B_O, d_bo), (DV, d_dvec), (B1, d_b1),
                       (B2, d_b2), (XPB, d_xpb), (IDBF, d_idbf), (ONESBF, d_onesbf),
                       (CW, d_cw), (arow, d_alog), (epsv, d_eps)):
        site("sync", lambda e, o=dst, i=src: e.dma_start(out=o, in_=i), inc="swc")
    consts_w = t("swc")

    site("scalar", lambda e: e.activation(arow, arow, AF.Exp),
         waits=[("swc", consts_w)], inc="ss")
    site("vector", lambda e: e.tensor_scalar(arow, arow, -1.0, None, ALU.mult),
         waits=[("ss", t("ss"))], inc="sv")
    av = t("sv")
    site("tensor", lambda e: e.matmul(ps_y[0][:, 0:16], ONES, arow, start=True, stop=True),
         waits=[("sv", av)], inc="st")
    site("tensor", lambda e: e.matmul(ps_y[1][0:16, 0:1], arow, ONES[0:1, 0:1], start=True, stop=True),
         inc="st")
    sv0 = t("st")
    site("vector", lambda e: e.tensor_copy(ARB, ps_y[0][:, 0:16]), waits=[("st", sv0)], inc="sv")
    site("vector", lambda e: e.tensor_copy(a_col, ps_y[1][0:16, 0:1]), inc="sv")
    bset("y0", "sv")
    bset("y1", "sv")
    site("vector", lambda e: e.memset(
        x_main.rearrange("p (c w) -> p c w", w=259)[:, :, 0:3], 0.0), inc="sv")
    site("vector", lambda e: e.memset(h_T, 0.0), inc="sv")
    site("vector", lambda e: e.memset(h_bf, 0.0), inc="sv")
    init_v = t("sv")

    cseq = [0]

    def loadcast(src_ap, dst_ap, ncols, extra_cast_waits=()):
        j = cseq[0]
        stg = wst[j % 2]
        sm = "sw%d" % (j % 2)

        def _dma(e, o=stg, i=src_ap, n=ncols):
            oo = o[:, 0:n]
            if len(i.ap) == 3:
                oo = oo.rearrange("p (c j) -> p c j", c=i.ap[1][1])
            return e.dma_start(out=oo, in_=i)
        site("sync", _dma, waits=[wlm(("stg", (j - 2) % 2))], inc=sm)
        wv = t(sm)
        site("vector", lambda e, o=dst_ap, i=stg, n=ncols: e.tensor_copy(o, i[:, 0:n]),
             waits=[(sm, wv)] + list(extra_cast_waits), inc="sv")
        mark(("stg", j % 2), "sv")
        cseq[0] += 1

    for q in range(8):
        for kc in range(8):
            loadcast(d_wi[kc * 128:(kc + 1) * 128, q * 512:(q + 1) * 512],
                     Wi[:, kc * 4096 + q * 512: kc * 4096 + (q + 1) * 512], 512)
        mark(("wiq", q), "sv")
    for hh in range(2):
        j = cseq[0]
        stg = wst[j % 2]
        _sm = "sw%d" % (j % 2)
        site("sync", lambda e, o=stg, hh=hh: e.dma_start(
            out=o[:, 0:264].rearrange("p (c j) -> p c j", j=33),
            in_=d_xp.rearrange("(c p) j -> p c j", p=128)[:, hh * 8:(hh + 1) * 8, :]),
            waits=[wlm(("stg", (j - 2) % 2))], inc=_sm)
        wv = t(_sm)
        site("vector", lambda e, i=stg, hh=hh: e.tensor_copy(
            XP.rearrange("p (c j) -> p c j", j=34)[:, hh * 8:(hh + 1) * 8, 0:33],
            i[:, 0:264].rearrange("p (c j) -> p c j", j=33)),
            waits=[(_sm, wv)], inc="sv")
        mark(("stg", j % 2), "sv")
        cseq[0] += 1
    weightsA_v = t("sv")
    for o in range(8):
        for hh in range(4):
            j = cseq[0]
            stg = wst[j % 2]
            _sm = "sw%d" % (j % 2)
            site("sync", lambda e, o_=stg, i=d_wo[o, hh * 4:(hh + 1) * 4].rearrange("c p j -> p c j"):
                 e.dma_start(out=o_[:, 0:512].rearrange("p (c j) -> p c j", j=128), in_=i),
                 waits=[wlm(("stg", (j - 2) % 2))], inc=_sm)
            wv = t(_sm)
            dstb = WoB[j % 2][:, 0:512]
            site("vector", lambda e, o_=dstb, i=stg: e.tensor_copy(o_, i[:, 0:512]),
                 waits=[(_sm, wv), wlm(("wob", j % 2))], inc="sv")
            mark(("stg", j % 2), "sv")
            vv = t("sv")
            _gm = "sgw%d" % (j % 2)
            site("gpsimd", lambda e, o_=d_wos[o, hh * 4:(hh + 1) * 4].rearrange("c p j -> p c j"),
                 i=dstb.rearrange("p (c j) -> p c j", j=128): e.dma_start(out=o_, in_=i),
                 waits=[("sv", vv)], inc=_gm)
            mark(("wob", j % 2), _gm)
            cseq[0] += 1
    jobsB0 = []
    for kc in range(8):
        jobsB0.append((d_u1[kc * 128:(kc + 1) * 128, :], kc * 256, 256))
    for r in range(2):
        for q in range(8):
            jobsB0.append((d_v1[r * 128:(r + 1) * 128, q * 512:(q + 1) * 512],
                           2048 + r * 4096 + q * 512, 512))
    for g in range(16):
        jobsB0.append((d_u2[g * 256:(g + 1) * 256, :].rearrange("(c p) j -> p c j", p=128),
                       10240 + g * 512, 512))
    for r in range(2):
        for q in range(2):
            jobsB0.append((d_v2[r * 128:(r + 1) * 128, q * 512:(q + 1) * 512],
                           18432 + r * 1024 + q * 512, 512))
    for (srcap, dcol, ncols) in jobsB0:
        j = cseq[0]
        stg = wst[j % 2]
        sm = "sw%d" % (j % 2)

        def _dma0(e, o=stg, i=srcap, n=ncols):
            oo = o[:, 0:n]
            if len(i.ap) == 3:
                oo = oo.rearrange("p (c j) -> p c j", c=i.ap[1][1])
            return e.dma_start(out=oo, in_=i)
        site("sync", _dma0, waits=[wlm(("stg", (j - 2) % 2))], inc=sm)
        wv = t(sm)
        dstb = WoB[j % 2][:, 0:ncols]
        site("vector", lambda e, o=dstb, i=stg, n=ncols: e.tensor_copy(o, i[:, 0:n]),
             waits=[(sm, wv), wlm(("wob", j % 2))], inc="sv")
        mark(("stg", j % 2), "sv")
        vv = t("sv")
        _gm = "sgw%d" % (j % 2)
        site("gpsimd", lambda e, o=dcol, n=ncols, i=dstb: e.dma_start(
            out=d_wbs[:, o:o + n], in_=i), waits=[("sv", vv)], inc=_gm)
        mark(("wob", j % 2), _gm)
        cseq[0] += 1
    sgw_tot = (t("sgw0"), t("sgw1"))

    def ln_pair_gen(srcs, norm_fns, wait_pre, sb=0):
        for j in (0, 1):
            b = 8 * (j + sb)
            site("vector", lambda e, j=j, b=b: e.reduce_sum(stt[:, b + 0:b + 1], srcs[j], axis=AXX),
                 waits=wait_pre if j == 0 else (), inc="sv")
        yield
        for j in (0, 1):
            b = 8 * (j + sb)
            site("vector", lambda e, j=j, b=b: e.scalar_tensor_tensor(
                sqdump[:, 0:1024], srcs[j], 1.0, srcs[j], ALU.bypass, ALU.mult,
                accum_out=stt[:, b + 1:b + 2]), inc="sv")
        yield
        for j in (0, 1):
            b = 8 * (j + sb)
            site("vector", lambda e, j=j, b=b: e.tensor_scalar(
                stt[:, b + 2:b + 3], stt[:, b + 0:b + 1], RD, None, ALU.mult), inc="sv")
        for j in (0, 1):
            b = 8 * (j + sb)
            site("vector", lambda e, j=j, b=b: e.tensor_tensor(
                stt[:, b + 5:b + 6], stt[:, b + 2:b + 3], stt[:, b + 2:b + 3], ALU.mult), inc="sv")
        yield
        vvs = {}
        for j in (0, 1):
            b = 8 * (j + sb)
            site("vector", lambda e, j=j, b=b: e.scalar_tensor_tensor(
                stt[:, b + 3:b + 4], stt[:, b + 1:b + 2], RD, stt[:, b + 5:b + 6],
                ALU.mult, ALU.subtract), inc="sv")
            vvs[j] = t("sv")
        for j in (0, 1):
            b = 8 * (j + sb)
            site("scalar", lambda e, j=j, b=b: e.activation(
                stt[:, b + 5:b + 6], stt[:, b + 3:b + 4], AF.Sqrt, bias=EPS),
                waits=[("sv", vvs[1])], inc="ss")
        yield
        sv_ = t("ss")
        for j in (0, 1):
            b = 8 * (j + sb)
            site("vector", lambda e, j=j, b=b: e.reciprocal(stt[:, b + 4:b + 5], stt[:, b + 5:b + 6]),
                 waits=[("ss", sv_)] if j == 0 else (), inc="sv")
        yield
        for j in (0, 1):
            b = 8 * (j + sb)
            site("vector", lambda e, j=j, b=b: norm_fns[j](e, stt[:, b + 2:b + 3], stt[:, b + 4:b + 5]), inc="sv")

    def ln_pair(srcs, norm_fns, wait_pre, sb=0):
        for _ in ln_pair_gen(srcs, norm_fns, wait_pre, sb=sb):
            pass

    xh = {}
    PUMPQ = []

    def pump(nsteps=1):
        for _ in range(nsteps):
            while PUMPQ:
                try:
                    next(PUMPQ[0])
                    return
                except StopIteration:
                    PUMPQ.pop(0)
            return

    def pump_rest():
        while PUMPQ:
            try:
                next(PUMPQ[0])
            except StopIteration:
                PUMPQ.pop(0)

    def emit_xload(i):
        if i >= NTL:
            return
        xb = x_tm[i % 2]
        _gp = "sg%d" % (i % 2)
        _xp_ = "sx%d" % (i % 2)
        w = [(_gp, 16 * ((i - 2) // 2 + 1))] if i >= 2 else []
        site("sync", lambda e, i=i, xb=xb: e.dma_start(
            out=xb.rearrange("p (j d) -> p j d", d=1024),
            in_=dx[i * TT:(i + 1) * TT, :].rearrange("(j p) d -> p j d", p=128)),
            waits=w, inc=_xp_)
        xh[i] = t(_xp_)

    def ln1_gen(i):
        xb = x_tm[i % 2]
        lt = ln1x_tm2[i % 2]
        srcs = [xb[:, 0:1024], xb[:, 1024:2048]]

        def mknorm(jj, src_):
            def f(e, mean, inv):
                return e.tensor_scalar(lt[:, jj * 1024:(jj + 1) * 1024], src_,
                                       mean, inv, ALU.subtract, ALU.mult)
            return f
        wpre = [("sx%d" % (i % 2), xh[i]), ("sv", init_v),
                wlm(("lnT", i - 2, 0)), wlm(("lnT", i - 2, 1))]
        yield from ln_pair_gen(srcs, [mknorm(0, srcs[0]), mknorm(1, srcs[1])], wpre, sb=2)
        mark(("ln1", i), "sv")

    def emit_lnT(i):
        lt = ln1x_tm2[i % 2]
        for jj in range(2):
            bk = "t%d" % jj
            wv = [wlm(("ln1", i)), bfree(bk), wlm(("aoT", i - 1))]
            for kc in range(8):
                site("tensor", lambda e, jj=jj, kc=kc, lt=lt: e.transpose(
                    ps_t[jj][:, kc * 128:(kc + 1) * 128],
                    lt[:, jj * 1024 + kc * 128: jj * 1024 + (kc + 1) * 128], IDBF),
                    waits=wv if kc == 0 else (), inc="st" if kc == 7 else None)
            mark(("lnT", i, jj), "st")
            stv = t("st")
            site("scalar", lambda e, jj=jj: e.copy(
                ab[:, 2048:4096].rearrange("p (k s) -> p k s", s=256)[:, :, jj * 128:(jj + 1) * 128],
                ps_t[jj][:, 0:1024].rearrange("p (k s) -> p k s", s=128)),
                waits=[("st", stv)], inc="ss")
            bset(bk, "ss")
        mark(("ln1fm", i), "ss")

    def make_epi(i):
        xb = x_tm[i % 2]
        rr = [r_tm[:, 0:1024], r_tm[:, 1024:2048]]

        def mkinp(rj_):
            def f(e, mean, inv):
                return e.tensor_scalar(rj_, rj_, mean, inv, ALU.subtract, ALU.mult)
            return f

        def gen():
            yield from ln_pair_gen(rr, [mkinp(rr[0]), mkinp(rr[1])], [("sv", t("sv"))], sb=0)
            yield
            for jj in (0, 1):
                site("vector", lambda e, jj=jj: e.tensor_tensor(rr[jj], rr[jj], GB, ALU.mult), inc="sv")
            yield
            for jj in (0, 1):
                site("vector", lambda e, jj=jj: e.tensor_tensor(rr[jj], rr[jj], BB, ALU.add), inc="sv")
            yield
            yield from ln_pair_gen(rr, [mkinp(rr[0]), mkinp(rr[1])], [("sv", t("sv"))], sb=0)
            yield
            for jj in (0, 1):
                site("vector", lambda e, jj=jj: e.tensor_tensor(
                    xb[:, jj * 1024:(jj + 1) * 1024], xb[:, jj * 1024:(jj + 1) * 1024],
                    rr[jj], ALU.add), inc="sv")
            mark(("x2", i), "sv")
            yield
            vv = t("sv")
            site("gpsimd", lambda e, i=i: e.dma_start(
                out=d_x2[i * TT:(i + 1) * TT, :].rearrange("(j p) d -> p j d", p=128),
                in_=xb.rearrange("p (j d) -> p j d", d=1024)),
                waits=[("sv", vv)], inc="sg%d" % (i % 2))
            yield
        return gen()

    emit_xload(0)
    emit_xload(1)
    for _ in ln1_gen(0):
        pass
    mark(("aoT", -1), "st")
    emit_lnT(0)

    for i in range(NTL):
        xb = x_tm[i % 2]
        if i + 1 < NTL:
            PUMPQ.append(ln1_gen(i + 1))

        def emit_conv_pair(cp):
            c0, c1 = 2 * cp, 2 * cp + 1
            accs = {c0: cacc[:, 0:256], c1: cacc[:, 256:512]}
            wv = [wlm(("xme", i, c0)), wlm(("xme", i, c1)),
                  wlm(("xce", i, c0 - 2)), wlm(("xce", i, c1 - 2))]
            for c in (c0, c1):
                site("vector", lambda e, c=c, acc=accs[c]: e.tensor_scalar(
                    acc, x_main[:, c * 259 + 0: c * 259 + 256],
                    CW[:, c * 4 + 0:c * 4 + 1], None, ALU.mult),
                    waits=wv if c == c0 else (), inc="sv")
            for k in range(1, 4):
                for c in (c0, c1):
                    site("vector", lambda e, c=c, k=k, acc=accs[c]: e.scalar_tensor_tensor(
                        acc, x_main[:, c * 259 + k: c * 259 + k + 256],
                        CW[:, c * 4 + k:c * 4 + k + 1], acc, ALU.mult, ALU.add), inc="sv")
            for c in (c0, c1):
                site("vector", lambda e, c=c: e.tensor_copy(
                    x_main[:, c * 259: c * 259 + 3],
                    x_main[:, c * 259 + 256: c * 259 + 259]), inc="sv")
                mark(("halo", i, c), "sv")
            vv = t("sv")
            for c in (c0, c1):
                site("scalar", lambda e, c=c, acc=accs[c]: e.activation(
                    xc[:, c * 256:(c + 1) * 256], acc, AF.Silu, bias=C_B[:, c:c + 1]),
                    waits=[("sv", vv)], inc="ss")
                mark(("xce", i, c), "ss")

        for o in range(32):
            bk = "ab%d" % (o % 2)
            wv = [wlm(("ln1fm", i)), wlm(("wiq", o // 4)), bfree(bk)]
            if i > 0 and o < 16:
                wv.append(wlm(("outproj", i - 1)))
            if i > 0 and o >= 16:
                wv.append(wlm(("gate", i - 1)))
            pb = ps_ab[o % 2]
            for kc in range(8):
                site("tensor", lambda e, o=o, kc=kc, pb=pb: e.matmul(
                    pb[:, 0:256],
                    Wi[:, kc * 4096 + o * 128: kc * 4096 + (o + 1) * 128],
                    ab[:, 2048 + kc * 256: 2048 + (kc + 1) * 256],
                    start=(kc == 0), stop=(kc == 7)),
                    waits=wv if kc == 0 else (), inc="st" if kc == 7 else None)
            if o == 31:
                mark(("ipg", i), "st")
            stv = t("st")
            if o < 16:
                c = o
                wv2 = [("st", stv), wlm(("halo", i - 1, c))]
                site("scalar", lambda e, c=c, pb=pb: e.activation(
                    x_main[:, c * 259 + 3: c * 259 + 259], pb[:, 0:256],
                    AF.Identity, bias=B_IN[:, c:c + 1]), waits=wv2, inc="ss")
                mark(("xme", i, c), "ss")
            else:
                c = o - 16
                site("scalar", lambda e, c=c, o=o, pb=pb: e.activation(
                    zs[:, c * 256:(c + 1) * 256], pb[:, 0:256],
                    AF.Silu, bias=B_IN[:, o:o + 1]), waits=[("st", stv)], inc="ss")
            mark(("ipe", i, o), "ss")
            if o == 31:
                mark(("zse", i), "ss")
            bset(bk, "ss")
            if o >= 3 and o % 2 == 1 and o <= 17:
                emit_conv_pair((o - 3) // 2)

        for kc in range(16):
            wv = [wlm(("xce", i, kc))]
            if kc == 0:
                wv.append(bfree("ab0"))
                wv.append(("sv", weightsA_v))
            site("tensor", lambda e, kc=kc: e.matmul(
                ps_a[0:33, 0:256], XP[:, kc * 34: kc * 34 + 33],
                xc[:, kc * 256:(kc + 1) * 256],
                start=(kc == 0), stop=(kc == 15)),
                waits=wv, inc="st" if kc == 15 else None)
        stv = t("st")
        site("scalar", lambda e: e.activation(ssm_fm[0:33, 0:256], ps_a[0:33, 0:256],
                                              AF.Identity, bias=XPB), waits=[("st", stv)], inc="ss")
        ssmfm_s = t("ss")
        bset("ab0", "ss")

        dt_v = {}
        for sc in range(2):
            site("tensor", lambda e, sc=sc: e.transpose(
                ps_y[sc][:, 0:33], ssm_fm[0:33, sc * 128:(sc + 1) * 128], ID32[0:33, 0:33]),
                waits=[("ss", ssmfm_s), bfree("y%d" % sc)], inc="st")
            stv = t("st")
            site("vector", lambda e, sc=sc: e.tensor_copy(ssm[:, sc * 34: sc * 34 + 33], ps_y[sc][:, 0:33]),
                 waits=[("st", stv)], inc="sv")
            bset("y%d" % sc, "sv")
            site("scalar", lambda e, sc=sc: e.dma_start(
                out=cflat2[sc], in_=ssm_fm[16:32, sc * 128:(sc + 1) * 128]),
                waits=[("ss", ssmfm_s), wlm(("cbfu",))], inc="sgc")
            cfl_g = t("sgc")
            site("scalar", lambda e, sc=sc: e.copy(cbflat2[sc], cflat2[sc]),
                 waits=[("sgc", cfl_g), wlm(("cbc", sc))], inc="ss")
            mark(("cfl", sc), "ss")
            mark(("cbfu",), "ss")
            vv = t("sv")
            site("scalar", lambda e, sc=sc: e.activation(sc_t0[:, sc:sc + 1], ssm[:, sc * 34 + 32:sc * 34 + 33], AF.Abs),
                 waits=[("sv", vv)], inc="ss")
            site("scalar", lambda e, sc=sc: e.activation(sc_t1[:, sc:sc + 1], ssm[:, sc * 34 + 32:sc * 34 + 33], AF.Relu), inc="ss")
            site("scalar", lambda e, sc=sc: e.activation(sc_t0[:, sc:sc + 1], sc_t0[:, sc:sc + 1], AF.Exp, scale=-1.0),
                 waits=[("ss", t("ss"))], inc="ss")
            site("scalar", lambda e, sc=sc: e.activation(sc_t0[:, sc:sc + 1], sc_t0[:, sc:sc + 1], AF.Ln, bias=1.0),
                 waits=[("ss", t("ss"))], inc="ss")
            sv_ = t("ss")
            site("vector", lambda e, sc=sc: e.tensor_tensor(sc_dt[:, sc:sc + 1], sc_t0[:, sc:sc + 1], sc_t1[:, sc:sc + 1], ALU.add),
                 waits=[("ss", sv_)], inc="sv")
            dt_v[sc] = t("sv")
            pump()

        xt_v = {}
        for sc in range(2):
            for half in range(2):
                bk = "t%d" % half
                for q in range(8):
                    kc = half * 8 + q
                    wv = [wlm(("xce", i, kc))]
                    if q == 0:
                        wv.append(bfree(bk))
                    site("tensor", lambda e, kc=kc, half=half, sc=sc, q=q: e.transpose(
                        ps_t[half][:, q * 128:(q + 1) * 128],
                        xc[:, kc * 256 + sc * 128: kc * 256 + sc * 128 + 128], IDBF),
                        waits=wv, inc="st" if q == 7 else None)
                stv = t("st")
                site("vector", lambda e, half=half, sc=sc: e.tensor_scalar(
                    Xt2[sc][:, half * 1024:(half + 1) * 1024], ps_t[half][:, 0:1024],
                    sc_dt[:, sc:sc + 1], None, ALU.mult),
                    waits=[("st", stv), ("sv", dt_v[sc])], inc="sv")
                bset(bk, "sv")
            xt_v[sc] = t("sv")
            pump()

        dT_v = {}
        kt_v = {}
        dec_s = {}
        for sc in range(2):
            dtap = sc_dt[:, sc:sc + 1]
            site("tensor", lambda e, dtap=dtap: e.matmul(ps_a[:, 0:1], TRI, dtap, start=True, stop=True),
                 waits=[("sv", dt_v[sc]), bfree("ab0")], inc="st")
            site("tensor", lambda e, dtap=dtap: e.matmul(ps_b[0:1, 0:128], dtap, TRI, start=True, stop=True),
                 waits=[bfree("ab1")], inc="st")
            site("tensor", lambda e, dtap=dtap: e.matmul(ps_c[0][:, 0:1], TRIS, dtap, start=True, stop=True),
                 waits=[bfree("c0")], inc="st")
            site("tensor", lambda e, sc=sc: e.transpose(ps_c[1][0:16, 0:128],
                                                        ssm[:, sc * 34 + 16: sc * 34 + 32], ID32),
                 waits=[bfree("c1")], inc="st")
            stv = t("st")
            site("vector", lambda e, sc=sc: e.tensor_copy(sc_ctm[:, sc:sc + 1], ps_a[:, 0:1]),
                 waits=[("st", stv)], inc="sv")
            bset("ab0", "sv")
            site("vector", lambda e, sc=sc: e.tensor_copy(sc_crow2[sc], ps_b[0:1, 0:128]), inc="sv")
            bset("ab1", "sv")
            site("vector", lambda e, sc=sc: e.tensor_copy(sc_r[:, sc:sc + 1], ps_c[0][:, 0:1]), inc="sv")
            bset("c0", "sv")
            site("vector", lambda e, sc=sc: e.tensor_copy(sc_Cf2[sc], ps_c[1][0:16, 0:128]), inc="sv")
            bset("c1", "sv")
            ev1 = t("sv")
            site("tensor", lambda e, sc=sc: e.matmul(ps_a[:, 0:128], ONES, sc_crow2[sc], start=True, stop=True),
                 waits=[("sv", ev1), bfree("ab0")], inc="st")
            site("tensor", lambda e, sc=sc: e.matmul(ps_b[0:16, 0:128], ONES[0:1, 0:16], sc_crow2[sc], start=True, stop=True),
                 waits=[bfree("ab1")], inc="st")
            site("tensor", lambda e, sc=sc: e.matmul(ps_b[0:16, 128:129], ONES[0:1, 0:16], sc_crow2[sc][0:1, 127:128],
                                                     start=True, stop=True), inc="st")
            stv = t("st")
            dTt = sc_dT if sc == 0 else sc_dT1
            site("vector", lambda e, sc=sc, dTt=dTt: e.scalar_tensor_tensor(
                dTt, ps_a[:, 0:128], sc_ctm[:, sc:sc + 1], UM, ALU.subtract, ALU.add),
                waits=[("st", stv)], inc="sv")
            bset("ab0", "sv")
            dT_v[sc] = t("sv")
            site("scalar", lambda e, sc=sc: e.activation(sc_E2s[sc], ps_b[0:16, 0:128], AF.Exp, scale=a_col),
                 waits=[("st", stv)], inc="ss")
            site("scalar", lambda e, sc=sc: e.activation(sc_dec2[sc], ps_b[0:16, 128:129], AF.Exp, scale=a_col), inc="ss")
            dec_s[sc] = t("ss")
            bset("ab1", "ss")
            sv_ = t("ss")
            site("vector", lambda e, sc=sc: e.tensor_tensor(Qt2[sc], sc_E2s[sc], sc_Cf2[sc], ALU.mult),
                 waits=[("ss", sv_)], inc="sv")
            site("vector", lambda e, sc=sc: e.tensor_scalar(sc_k0, ARB, sc_r[:, sc:sc + 1], None, ALU.mult),
                 waits=[("sv", t("sv"))], inc="sv")
            vv = t("sv")
            site("scalar", lambda e: e.activation(sc_k0, sc_k0, AF.Exp), waits=[("sv", vv)], inc="ss")
            sv_ = t("ss")
            site("vector", lambda e, sc=sc: e.tensor_tensor(Kt2[sc], sc_k0, ssm[:, sc * 34: sc * 34 + 16], ALU.mult),
                 waits=[("ss", sv_), ("sv", t("sv"))], inc="sv")
            kt_v[sc] = t("sv")
            pump()

        CBANKS = [("ab0", ps_a), ("ab1", ps_b), ("c0", ps_c[0]), ("c1", ps_c[1])]

        def emit_mblock(sc):
            dTt = sc_dT if sc == 0 else sc_dT1
            for n in range(16):
                bk, pb = CBANKS[n % 4]
                col = (n // 4) * 128
                site("tensor", lambda e, sc=sc, n=n, pb=pb, col=col: e.matmul(
                    pb[:, col:col + 128], ONESBF, cbflat2[sc][0:1, n * 128:(n + 1) * 128],
                    start=True, stop=True),
                    waits=[bfree(bk) if n < 4 else None, wlm(("cfl", sc))], inc="st")
                mark(("cbst", sc, n), "st")
            mark(("cbc", sc), "st")
            for n in range(16):
                ns = sc * 16 + n
                site("scalar", lambda e, sc=sc, n=n, dTt=dTt: e.activation(
                    sc_E[n % 2], dTt, AF.Exp, scale=ARB[:, n:n + 1]),
                    waits=[("sv", dT_v[sc]), wlm(("mteg", i, ns - 2))], inc="ss")
                mark(("exps", sc, n), "ss")

        def emit_mstt(sc, n):
            ns = sc * 16 + n
            bk, pb = CBANKS[n % 4]
            col = (n // 4) * 128
            site("vector", lambda e, sc=sc, n=n, pb=pb, col=col: e.scalar_tensor_tensor(
                Mt2[sc][:, n * 128:(n + 1) * 128], sc_E[n % 2],
                ssm[:, sc * 34 + n: sc * 34 + n + 1], pb[:, col:col + 128], ALU.mult, ALU.mult),
                waits=[wlm(("exps", sc, n)), wlm(("cbst", sc, n))], inc="sv")
            mark(("mteg", i, ns), "sv")
            mark(("mte", i, sc, n), "sv")
            if n >= 12:
                bset(CBANKS[n % 4][0], "sv")

        emit_mblock(0)
        for n in range(16):
            emit_mstt(0, n)
            if n % 4 == 3:
                pump()
        emit_mblock(1)

        def emit_intra_chunk(sc, interleave_next):
            hbf_w = wlm(("hbf",))
            for dc in range(16):
                bk = "y%d" % (dc % 2)
                pb = ps_y[dc % 2]
                for n in range(16):
                    wv = []
                    if n == 0:
                        wv = [hbf_w, bfree(bk), ("sv", xt_v[sc])]
                    wv.append(wlm(("mte", i, sc, n)))
                    site("tensor", lambda e, sc=sc, dc=dc, n=n, pb=pb: e.matmul(
                        pb[:, 0:128], Xt2[sc][:, dc * 128:(dc + 1) * 128],
                        Mt2[sc][:, n * 128:(n + 1) * 128], start=(n == 0), stop=False),
                        waits=wv, inc=None)
                site("tensor", lambda e, sc=sc, dc=dc, pb=pb: e.matmul(
                    pb[:, 0:128], h_bf[0:16, dc * 128:(dc + 1) * 128], Qt2[sc],
                    start=False, stop=True), inc="st")
                if dc == 15:
                    mark(("inter", i, sc), "st")
                stv = t("st")
                y1b = sc_y1 if dc % 2 == 0 else sc_y1b
                site("vector", lambda e, dc=dc, sc=sc, pb=pb, y1b=y1b: e.scalar_tensor_tensor(
                    y1b, xc[:, dc * 256 + sc * 128: dc * 256 + sc * 128 + 128],
                    DV[:, dc:dc + 1], pb[:, 0:128], ALU.mult, ALU.add),
                    waits=[("st", stv), wlm(("zse", i))], inc="sv")
                site("vector", lambda e, dc=dc, sc=sc, y1b=y1b: e.tensor_tensor(
                    x_main[:, dc * 259 + 3 + sc * 128: dc * 259 + 3 + sc * 128 + 128],
                    y1b, zs[:, dc * 256 + sc * 128: dc * 256 + sc * 128 + 128], ALU.mult),
                    waits=[("sv", t("sv"))], inc="sv")
                bset(bk, "sv")
                if interleave_next:
                    emit_mstt(1, dc)
                else:
                    pump()
            if sc == 1:
                mark(("gate", i), "sv")

        def emit_state(sc):
            for q in range(4):
                bk = "c%d" % (q % 2)
                pb = ps_c[q % 2]
                site("tensor", lambda e, sc=sc, q=q, pb=pb: e.matmul(
                    pb[0:16, 0:512], Kt2[sc], Xt2[sc][:, q * 512:(q + 1) * 512], start=True, stop=True),
                    waits=[("sv", kt_v[sc]), ("sv", xt_v[sc]), bfree(bk)], inc="st")
                stv = t("st")
                site("vector", lambda e, sc=sc, q=q, pb=pb: e.scalar_tensor_tensor(
                    h_T[0:16, q * 512:(q + 1) * 512], h_T[0:16, q * 512:(q + 1) * 512],
                    sc_dec2[sc], pb[0:16, 0:512], ALU.mult, ALU.add),
                    waits=[("st", stv), ("ss", dec_s[sc])], inc="sv")
                bset(bk, "sv")
            site("vector", lambda e: e.tensor_copy(h_bf, h_T),
                 waits=[wlm(("inter", i, sc)), ("sv", t("sv"))], inc="sv")
            mark(("hbf",), "sv")

        emit_intra_chunk(0, True)
        emit_state(0)
        emit_intra_chunk(1, False)
        emit_state(1)

        for o in range(8):
            _wm = "swo%d" % (o % 2)
            site("sync", lambda e, o=o: e.dma_start(
                out=WoB[o % 2].rearrange("p (c j) -> p c j", j=128),
                in_=d_wos[o].rearrange("c p j -> p c j")),
                waits=[wlm(("opg", i, o - 2)) or wlm(("opg", i - 1, o + 6)),
                       ("sgw0", sgw_tot[0]), ("sgw1", sgw_tot[1])], inc=_wm)
            wo_here = t(_wm)
            bk = "ab%d" % (o % 2)
            pb = ps_ab[o % 2]
            wv = [(_wm, wo_here), wlm(("gate", i)), bfree(bk)]
            for kc in range(16):
                site("tensor", lambda e, o=o, kc=kc, pb=pb: e.matmul(
                    pb[:, 0:256], WoB[o % 2][:, kc * 128:(kc + 1) * 128],
                    x_main[:, kc * 259 + 3: kc * 259 + 259],
                    start=(kc == 0), stop=(kc == 15)),
                    waits=wv if kc == 0 else (), inc="st" if kc == 15 else None)
            mark(("opg", i, o), "st")
            stv = t("st")
            site("scalar", lambda e, o=o, pb=pb: e.activation(
                ao[:, o * 256:(o + 1) * 256], pb[:, 0:256], AF.Identity, bias=B_O[:, o:o + 1]),
                waits=[("st", stv)], inc="ss")
            mark(("aoe", i, o), "ss")
            bset(bk, "ss")
            pump()
        mark(("outproj", i), "st")

        stv0 = stv1 = None
        for jj in range(2):
            bk = "t%d" % jj
            wv = [wlm(("aoe", i, 7)), bfree(bk)]
            for o in range(8):
                site("tensor", lambda e, jj=jj, o=o: e.transpose(
                    ps_t[jj][:, o * 128:(o + 1) * 128],
                    ao[:, o * 256 + jj * 128: o * 256 + jj * 128 + 128], IDBF),
                    waits=wv if o == 0 else (), inc="st" if o == 7 else None)
            if jj == 0:
                stv0 = t("st")
            else:
                mark(("aoT", i), "st")
                stv1 = t("st")

        pump_rest()
        lt = ln1x_tm2[i % 2]
        rr = [r_tm[:, 0:1024], r_tm[:, 1024:2048]]
        site("vector", lambda e, lt=lt, rr=rr: e.tensor_tensor(
            rr[0], ps_t[0][:, 0:1024], lt[:, 0:1024], ALU.add),
            waits=[("st", stv0)], inc="sv")
        site("vector", lambda e, lt=lt, rr=rr: e.tensor_tensor(
            rr[1], ps_t[1][:, 0:1024], lt[:, 1024:2048], ALU.add),
            waits=[("st", stv1)], inc="sv")
        bset("t0", "sv")
        bset("t1", "sv")
        if i + 1 < NTL:
            emit_lnT(i + 1)
        PUMPQ.append(make_epi(i))
        emit_xload(i + 2)
    pump_rest()
    lastA_ip = lm[("ipg", NTL - 1)]
    gA = (t("sg0"), t("sg1"))

    for g in range(40):
        _sm = "sw%d" % (g % 2)
        site("sync", lambda e, g=g: e.dma_start(out=wa[:, g * 512:(g + 1) * 512],
                                                in_=d_wbs[:, g * 512:(g + 1) * 512]),
             waits=[("st", lastA_ip[1]), ("sgw0", sgw_tot[0]), ("sgw1", sgw_tot[1])], inc=_sm)
    wbs_w = (t("sw0"), t("sw1"))
    site("vector", lambda e: e.engine_nop(), waits=[("sw0", wbs_w[0]), ("sw1", wbs_w[1])], inc="sv")
    weightsB_v = t("sv")

    xhB = {}

    def emit_x2load(i):
        if i >= NTL:
            return
        xb = x_tm[i % 2]
        _gp = "sg%d" % (i % 2)
        _xp_ = "sx%d" % (i % 2)
        w = [("sg0", gA[0]), ("sg1", gA[1])]
        if i >= 2:
            w.append((_gp, t(_gp)))
        site("sync", lambda e, i=i, xb=xb: e.dma_start(
            out=xb.rearrange("p (j d) -> p j d", d=1024),
            in_=d_x2[i * TT:(i + 1) * TT, :].rearrange("(j p) d -> p j d", p=128)),
            waits=w, inc=_xp_)
        xhB[i] = t(_xp_)

    def emit_ln3(i):
        xb = x_tm[i % 2]
        srcs = [xb[:, 0:1024], xb[:, 1024:2048]]

        def mknorm3(jj, src_):
            def f(e, mean, inv):
                return e.tensor_scalar(ln3_tm[:, jj * 1024:(jj + 1) * 1024], src_,
                                       mean, inv, ALU.subtract, ALU.mult)
            return f
        wpre = [("sx%d" % (i % 2), xhB[i]), ("sv", weightsB_v), wlm(("ln3u", i - 1))]
        ln_pair(srcs, [mknorm3(0, srcs[0]), mknorm3(1, srcs[1])], wpre, sb=2)
        mark(("ln3", i), "sv")

    def emit_lnT3(i):
        for jj in range(2):
            bk = "t%d" % jj
            wv = [wlm(("ln3", i)), bfree(bk)]
            for kc in range(8):
                site("tensor", lambda e, jj=jj, kc=kc: e.transpose(
                    ps_t[jj][:, kc * 128:(kc + 1) * 128],
                    ln3_tm[:, jj * 1024 + kc * 128: jj * 1024 + (kc + 1) * 128], IDBF),
                    waits=wv if kc == 0 else (), inc="st" if kc == 7 else None)
            stv = t("st")
            site("scalar", lambda e, jj=jj: e.copy(
                ab[:, 2048:4096].rearrange("p (k s) -> p k s", s=256)[:, :, jj * 128:(jj + 1) * 128],
                ps_t[jj][:, 0:1024].rearrange("p (k s) -> p k s", s=128)),
                waits=[("st", stv)], inc="ss")
            bset(bk, "ss")
        mark(("ln3fm", i), "ss")

    emit_x2load(0)
    emit_x2load(1)
    emit_ln3(0)
    emit_lnT3(0)

    for i in range(NTL):
        xb = x_tm[i % 2]

        BB4 = [("ab0", ps_a), ("ab1", ps_b), ("y0", ps_y[0]), ("y1", ps_y[1])]

        for r in range(2):
            bk, pb = BB4[r % 4]
            wv = [wlm(("ln3fm", i)), bfree(bk)]
            for kc in range(8):
                site("tensor", lambda e, r=r, kc=kc, pb=pb: e.matmul(
                    pb[:, 0:256], U1[:, kc * 256 + r * 128: kc * 256 + r * 128 + 128],
                    ab[:, 2048 + kc * 256: 2048 + (kc + 1) * 256],
                    start=(kc == 0), stop=(kc == 7)),
                    waits=wv if kc == 0 else (), inc="st" if kc == 7 else None)
            stv = t("st")
            site("scalar", lambda e, r=r, pb=pb: e.copy(
                Pf[:, r * 256:(r + 1) * 256], pb[:, 0:256]), waits=[("st", stv)], inc="ss")
            bset(bk, "ss")
        pf_s = t("ss")

        for f in range(32):
            bk, pb = BB4[f % 4]
            wv = [("ss", pf_s), bfree(bk)]
            for r in range(2):
                site("tensor", lambda e, f=f, r=r, pb=pb: e.matmul(
                    pb[:, 0:256], V1[:, r * 4096 + f * 128: r * 4096 + (f + 1) * 128],
                    Pf[:, r * 256:(r + 1) * 256], start=(r == 0), stop=(r == 1)),
                    waits=wv if r == 0 else (), inc="st" if r == 1 else None)
            stv = t("st")
            site("scalar", lambda e, f=f, pb=pb: e.activation(
                Hf[:, f * 256:(f + 1) * 256], pb[:, 0:256], AF.Gelu_apprx_tanh,
                bias=B1[:, f:f + 1]), waits=[("st", stv)], inc="ss")
            mark(("hfe", i, f), "ss")
            bset(bk, "ss")

        for r in range(2):
            bk, pb = BB4[r % 4]
            wv = [wlm(("hfe", i, 31)), bfree(bk)]
            for kc in range(32):
                site("tensor", lambda e, r=r, kc=kc, pb=pb: e.matmul(
                    pb[:, 0:256], U2[:, kc * 256 + r * 128: kc * 256 + r * 128 + 128],
                    Hf[:, kc * 256:(kc + 1) * 256], start=(kc == 0), stop=(kc == 31)),
                    waits=wv if kc == 0 else (), inc="st" if kc == 31 else None)
            stv = t("st")
            site("scalar", lambda e, r=r, pb=pb: e.copy(
                Qf[:, r * 256:(r + 1) * 256], pb[:, 0:256]), waits=[("st", stv)], inc="ss")
            bset(bk, "ss")
        qf_s = t("ss")

        for o in range(8):
            bk, pb = BB4[o % 4]
            wv = [("ss", qf_s), bfree(bk)]
            for r in range(2):
                site("tensor", lambda e, o=o, r=r, pb=pb: e.matmul(
                    pb[:, 0:256], V2[:, r * 1024 + o * 128: r * 1024 + (o + 1) * 128],
                    Qf[:, r * 256:(r + 1) * 256], start=(r == 0), stop=(r == 1)),
                    waits=wv if r == 0 else (), inc="st" if r == 1 else None)
            stv = t("st")
            site("scalar", lambda e, o=o, pb=pb: e.activation(
                fo[:, o * 256:(o + 1) * 256], pb[:, 0:256], AF.Identity, bias=B2[:, o:o + 1]),
                waits=[("st", stv)], inc="ss")
            mark(("foe", i, o), "ss")
            bset(bk, "ss")

        stv0 = stv1 = None
        for jj in range(2):
            bk = "t%d" % jj
            wv = [wlm(("foe", i, 7)), bfree(bk)]
            for o in range(8):
                site("tensor", lambda e, jj=jj, o=o: e.transpose(
                    ps_t[jj][:, o * 128:(o + 1) * 128],
                    fo[:, o * 256 + jj * 128: o * 256 + jj * 128 + 128], IDBF),
                    waits=wv if o == 0 else (), inc="st" if o == 7 else None)
            if jj == 0:
                stv0 = t("st")
            else:
                stv1 = t("st")
        site("vector", lambda e: e.tensor_copy(fo_tm[:, 0:1024], ps_t[0][:, 0:1024]),
             waits=[("st", stv0)], inc="sv")
        site("vector", lambda e: e.tensor_copy(fo_tm[:, 1024:2048], ps_t[1][:, 0:1024]),
             waits=[("st", stv1)], inc="sv")
        bset("t0", "sv")
        bset("t1", "sv")

        if i + 1 < NTL:
            emit_ln3(i + 1)
            emit_lnT3(i + 1)

        fos = [fo_tm[:, 0:1024], fo_tm[:, 1024:2048]]
        NRM = (r_tm[:, 0:1024], r_tm[:, 1024:2048])

        def mknorm4(jj, src_):
            def f(e, mean, inv):
                return e.tensor_scalar(NRM[jj][:, 0:1024], src_, mean, inv, ALU.subtract, ALU.mult)
            return f
        ln_pair(fos, [mknorm4(0, fos[0]), mknorm4(1, fos[1])], [("sv", t("sv"))], sb=0)
        for jj in (0, 1):
            site("vector", lambda e, jj=jj, xb=xb: e.tensor_tensor(
                xb[:, jj * 1024:(jj + 1) * 1024], xb[:, jj * 1024:(jj + 1) * 1024],
                NRM[jj][:, 0:1024], ALU.add), inc="sv")
        mark(("ln3u", i), "sv")
        vv = t("sv")
        site("gpsimd", lambda e, i=i, xb=xb: e.dma_start(
            out=d_out[i * TT:(i + 1) * TT, :].rearrange("(j p) d -> p j d", p=128),
            in_=xb.rearrange("p (j d) -> p j d", d=1024)),
            waits=[("sv", vv)], inc="sg%d" % (i % 2))
        emit_x2load(i + 2)

    site("gpsimd", lambda e: e.engine_nop(),
         waits=[("sg0", t("sg0")), ("sg1", t("sg1")), ("sgc", t("sgc"))])

    CONSTW = consts_w
    with nc.Block() as block:
        def mk(name):
            def f(e):
                seen = {}
                if name in ("tensor", "vector", "scalar", "gpsimd"):
                    e.wait_ge(SEM["swc"], CONSTW)
                    seen[id(SEM["swc"])] = CONSTW
                for (eng, fn, ws, inc) in sites:
                    if eng != name:
                        continue
                    for (sh, v) in ws:
                        if seen.get(id(sh), -1) < v:
                            e.wait_ge(sh, v)
                            seen[id(sh)] = v
                    inst = fn(e)
                    if inc is not None:
                        inst.then_inc(inc[0], inc[1])
            return f
        block.sync(mk("sync"))
        block.tensor(mk("tensor"))
        block.vector(mk("vector"))
        block.scalar(mk("scalar"))
        block.gpsimd(mk("gpsimd"))
    return nc


def _host_inputs(inputs):
    f32 = np.float32
    A = np.asarray(inputs["L0_attn_A_log"], f32)
    if not (A == A[0]).all():
        raise RuntimeError("A_log rows differ; general-A fallback not implemented")
    conv_w = np.asarray(inputs["L0_attn_conv_w"], f32)
    ident = np.eye(128, dtype=f32)
    u = np.arange(128)
    return {
        "w_inT": np.ascontiguousarray(np.asarray(inputs["L0_attn_in_proj_w"], f32).T),
        "b_in": np.ascontiguousarray(np.asarray(inputs["L0_attn_in_proj_b"], f32).reshape(32, 128).T),
        "convw": np.ascontiguousarray(conv_w[:, 0, :].reshape(16, 128, 4).transpose(1, 0, 2).reshape(128, 64)),
        "conv_b": np.ascontiguousarray(np.asarray(inputs["L0_attn_conv_b"], f32).reshape(16, 128).T),
        "xprojT": np.ascontiguousarray(np.asarray(inputs["L0_attn_x_proj_w"], f32).T),
        "xproj_b": np.ascontiguousarray(np.asarray(inputs["L0_attn_x_proj_b"], f32).reshape(33, 1)),
        "alog_row": np.ascontiguousarray(A[0:1, :]),
        "dvec": np.ascontiguousarray(np.asarray(inputs["L0_attn_D"], f32).reshape(16, 128).T),
        "w_o_omaj": np.ascontiguousarray(
            np.asarray(inputs["L0_attn_out_proj_w"], f32).reshape(8, 128, 16, 128).transpose(0, 2, 3, 1)),
        "b_out": np.ascontiguousarray(np.asarray(inputs["L0_attn_out_proj_b"], f32).reshape(8, 128).T),
        "gamma_bc": np.ascontiguousarray(np.tile(np.asarray(inputs["L0_attn_ln_gamma"], f32)[None, :], (128, 1))),
        "beta_bc": np.ascontiguousarray(np.tile(np.asarray(inputs["L0_attn_ln_beta"], f32)[None, :], (128, 1))),
        "u1": np.ascontiguousarray(np.asarray(inputs["L0_ffn_U1"], f32)),
        "v1": np.ascontiguousarray(np.asarray(inputs["L0_ffn_V1"], f32)),
        "b1": np.ascontiguousarray(np.asarray(inputs["L0_ffn_b1"], f32).reshape(32, 128).T),
        "u2": np.ascontiguousarray(np.asarray(inputs["L0_ffn_U2"], f32)),
        "v2": np.ascontiguousarray(np.asarray(inputs["L0_ffn_V2"], f32)),
        "b2": np.ascontiguousarray(np.asarray(inputs["L0_ffn_b2"], f32).reshape(8, 128).T),
        "ident32": ident,
        "identbf": ident.astype(BF),
        "tri": np.ascontiguousarray((u[:, None] <= u[None, :]).astype(f32)),
        "tris": np.ascontiguousarray((u[:, None] > u[None, :]).astype(f32)),
        "umask": np.ascontiguousarray((LARGE * (u[:, None] > u[None, :])).astype(f32)),
        "onesrow": np.ones((1, 128), f32),
        "epsv": np.full((128, 1), EPS, f32),
        "onesrowbf": np.ones((1, 128), BF),
    }


def kernel(**inputs):
    x = np.asarray(inputs["L0_attn_x"], np.float32)
    base = _host_inputs(inputs)
    nc = build_program()
    in_maps = [dict(base, x=np.ascontiguousarray(x[b])) for b in range(8)]
    res = run_bass_kernel_spmd(nc, in_maps, core_ids=list(range(8)))
    out = np.stack([np.asarray(res.results[b]["out"]) for b in range(8)], axis=0)
    return out.astype(np.float32)


# version 13
# speedup vs baseline: 1.1096x; 1.0250x over previous
import numpy as np
import ml_dtypes
import concourse.bass as bass
import concourse.mybir as mybir
from concourse.bass_utils import run_bass_kernel_spmd

F32 = mybir.dt.float32
BF16 = mybir.dt.bfloat16
AF = mybir.ActivationFunctionType
ALU = mybir.AluOpType
AXX = mybir.AxisListType.X
BF = ml_dtypes.bfloat16

S, D, DI, NS = 8192, 1024, 2048, 16
TT = 256
NT = S // TT
LARGE = 1.0e6
EPS = 1e-5
RD = 1.0 / D


def build_program(SL=S, dbg=False):
    NTL = SL // TT
    nc = bass.Bass("TRN2", target_bir_lowering=False, detect_race_conditions=False)

    dx = nc.dram_tensor("x", [SL, D], F32, kind="ExternalInput").ap()
    d_wi = nc.dram_tensor("w_inT", [D, 2 * DI], F32, kind="ExternalInput").ap()
    d_bin = nc.dram_tensor("b_in", [128, 32], F32, kind="ExternalInput").ap()
    d_cw = nc.dram_tensor("convw", [128, 64], F32, kind="ExternalInput").ap()
    d_cb = nc.dram_tensor("conv_b", [128, 16], F32, kind="ExternalInput").ap()
    d_xp = nc.dram_tensor("xprojT", [DI, 33], F32, kind="ExternalInput").ap()
    d_xpb = nc.dram_tensor("xproj_b", [33, 1], F32, kind="ExternalInput").ap()
    d_alog = nc.dram_tensor("alog_row", [1, NS], F32, kind="ExternalInput").ap()
    d_dvec = nc.dram_tensor("dvec", [128, 16], F32, kind="ExternalInput").ap()
    d_wo = nc.dram_tensor("w_o_omaj", [8, 16, 128, 128], F32, kind="ExternalInput").ap()
    d_bo = nc.dram_tensor("b_out", [128, 8], F32, kind="ExternalInput").ap()
    d_gb = nc.dram_tensor("gamma_bc", [128, D], F32, kind="ExternalInput").ap()
    d_bb = nc.dram_tensor("beta_bc", [128, D], F32, kind="ExternalInput").ap()
    d_u1 = nc.dram_tensor("u1", [D, 256], F32, kind="ExternalInput").ap()
    d_v1 = nc.dram_tensor("v1", [256, 4096], F32, kind="ExternalInput").ap()
    d_b1 = nc.dram_tensor("b1", [128, 32], F32, kind="ExternalInput").ap()
    d_u2 = nc.dram_tensor("u2", [4096, 256], F32, kind="ExternalInput").ap()
    d_v2 = nc.dram_tensor("v2", [256, D], F32, kind="ExternalInput").ap()
    d_b2 = nc.dram_tensor("b2", [128, 8], F32, kind="ExternalInput").ap()
    d_id32 = nc.dram_tensor("ident32", [128, 128], F32, kind="ExternalInput").ap()
    d_idbf = nc.dram_tensor("identbf", [128, 128], BF16, kind="ExternalInput").ap()
    d_tri = nc.dram_tensor("tri", [128, 128], F32, kind="ExternalInput").ap()
    d_tris = nc.dram_tensor("tris", [128, 128], F32, kind="ExternalInput").ap()
    d_um = nc.dram_tensor("umask", [128, 128], F32, kind="ExternalInput").ap()
    d_ones = nc.dram_tensor("onesrow", [1, 128], F32, kind="ExternalInput").ap()
    d_eps = nc.dram_tensor("epsv", [128, 1], F32, kind="ExternalInput").ap()
    d_onesbf = nc.dram_tensor("onesrowbf", [1, 128], BF16, kind="ExternalInput").ap()
    d_out = nc.dram_tensor("out", [SL, D], F32, kind="ExternalOutput").ap()
    d_x2 = nc.dram_tensor("x2scratch", [SL, D], F32,
                          kind="ExternalOutput" if dbg else "Internal").ap()
    if dbg:
        d_dab = nc.dram_tensor("dbg_ab", [128, 23040], BF16, kind="ExternalOutput").ap()
        d_daf = nc.dram_tensor("dbg_af", [128, 13056], F32, kind="ExternalOutput").ap()
    d_wos = nc.dram_tensor("wo_scr", [8, 16, 128, 128], BF16, kind="Internal").ap()
    d_wbs = nc.dram_tensor("wb_scr", [128, 20480], BF16, kind="Internal").ap()

    wa = nc.alloc_sbuf_tensor("wa", [128, 33312], BF16)
    Wi = wa[:, 0:32768]
    XP = wa[:, 32768:33312]
    U1 = wa[:, 0:2048]
    V1 = wa[:, 2048:10240]
    U2 = wa[:, 10240:18432]
    V2 = wa[:, 18432:20480]

    wob = nc.alloc_sbuf_tensor("wob", [128, 4096], BF16)
    WoB = [wob[:, 0:2048], wob[:, 2048:4096]]

    ab = nc.alloc_sbuf_tensor("ab", [128, 29312], BF16)
    ln1x_tm2 = [ab[:, 0:2048], ab[:, 27264:29312]]
    ln1x_fm = ab[:, 2048:4096]
    ao = ln1x_fm
    x_main = ab[:, 4096:8240]
    zs = ab[:, 8240:12336]
    xc = ab[:, 12336:16432]
    Xt2 = [ab[:, 16432:18480], ab[:, 22848:24896]]
    Mt2 = [ab[:, 18480:20528], ab[:, 24896:26944]]
    h_bf = ab[0:16, 20528:22576]
    Qt2 = [ab[0:16, 22576:22704], ab[0:16, 26944:27072]]
    Kt2 = [ab[:, 27072:27088], ab[:, 27088:27104]]
    ln3_tm = ab[:, 0:2048]
    ln3_fm = ab[:, 2048:4096]
    Pf = ab[:, 4096:4608]
    Hf = ab[:, 4608:12800]
    Qf = ab[:, 12800:13312]
    fo = ab[:, 13312:15360]
    fo_tm = ab[:, 15360:17408]

    af = nc.alloc_sbuf_tensor("af", [128, 10916], F32)
    x_tm = [af[:, 0:2048], af[:, 2048:4096]]
    r_tm = af[:, 4096:6144]
    wst = [af[:, 6144:6656], af[:, 6656:7168]]
    h_T = af[0:16, 7168:9216]
    ssm = af[:, 9216:9284]
    scr = af[:, 9284:9828]
    stt = af[:, 9828:9892]
    cacc = af[:, 9892:10404]
    sc_y1b = af[:, 10404:10532]
    sc_dT1 = af[:, 10532:10660]
    ssm_fm = af[:, 10660:10916]
    sc_dt = scr[:, 0:2]
    sc_t0 = scr[:, 2:4]
    sc_t1 = scr[:, 4:6]
    sc_ctm = scr[:, 6:8]
    sc_r = scr[:, 8:10]
    sc_dT = scr[:, 16:144]
    sc_E = [scr[:, 144:272], scr[:, 272:400]]
    sc_y1 = scr[:, 400:528]
    sc_k0 = scr[:, 528:544]
    small = nc.alloc_sbuf_tensor("small", [128, 832], F32)
    sc_Cf2 = [small[0:16, 0:128], small[0:16, 128:256]]
    sc_E2s = [small[0:16, 256:384], small[0:16, 384:512]]
    sc_crow2 = [small[0:1, 512:640], small[0:1, 640:768]]
    a_col = small[0:16, 768:769]
    sc_dec2 = [small[0:16, 770:771], small[0:16, 771:772]]
    arow = small[0:1, 772:788]
    epsv = small[:, 800:801]
    nc.const_aps.register(F32, EPS, epsv)

    cf = nc.alloc_sbuf_tensor("cf", [128, 1664], F32)
    ID32 = cf[:, 0:128]
    TRI = cf[:, 128:256]
    TRIS = cf[:, 256:384]
    UM = cf[:, 384:512]
    ONES = cf[0:1, 512:640]
    GB = cf[:, 640:1664]
    cf2 = nc.alloc_sbuf_tensor("cf2", [128, 1248], F32)
    BB = cf2[:, 0:1024]
    B_IN = cf2[:, 1024:1056]
    C_B = cf2[:, 1056:1072]
    B_O = cf2[:, 1072:1080]
    DV = cf2[:, 1080:1096]
    B1 = cf2[:, 1096:1128]
    B2 = cf2[:, 1128:1136]
    XPB = cf2[0:33, 1136:1137]
    ARB = cf2[:, 1140:1156]
    CW = cf2[:, 1156:1220]
    cbf = nc.alloc_sbuf_tensor("cbf", [128, 256], BF16)
    IDBF = cbf[:, 0:128]
    ONESBF = cbf[0:1, 128:256]
    _cf0 = nc.alloc_sbuf_tensor("cflat", [1, 2048], F32)
    cflat2 = [_cf0[0:1, 0:2048], _cf0[0:1, 0:2048]]
    _cb0 = nc.alloc_sbuf_tensor("cbflat", [1, 4096], BF16)
    cbflat2 = [_cb0[0:1, 0:2048], _cb0[0:1, 2048:4096]]
    sqdump = nc.alloc_sbuf_tensor("sqdump", [128, 1024], BF16)
    ssmbf = nc.alloc_sbuf_tensor("ssmbf", [33, 256], BF16)

    ps_a = nc.alloc_psum_tensor("ps_a", [128, 512], F32)
    ps_b = nc.alloc_psum_tensor("ps_b", [128, 512], F32)
    ps_ab = [ps_a, ps_b]
    ps_t = [nc.alloc_psum_tensor("ps_t1", [128, 1024], BF16),
            nc.alloc_psum_tensor("ps_t2", [128, 1024], BF16)]
    ps_y = [nc.alloc_psum_tensor("ps_y1", [128, 512], F32),
            nc.alloc_psum_tensor("ps_y2", [128, 512], F32)]
    ps_c = [nc.alloc_psum_tensor("ps_c1", [128, 512], F32),
            nc.alloc_psum_tensor("ps_c2", [128, 512], F32)]

    import contextlib
    stx = contextlib.ExitStack()
    SEM, TOT = {}, {}
    for name in ("swc", "sw0", "sw1", "sx0", "sx1", "swo0", "swo1",
                 "sg0", "sg1", "sgw0", "sgw1", "sgc", "st", "sv", "ss"):
        SEM[name] = stx.enter_context(nc.semaphore(name))
        TOT[name] = 0
    DMA_SEMS = ("swc", "sw0", "sw1", "sx0", "sx1", "swo0", "swo1",
                "sg0", "sg1", "sgw0", "sgw1", "sgc")

    sites = []
    lm = {}
    bank = {}

    def site(eng, fn, waits=(), inc=None):
        ws = []
        for w in waits:
            if w is None:
                continue
            sname, v = w
            if v > 0:
                ws.append((SEM[sname], v))
        if inc is not None:
            TOT[inc] += 16 if inc in DMA_SEMS else 1
        sites.append((eng, fn, ws,
                      (SEM[inc], 16 if inc in DMA_SEMS else 1) if inc else None))

    def t(s):
        return TOT[s]

    def mark(key, sem):
        lm[key] = (sem, TOT[sem])

    def wlm(key):
        e = lm.get(key)
        return (e[0], e[1]) if e else None

    def bfree(bk):
        e = bank.get(bk)
        return (e[0], e[1]) if e else None

    def bset(bk, sem):
        bank[bk] = (sem, TOT[sem])

    for (dst, src) in ((ID32, d_id32), (TRI, d_tri), (TRIS, d_tris), (UM, d_um),
                       (ONES, d_ones), (GB, d_gb), (BB, d_bb), (B_IN, d_bin),
                       (C_B, d_cb), (B_O, d_bo), (DV, d_dvec), (B1, d_b1),
                       (B2, d_b2), (XPB, d_xpb), (IDBF, d_idbf), (ONESBF, d_onesbf),
                       (CW, d_cw), (arow, d_alog), (epsv, d_eps)):
        site("sync", lambda e, o=dst, i=src: e.dma_start(out=o, in_=i), inc="swc")
    consts_w = t("swc")

    site("scalar", lambda e: e.activation(arow, arow, AF.Exp),
         waits=[("swc", consts_w)], inc="ss")
    site("vector", lambda e: e.tensor_scalar(arow, arow, -1.0, None, ALU.mult),
         waits=[("ss", t("ss"))], inc="sv")
    av = t("sv")
    site("tensor", lambda e: e.matmul(ps_y[0][:, 0:16], ONES, arow, start=True, stop=True),
         waits=[("sv", av)], inc="st")
    site("tensor", lambda e: e.matmul(ps_y[1][0:16, 0:1], arow, ONES[0:1, 0:1], start=True, stop=True),
         inc="st")
    sv0 = t("st")
    site("vector", lambda e: e.tensor_copy(ARB, ps_y[0][:, 0:16]), waits=[("st", sv0)], inc="sv")
    site("vector", lambda e: e.tensor_copy(a_col, ps_y[1][0:16, 0:1]), inc="sv")
    bset("y0", "sv")
    bset("y1", "sv")
    site("vector", lambda e: e.memset(
        x_main.rearrange("p (c w) -> p c w", w=259)[:, :, 0:3], 0.0), inc="sv")
    site("vector", lambda e: e.memset(h_T, 0.0), inc="sv")
    site("vector", lambda e: e.memset(h_bf, 0.0), inc="sv")
    init_v = t("sv")

    xh = {}
    for _i in (0, 1):
        site("sync", lambda e, i=_i: e.dma_start(
            out=x_tm[i % 2].rearrange("p (j d) -> p j d", d=1024),
            in_=dx[i * TT:(i + 1) * TT, :].rearrange("(j p) d -> p j d", p=128)),
            inc="sx%d" % (_i % 2))
        xh[_i] = t("sx%d" % (_i % 2))

    cseq = [0]

    def loadcast(src_ap, dst_ap, ncols, extra_cast_waits=()):
        j = cseq[0]
        stg = wst[j % 2]
        sm = "sw%d" % (j % 2)

        def _dma(e, o=stg, i=src_ap, n=ncols):
            oo = o[:, 0:n]
            if len(i.ap) == 3:
                oo = oo.rearrange("p (c j) -> p c j", c=i.ap[1][1])
            return e.dma_start(out=oo, in_=i)
        site("sync", _dma, waits=[wlm(("stg", (j - 2) % 2))], inc=sm)
        wv = t(sm)
        site("vector", lambda e, o=dst_ap, i=stg, n=ncols: e.tensor_copy(o, i[:, 0:n]),
             waits=[(sm, wv)] + list(extra_cast_waits), inc="sv")
        mark(("stg", j % 2), "sv")
        cseq[0] += 1

    for q in range(8):
        for kc in range(8):
            loadcast(d_wi[kc * 128:(kc + 1) * 128, q * 512:(q + 1) * 512],
                     Wi[:, kc * 4096 + q * 512: kc * 4096 + (q + 1) * 512], 512)
        mark(("wiq", q), "sv")
    for hh in range(2):
        j = cseq[0]
        stg = wst[j % 2]
        _sm = "sw%d" % (j % 2)
        site("sync", lambda e, o=stg, hh=hh: e.dma_start(
            out=o[:, 0:264].rearrange("p (c j) -> p c j", j=33),
            in_=d_xp.rearrange("(c p) j -> p c j", p=128)[:, hh * 8:(hh + 1) * 8, :]),
            waits=[wlm(("stg", (j - 2) % 2))], inc=_sm)
        wv = t(_sm)
        site("vector", lambda e, i=stg, hh=hh: e.tensor_copy(
            XP.rearrange("p (c j) -> p c j", j=34)[:, hh * 8:(hh + 1) * 8, 0:33],
            i[:, 0:264].rearrange("p (c j) -> p c j", j=33)),
            waits=[(_sm, wv)], inc="sv")
        mark(("stg", j % 2), "sv")
        cseq[0] += 1
    weightsA_v = t("sv")
    for o in range(8):
        for hh in range(4):
            j = cseq[0]
            stg = wst[j % 2]
            _sm = "sw%d" % (j % 2)
            site("sync", lambda e, o_=stg, i=d_wo[o, hh * 4:(hh + 1) * 4].rearrange("c p j -> p c j"):
                 e.dma_start(out=o_[:, 0:512].rearrange("p (c j) -> p c j", j=128), in_=i),
                 waits=[wlm(("stg", (j - 2) % 2))], inc=_sm)
            wv = t(_sm)
            dstb = WoB[j % 2][:, 0:512]
            site("vector", lambda e, o_=dstb, i=stg: e.tensor_copy(o_, i[:, 0:512]),
                 waits=[(_sm, wv), wlm(("wob", j % 2))], inc="sv")
            mark(("stg", j % 2), "sv")
            vv = t("sv")
            _gm = "sgw%d" % (j % 2)
            site("gpsimd", lambda e, o_=d_wos[o, hh * 4:(hh + 1) * 4].rearrange("c p j -> p c j"),
                 i=dstb.rearrange("p (c j) -> p c j", j=128): e.dma_start(out=o_, in_=i),
                 waits=[("sv", vv)], inc=_gm)
            mark(("wob", j % 2), _gm)
            cseq[0] += 1
    jobsB0 = []
    for kc in range(8):
        jobsB0.append((d_u1[kc * 128:(kc + 1) * 128, :], kc * 256, 256))
    for r in range(2):
        for q in range(8):
            jobsB0.append((d_v1[r * 128:(r + 1) * 128, q * 512:(q + 1) * 512],
                           2048 + r * 4096 + q * 512, 512))
    for g in range(16):
        jobsB0.append((d_u2[g * 256:(g + 1) * 256, :].rearrange("(c p) j -> p c j", p=128),
                       10240 + g * 512, 512))
    for r in range(2):
        for q in range(2):
            jobsB0.append((d_v2[r * 128:(r + 1) * 128, q * 512:(q + 1) * 512],
                           18432 + r * 1024 + q * 512, 512))
    for (srcap, dcol, ncols) in jobsB0:
        j = cseq[0]
        stg = wst[j % 2]
        sm = "sw%d" % (j % 2)

        def _dma0(e, o=stg, i=srcap, n=ncols):
            oo = o[:, 0:n]
            if len(i.ap) == 3:
                oo = oo.rearrange("p (c j) -> p c j", c=i.ap[1][1])
            return e.dma_start(out=oo, in_=i)
        site("sync", _dma0, waits=[wlm(("stg", (j - 2) % 2))], inc=sm)
        wv = t(sm)
        dstb = WoB[j % 2][:, 0:ncols]
        site("vector", lambda e, o=dstb, i=stg, n=ncols: e.tensor_copy(o, i[:, 0:n]),
             waits=[(sm, wv), wlm(("wob", j % 2))], inc="sv")
        mark(("stg", j % 2), "sv")
        vv = t("sv")
        _gm = "sgw%d" % (j % 2)
        site("gpsimd", lambda e, o=dcol, n=ncols, i=dstb: e.dma_start(
            out=d_wbs[:, o:o + n], in_=i), waits=[("sv", vv)], inc=_gm)
        mark(("wob", j % 2), _gm)
        cseq[0] += 1
    sgw_tot = (t("sgw0"), t("sgw1"))

    def ln_pair_gen(srcs, norm_fns, wait_pre, sb=0):
        for j in (0, 1):
            b = 8 * (j + sb)
            site("vector", lambda e, j=j, b=b: e.reduce_sum(stt[:, b + 0:b + 1], srcs[j], axis=AXX),
                 waits=wait_pre if j == 0 else (), inc="sv")
        yield
        for j in (0, 1):
            b = 8 * (j + sb)
            site("vector", lambda e, j=j, b=b: e.scalar_tensor_tensor(
                sqdump[:, 0:1024], srcs[j], 1.0, srcs[j], ALU.bypass, ALU.mult,
                accum_out=stt[:, b + 1:b + 2]), inc="sv")
        yield
        for j in (0, 1):
            b = 8 * (j + sb)
            site("vector", lambda e, j=j, b=b: e.tensor_scalar(
                stt[:, b + 2:b + 3], stt[:, b + 0:b + 1], RD, None, ALU.mult), inc="sv")
        for j in (0, 1):
            b = 8 * (j + sb)
            site("vector", lambda e, j=j, b=b: e.tensor_tensor(
                stt[:, b + 5:b + 6], stt[:, b + 2:b + 3], stt[:, b + 2:b + 3], ALU.mult), inc="sv")
        yield
        vvs = {}
        for j in (0, 1):
            b = 8 * (j + sb)
            site("vector", lambda e, j=j, b=b: e.scalar_tensor_tensor(
                stt[:, b + 3:b + 4], stt[:, b + 1:b + 2], RD, stt[:, b + 5:b + 6],
                ALU.mult, ALU.subtract), inc="sv")
            vvs[j] = t("sv")
        for j in (0, 1):
            b = 8 * (j + sb)
            site("scalar", lambda e, j=j, b=b: e.activation(
                stt[:, b + 5:b + 6], stt[:, b + 3:b + 4], AF.Sqrt, bias=EPS),
                waits=[("sv", vvs[1])], inc="ss")
        yield
        sv_ = t("ss")
        for j in (0, 1):
            b = 8 * (j + sb)
            site("vector", lambda e, j=j, b=b: e.reciprocal(stt[:, b + 4:b + 5], stt[:, b + 5:b + 6]),
                 waits=[("ss", sv_)] if j == 0 else (), inc="sv")
        yield
        for j in (0, 1):
            b = 8 * (j + sb)
            site("vector", lambda e, j=j, b=b: norm_fns[j](e, stt[:, b + 2:b + 3], stt[:, b + 4:b + 5]), inc="sv")

    def ln_pair(srcs, norm_fns, wait_pre, sb=0):
        for _ in ln_pair_gen(srcs, norm_fns, wait_pre, sb=sb):
            pass

    PUMPQ = []

    def pump(nsteps=1):
        for _ in range(nsteps):
            while PUMPQ:
                try:
                    next(PUMPQ[0])
                    return
                except StopIteration:
                    PUMPQ.pop(0)
            return

    def pump_rest():
        while PUMPQ:
            try:
                next(PUMPQ[0])
            except StopIteration:
                PUMPQ.pop(0)

    def emit_xload(i):
        if i >= NTL:
            return
        xb = x_tm[i % 2]
        _gp = "sg%d" % (i % 2)
        _xp_ = "sx%d" % (i % 2)
        w = [(_gp, 16 * ((i - 2) // 2 + 1))] if i >= 2 else []
        site("sync", lambda e, i=i, xb=xb: e.dma_start(
            out=xb.rearrange("p (j d) -> p j d", d=1024),
            in_=dx[i * TT:(i + 1) * TT, :].rearrange("(j p) d -> p j d", p=128)),
            waits=w, inc=_xp_)
        xh[i] = t(_xp_)

    def ln1_gen(i):
        xb = x_tm[i % 2]
        lt = ln1x_tm2[i % 2]
        srcs = [xb[:, 0:1024], xb[:, 1024:2048]]

        def mknorm(jj, src_):
            def f(e, mean, inv):
                return e.tensor_scalar(lt[:, jj * 1024:(jj + 1) * 1024], src_,
                                       mean, inv, ALU.subtract, ALU.mult)
            return f
        wpre = [("sx%d" % (i % 2), xh[i]), ("sv", init_v),
                wlm(("lnT", i - 2, 0)), wlm(("lnT", i - 2, 1))]
        yield from ln_pair_gen(srcs, [mknorm(0, srcs[0]), mknorm(1, srcs[1])], wpre, sb=2)
        mark(("ln1", i), "sv")

    def emit_lnT(i):
        lt = ln1x_tm2[i % 2]
        for jj in range(2):
            bk = "t%d" % jj
            wv = [wlm(("ln1", i)), bfree(bk), wlm(("aoT", i - 1))]
            for kc in range(8):
                site("tensor", lambda e, jj=jj, kc=kc, lt=lt: e.transpose(
                    ps_t[jj][:, kc * 128:(kc + 1) * 128],
                    lt[:, jj * 1024 + kc * 128: jj * 1024 + (kc + 1) * 128], IDBF),
                    waits=wv if kc == 0 else (), inc="st" if kc == 7 else None)
            mark(("lnT", i, jj), "st")
            stv = t("st")
            site("scalar", lambda e, jj=jj: e.copy(
                ab[:, 2048:4096].rearrange("p (k s) -> p k s", s=256)[:, :, jj * 128:(jj + 1) * 128],
                ps_t[jj][:, 0:1024].rearrange("p (k s) -> p k s", s=128)),
                waits=[("st", stv)], inc="ss")
            bset(bk, "ss")
        mark(("ln1fm", i), "ss")

    def make_epi(i):
        xb = x_tm[i % 2]
        rr = [r_tm[:, 0:1024], r_tm[:, 1024:2048]]

        def mkinp(rj_):
            def f(e, mean, inv):
                return e.tensor_scalar(rj_, rj_, mean, inv, ALU.subtract, ALU.mult)
            return f

        def gen():
            yield from ln_pair_gen(rr, [mkinp(rr[0]), mkinp(rr[1])], [("sv", t("sv"))], sb=0)
            yield
            for jj in (0, 1):
                site("vector", lambda e, jj=jj: e.tensor_tensor(rr[jj], rr[jj], GB, ALU.mult), inc="sv")
            yield
            for jj in (0, 1):
                site("vector", lambda e, jj=jj: e.tensor_tensor(rr[jj], rr[jj], BB, ALU.add), inc="sv")
            yield
            yield from ln_pair_gen(rr, [mkinp(rr[0]), mkinp(rr[1])], [("sv", t("sv"))], sb=0)
            yield
            for jj in (0, 1):
                site("vector", lambda e, jj=jj: e.tensor_tensor(
                    xb[:, jj * 1024:(jj + 1) * 1024], xb[:, jj * 1024:(jj + 1) * 1024],
                    rr[jj], ALU.add), inc="sv")
            mark(("x2", i), "sv")
            yield
            vv = t("sv")
            site("gpsimd", lambda e, i=i: e.dma_start(
                out=d_x2[i * TT:(i + 1) * TT, :].rearrange("(j p) d -> p j d", p=128),
                in_=xb.rearrange("p (j d) -> p j d", d=1024)),
                waits=[("sv", vv)], inc="sg%d" % (i % 2))
            yield
        return gen()

    for _ in ln1_gen(0):
        pass
    mark(("aoT", -1), "st")
    emit_lnT(0)

    for i in range(NTL):
        xb = x_tm[i % 2]
        if i + 1 < NTL:
            PUMPQ.append(ln1_gen(i + 1))

        def emit_conv_pair(cp):
            c0, c1 = 2 * cp, 2 * cp + 1
            accs = {c0: cacc[:, 0:256], c1: cacc[:, 256:512]}
            wv = [wlm(("xme", i, c0)), wlm(("xme", i, c1)),
                  wlm(("xce", i, c0 - 2)), wlm(("xce", i, c1 - 2))]
            for c in (c0, c1):
                site("vector", lambda e, c=c, acc=accs[c]: e.tensor_scalar(
                    acc, x_main[:, c * 259 + 0: c * 259 + 256],
                    CW[:, c * 4 + 0:c * 4 + 1], None, ALU.mult),
                    waits=wv if c == c0 else (), inc="sv")
            for k in range(1, 4):
                for c in (c0, c1):
                    site("vector", lambda e, c=c, k=k, acc=accs[c]: e.scalar_tensor_tensor(
                        acc, x_main[:, c * 259 + k: c * 259 + k + 256],
                        CW[:, c * 4 + k:c * 4 + k + 1], acc, ALU.mult, ALU.add), inc="sv")
            for c in (c0, c1):
                site("vector", lambda e, c=c: e.tensor_copy(
                    x_main[:, c * 259: c * 259 + 3],
                    x_main[:, c * 259 + 256: c * 259 + 259]), inc="sv")
                mark(("halo", i, c), "sv")
            vv = t("sv")
            for c in (c0, c1):
                site("scalar", lambda e, c=c, acc=accs[c]: e.activation(
                    xc[:, c * 256:(c + 1) * 256], acc, AF.Silu, bias=C_B[:, c:c + 1]),
                    waits=[("sv", vv)], inc="ss")
                mark(("xce", i, c), "ss")

        for o in range(32):
            bk = "ab%d" % (o % 2)
            wv = [wlm(("ln1fm", i)), wlm(("wiq", o // 4)), bfree(bk)]
            if i > 0 and o < 16:
                wv.append(wlm(("outproj", i - 1)))
            if i > 0 and o >= 16:
                wv.append(wlm(("gate", i - 1)))
            pb = ps_ab[o % 2]
            for kc in range(8):
                site("tensor", lambda e, o=o, kc=kc, pb=pb: e.matmul(
                    pb[:, 0:256],
                    Wi[:, kc * 4096 + o * 128: kc * 4096 + (o + 1) * 128],
                    ab[:, 2048 + kc * 256: 2048 + (kc + 1) * 256],
                    start=(kc == 0), stop=(kc == 7)),
                    waits=wv if kc == 0 else (), inc="st" if kc == 7 else None)
            if o == 31:
                mark(("ipg", i), "st")
            stv = t("st")
            if o < 16:
                c = o
                wv2 = [("st", stv), wlm(("halo", i - 1, c))]
                site("scalar", lambda e, c=c, pb=pb: e.activation(
                    x_main[:, c * 259 + 3: c * 259 + 259], pb[:, 0:256],
                    AF.Identity, bias=B_IN[:, c:c + 1]), waits=wv2, inc="ss")
                mark(("xme", i, c), "ss")
            else:
                c = o - 16
                site("scalar", lambda e, c=c, o=o, pb=pb: e.activation(
                    zs[:, c * 256:(c + 1) * 256], pb[:, 0:256],
                    AF.Silu, bias=B_IN[:, o:o + 1]), waits=[("st", stv)], inc="ss")
            mark(("ipe", i, o), "ss")
            if o == 31:
                mark(("zse", i), "ss")
            bset(bk, "ss")
            if o >= 3 and o % 2 == 1 and o <= 17:
                emit_conv_pair((o - 3) // 2)

        for kc in range(16):
            wv = [wlm(("xce", i, kc))]
            if kc == 0:
                wv.append(bfree("ab0"))
                wv.append(("sv", weightsA_v))
            site("tensor", lambda e, kc=kc: e.matmul(
                ps_a[0:33, 0:256], XP[:, kc * 34: kc * 34 + 33],
                xc[:, kc * 256:(kc + 1) * 256],
                start=(kc == 0), stop=(kc == 15)),
                waits=wv, inc="st" if kc == 15 else None)
        stv = t("st")
        site("scalar", lambda e: e.activation(ssm_fm[0:33, 0:256], ps_a[0:33, 0:256],
                                              AF.Identity, bias=XPB), waits=[("st", stv)], inc="ss")
        site("scalar", lambda e: e.activation(ssmbf[0:33, 0:256], ps_a[0:33, 0:256],
                                              AF.Identity, bias=XPB), waits=[wlm(("cbfu",))], inc="ss")
        ssmfm_s = t("ss")
        bset("ab0", "ss")

        dt_v = {}
        for sc in range(2):
            site("tensor", lambda e, sc=sc: e.transpose(
                ps_y[sc][:, 0:33], ssm_fm[0:33, sc * 128:(sc + 1) * 128], ID32[0:33, 0:33]),
                waits=[("ss", ssmfm_s), bfree("y%d" % sc)], inc="st")
            stv = t("st")
            site("vector", lambda e, sc=sc: e.tensor_copy(ssm[:, sc * 34: sc * 34 + 33], ps_y[sc][:, 0:33]),
                 waits=[("st", stv)], inc="sv")
            bset("y%d" % sc, "sv")
            site("scalar", lambda e, sc=sc: e.dma_start(
                out=cbflat2[sc], in_=ssmbf[16:32, sc * 128:(sc + 1) * 128]),
                waits=[("ss", ssmfm_s), wlm(("cbc", sc))], inc="sgc")
            mark(("cfl", sc), "sgc")
            if sc == 1:
                mark(("cbfu",), "sgc")
            vv = t("sv")
            site("scalar", lambda e, sc=sc: e.activation(sc_t0[:, sc:sc + 1], ssm[:, sc * 34 + 32:sc * 34 + 33], AF.Abs),
                 waits=[("sv", vv)], inc="ss")
            site("scalar", lambda e, sc=sc: e.activation(sc_t1[:, sc:sc + 1], ssm[:, sc * 34 + 32:sc * 34 + 33], AF.Relu), inc="ss")
            site("scalar", lambda e, sc=sc: e.activation(sc_t0[:, sc:sc + 1], sc_t0[:, sc:sc + 1], AF.Exp, scale=-1.0),
                 waits=[("ss", t("ss"))], inc="ss")
            site("scalar", lambda e, sc=sc: e.activation(sc_t0[:, sc:sc + 1], sc_t0[:, sc:sc + 1], AF.Ln, bias=1.0),
                 waits=[("ss", t("ss"))], inc="ss")
            sv_ = t("ss")
            site("vector", lambda e, sc=sc: e.tensor_tensor(sc_dt[:, sc:sc + 1], sc_t0[:, sc:sc + 1], sc_t1[:, sc:sc + 1], ALU.add),
                 waits=[("ss", sv_)], inc="sv")
            dt_v[sc] = t("sv")
            pump()

        xt_v = {}
        for sc in range(2):
            for half in range(2):
                bk = "t%d" % half
                for q in range(8):
                    kc = half * 8 + q
                    wv = [wlm(("xce", i, kc))]
                    if q == 0:
                        wv.append(bfree(bk))
                    site("tensor", lambda e, kc=kc, half=half, sc=sc, q=q: e.transpose(
                        ps_t[half][:, q * 128:(q + 1) * 128],
                        xc[:, kc * 256 + sc * 128: kc * 256 + sc * 128 + 128], IDBF),
                        waits=wv, inc="st" if q == 7 else None)
                stv = t("st")
                site("vector", lambda e, half=half, sc=sc: e.tensor_scalar(
                    Xt2[sc][:, half * 1024:(half + 1) * 1024], ps_t[half][:, 0:1024],
                    sc_dt[:, sc:sc + 1], None, ALU.mult),
                    waits=[("st", stv), ("sv", dt_v[sc])], inc="sv")
                bset(bk, "sv")
            xt_v[sc] = t("sv")
            pump()

        dT_v = {}
        kt_v = {}
        dec_s = {}
        for sc in range(2):
            dtap = sc_dt[:, sc:sc + 1]
            site("tensor", lambda e, dtap=dtap: e.matmul(ps_a[:, 0:1], TRI, dtap, start=True, stop=True),
                 waits=[("sv", dt_v[sc]), bfree("ab0")], inc="st")
            site("tensor", lambda e, dtap=dtap: e.matmul(ps_b[0:1, 0:128], dtap, TRI, start=True, stop=True),
                 waits=[bfree("ab1")], inc="st")
            site("tensor", lambda e, dtap=dtap: e.matmul(ps_c[0][:, 0:1], TRIS, dtap, start=True, stop=True),
                 waits=[bfree("c0")], inc="st")
            site("tensor", lambda e, sc=sc: e.transpose(ps_c[1][0:16, 0:128],
                                                        ssm[:, sc * 34 + 16: sc * 34 + 32], ID32),
                 waits=[bfree("c1")], inc="st")
            stv = t("st")
            site("vector", lambda e, sc=sc: e.tensor_copy(sc_ctm[:, sc:sc + 1], ps_a[:, 0:1]),
                 waits=[("st", stv)], inc="sv")
            bset("ab0", "sv")
            site("vector", lambda e, sc=sc: e.tensor_copy(sc_crow2[sc], ps_b[0:1, 0:128]), inc="sv")
            bset("ab1", "sv")
            site("vector", lambda e, sc=sc: e.tensor_copy(sc_r[:, sc:sc + 1], ps_c[0][:, 0:1]), inc="sv")
            bset("c0", "sv")
            site("vector", lambda e, sc=sc: e.tensor_copy(sc_Cf2[sc], ps_c[1][0:16, 0:128]), inc="sv")
            bset("c1", "sv")
            ev1 = t("sv")
            site("tensor", lambda e, sc=sc: e.matmul(ps_a[:, 0:128], ONES, sc_crow2[sc], start=True, stop=True),
                 waits=[("sv", ev1), bfree("ab0")], inc="st")
            site("tensor", lambda e, sc=sc: e.matmul(ps_b[0:16, 0:128], ONES[0:1, 0:16], sc_crow2[sc], start=True, stop=True),
                 waits=[bfree("ab1")], inc="st")
            site("tensor", lambda e, sc=sc: e.matmul(ps_b[0:16, 128:129], ONES[0:1, 0:16], sc_crow2[sc][0:1, 127:128],
                                                     start=True, stop=True), inc="st")
            stv = t("st")
            dTt = sc_dT if sc == 0 else sc_dT1
            site("vector", lambda e, sc=sc, dTt=dTt: e.scalar_tensor_tensor(
                dTt, ps_a[:, 0:128], sc_ctm[:, sc:sc + 1], UM, ALU.subtract, ALU.add),
                waits=[("st", stv)], inc="sv")
            bset("ab0", "sv")
            dT_v[sc] = t("sv")
            site("scalar", lambda e, sc=sc: e.activation(sc_E2s[sc], ps_b[0:16, 0:128], AF.Exp, scale=a_col),
                 waits=[("st", stv)], inc="ss")
            site("scalar", lambda e, sc=sc: e.activation(sc_dec2[sc], ps_b[0:16, 128:129], AF.Exp, scale=a_col), inc="ss")
            dec_s[sc] = t("ss")
            bset("ab1", "ss")
            sv_ = t("ss")
            site("vector", lambda e, sc=sc: e.tensor_tensor(Qt2[sc], sc_E2s[sc], sc_Cf2[sc], ALU.mult),
                 waits=[("ss", sv_)], inc="sv")
            site("vector", lambda e, sc=sc: e.tensor_scalar(sc_k0, ARB, sc_r[:, sc:sc + 1], None, ALU.mult),
                 waits=[("sv", t("sv"))], inc="sv")
            vv = t("sv")
            site("scalar", lambda e: e.activation(sc_k0, sc_k0, AF.Exp), waits=[("sv", vv)], inc="ss")
            sv_ = t("ss")
            site("vector", lambda e, sc=sc: e.tensor_tensor(Kt2[sc], sc_k0, ssm[:, sc * 34: sc * 34 + 16], ALU.mult),
                 waits=[("ss", sv_), ("sv", t("sv"))], inc="sv")
            kt_v[sc] = t("sv")
            pump()

        CBANKS = [("ab0", ps_a), ("ab1", ps_b), ("c0", ps_c[0]), ("c1", ps_c[1])]

        def emit_mblock(sc):
            dTt = sc_dT if sc == 0 else sc_dT1
            for n in range(16):
                bk, pb = CBANKS[n % 4]
                col = (n // 4) * 128
                site("tensor", lambda e, sc=sc, n=n, pb=pb, col=col: e.matmul(
                    pb[:, col:col + 128], ONESBF, cbflat2[sc][0:1, n * 128:(n + 1) * 128],
                    start=True, stop=True),
                    waits=[bfree(bk) if n < 4 else None, wlm(("cfl", sc))], inc="st")
                mark(("cbst", sc, n), "st")
            mark(("cbc", sc), "st")
            for n in range(16):
                ns = sc * 16 + n
                site("scalar", lambda e, sc=sc, n=n, dTt=dTt: e.activation(
                    sc_E[n % 2], dTt, AF.Exp, scale=ARB[:, n:n + 1]),
                    waits=[("sv", dT_v[sc]), wlm(("mteg", i, ns - 2))], inc="ss")
                mark(("exps", sc, n), "ss")

        def emit_mstt(sc, n):
            ns = sc * 16 + n
            bk, pb = CBANKS[n % 4]
            col = (n // 4) * 128
            site("vector", lambda e, sc=sc, n=n, pb=pb, col=col: e.scalar_tensor_tensor(
                Mt2[sc][:, n * 128:(n + 1) * 128], sc_E[n % 2],
                ssm[:, sc * 34 + n: sc * 34 + n + 1], pb[:, col:col + 128], ALU.mult, ALU.mult),
                waits=[wlm(("exps", sc, n)), wlm(("cbst", sc, n))], inc="sv")
            mark(("mteg", i, ns), "sv")
            mark(("mte", i, sc, n), "sv")
            if n >= 12:
                bset(CBANKS[n % 4][0], "sv")

        emit_mblock(0)
        for n in range(16):
            emit_mstt(0, n)
            if n % 4 == 3:
                pump()
        emit_mblock(1)

        def emit_intra_chunk(sc, interleave_next):
            hbf_w = wlm(("hbf",))
            for dc in range(16):
                bk = "y%d" % (dc % 2)
                pb = ps_y[dc % 2]
                for n in range(16):
                    wv = []
                    if n == 0:
                        wv = [hbf_w, bfree(bk), ("sv", xt_v[sc])]
                    wv.append(wlm(("mte", i, sc, n)))
                    site("tensor", lambda e, sc=sc, dc=dc, n=n, pb=pb: e.matmul(
                        pb[:, 0:128], Xt2[sc][:, dc * 128:(dc + 1) * 128],
                        Mt2[sc][:, n * 128:(n + 1) * 128], start=(n == 0), stop=False),
                        waits=wv, inc=None)
                site("tensor", lambda e, sc=sc, dc=dc, pb=pb: e.matmul(
                    pb[:, 0:128], h_bf[0:16, dc * 128:(dc + 1) * 128], Qt2[sc],
                    start=False, stop=True), inc="st")
                if dc == 15:
                    mark(("inter", i, sc), "st")
                stv = t("st")
                y1b = sc_y1 if dc % 2 == 0 else sc_y1b
                site("vector", lambda e, dc=dc, sc=sc, pb=pb, y1b=y1b: e.scalar_tensor_tensor(
                    y1b, xc[:, dc * 256 + sc * 128: dc * 256 + sc * 128 + 128],
                    DV[:, dc:dc + 1], pb[:, 0:128], ALU.mult, ALU.add),
                    waits=[("st", stv), wlm(("zse", i))], inc="sv")
                site("vector", lambda e, dc=dc, sc=sc, y1b=y1b: e.tensor_tensor(
                    x_main[:, dc * 259 + 3 + sc * 128: dc * 259 + 3 + sc * 128 + 128],
                    y1b, zs[:, dc * 256 + sc * 128: dc * 256 + sc * 128 + 128], ALU.mult),
                    waits=[("sv", t("sv"))], inc="sv")
                bset(bk, "sv")
                if interleave_next:
                    emit_mstt(1, dc)
                    if dc % 2 == 1:
                        pump()
                else:
                    pump()
            if sc == 1:
                mark(("gate", i), "sv")

        def emit_state(sc):
            for q in range(4):
                bk = "c%d" % (q % 2)
                pb = ps_c[q % 2]
                site("tensor", lambda e, sc=sc, q=q, pb=pb: e.matmul(
                    pb[0:16, 0:512], Kt2[sc], Xt2[sc][:, q * 512:(q + 1) * 512], start=True, stop=True),
                    waits=[("sv", kt_v[sc]), ("sv", xt_v[sc]), bfree(bk)], inc="st")
                stv = t("st")
                site("vector", lambda e, sc=sc, q=q, pb=pb: e.scalar_tensor_tensor(
                    h_T[0:16, q * 512:(q + 1) * 512], h_T[0:16, q * 512:(q + 1) * 512],
                    sc_dec2[sc], pb[0:16, 0:512], ALU.mult, ALU.add),
                    waits=[("st", stv), ("ss", dec_s[sc])], inc="sv")
                bset(bk, "sv")
            site("vector", lambda e: e.tensor_copy(h_bf, h_T),
                 waits=[wlm(("inter", i, sc)), ("sv", t("sv"))], inc="sv")
            mark(("hbf",), "sv")

        emit_intra_chunk(0, True)
        emit_state(0)
        emit_intra_chunk(1, False)
        emit_state(1)

        for o in range(8):
            _wm = "swo%d" % (o % 2)
            site("sync", lambda e, o=o: e.dma_start(
                out=WoB[o % 2].rearrange("p (c j) -> p c j", j=128),
                in_=d_wos[o].rearrange("c p j -> p c j")),
                waits=[wlm(("opg", i, o - 2)) or wlm(("opg", i - 1, o + 6)),
                       ("sgw0", sgw_tot[0]), ("sgw1", sgw_tot[1])], inc=_wm)
            wo_here = t(_wm)
            bk = "ab%d" % (o % 2)
            pb = ps_ab[o % 2]
            wv = [(_wm, wo_here), wlm(("gate", i)), bfree(bk)]
            for kc in range(16):
                site("tensor", lambda e, o=o, kc=kc, pb=pb: e.matmul(
                    pb[:, 0:256], WoB[o % 2][:, kc * 128:(kc + 1) * 128],
                    x_main[:, kc * 259 + 3: kc * 259 + 259],
                    start=(kc == 0), stop=(kc == 15)),
                    waits=wv if kc == 0 else (), inc="st" if kc == 15 else None)
            mark(("opg", i, o), "st")
            stv = t("st")
            site("scalar", lambda e, o=o, pb=pb: e.activation(
                ao[:, o * 256:(o + 1) * 256], pb[:, 0:256], AF.Identity, bias=B_O[:, o:o + 1]),
                waits=[("st", stv)], inc="ss")
            mark(("aoe", i, o), "ss")
            bset(bk, "ss")
            pump()
        mark(("outproj", i), "st")

        stv0 = stv1 = None
        for jj in range(2):
            bk = "t%d" % jj
            wv = [wlm(("aoe", i, 7)), bfree(bk)]
            for o in range(8):
                site("tensor", lambda e, jj=jj, o=o: e.transpose(
                    ps_t[jj][:, o * 128:(o + 1) * 128],
                    ao[:, o * 256 + jj * 128: o * 256 + jj * 128 + 128], IDBF),
                    waits=wv if o == 0 else (), inc="st" if o == 7 else None)
            if jj == 0:
                stv0 = t("st")
            else:
                mark(("aoT", i), "st")
                stv1 = t("st")

        pump_rest()
        lt = ln1x_tm2[i % 2]
        rr = [r_tm[:, 0:1024], r_tm[:, 1024:2048]]
        site("vector", lambda e, lt=lt, rr=rr: e.tensor_tensor(
            rr[0], ps_t[0][:, 0:1024], lt[:, 0:1024], ALU.add),
            waits=[("st", stv0)], inc="sv")
        site("vector", lambda e, lt=lt, rr=rr: e.tensor_tensor(
            rr[1], ps_t[1][:, 0:1024], lt[:, 1024:2048], ALU.add),
            waits=[("st", stv1)], inc="sv")
        bset("t0", "sv")
        bset("t1", "sv")
        if i + 1 < NTL:
            emit_lnT(i + 1)
        PUMPQ.append(make_epi(i))
        emit_xload(i + 2)
    pump_rest()
    lastA_ip = lm[("ipg", NTL - 1)]
    gA = (t("sg0"), t("sg1"))

    for g in range(40):
        _sm = "sw%d" % (g % 2)
        site("sync", lambda e, g=g: e.dma_start(out=wa[:, g * 512:(g + 1) * 512],
                                                in_=d_wbs[:, g * 512:(g + 1) * 512]),
             waits=[("st", lastA_ip[1]), ("sgw0", sgw_tot[0]), ("sgw1", sgw_tot[1])], inc=_sm)
    wbs_w = (t("sw0"), t("sw1"))
    site("vector", lambda e: e.engine_nop(), waits=[("sw0", wbs_w[0]), ("sw1", wbs_w[1])], inc="sv")
    weightsB_v = t("sv")

    xhB = {}

    def emit_x2load(i):
        if i >= NTL:
            return
        xb = x_tm[i % 2]
        _gp = "sg%d" % (i % 2)
        _xp_ = "sx%d" % (i % 2)
        w = [("sg0", gA[0]), ("sg1", gA[1])]
        if i >= 2:
            w.append((_gp, t(_gp)))
        site("sync", lambda e, i=i, xb=xb: e.dma_start(
            out=xb.rearrange("p (j d) -> p j d", d=1024),
            in_=d_x2[i * TT:(i + 1) * TT, :].rearrange("(j p) d -> p j d", p=128)),
            waits=w, inc=_xp_)
        xhB[i] = t(_xp_)

    def emit_ln3(i):
        xb = x_tm[i % 2]
        srcs = [xb[:, 0:1024], xb[:, 1024:2048]]

        def mknorm3(jj, src_):
            def f(e, mean, inv):
                return e.tensor_scalar(ln3_tm[:, jj * 1024:(jj + 1) * 1024], src_,
                                       mean, inv, ALU.subtract, ALU.mult)
            return f
        wpre = [("sx%d" % (i % 2), xhB[i]), ("sv", weightsB_v), wlm(("ln3u", i - 1))]
        ln_pair(srcs, [mknorm3(0, srcs[0]), mknorm3(1, srcs[1])], wpre, sb=2)
        mark(("ln3", i), "sv")

    def emit_lnT3(i):
        for jj in range(2):
            bk = "t%d" % jj
            wv = [wlm(("ln3", i)), bfree(bk)]
            for kc in range(8):
                site("tensor", lambda e, jj=jj, kc=kc: e.transpose(
                    ps_t[jj][:, kc * 128:(kc + 1) * 128],
                    ln3_tm[:, jj * 1024 + kc * 128: jj * 1024 + (kc + 1) * 128], IDBF),
                    waits=wv if kc == 0 else (), inc="st" if kc == 7 else None)
            stv = t("st")
            site("scalar", lambda e, jj=jj: e.copy(
                ab[:, 2048:4096].rearrange("p (k s) -> p k s", s=256)[:, :, jj * 128:(jj + 1) * 128],
                ps_t[jj][:, 0:1024].rearrange("p (k s) -> p k s", s=128)),
                waits=[("st", stv)], inc="ss")
            bset(bk, "ss")
        mark(("ln3fm", i), "ss")

    emit_x2load(0)
    emit_x2load(1)
    emit_ln3(0)
    emit_lnT3(0)

    for i in range(NTL):
        xb = x_tm[i % 2]

        BB4 = [("ab0", ps_a), ("ab1", ps_b), ("y0", ps_y[0]), ("y1", ps_y[1])]

        for r in range(2):
            bk, pb = BB4[r % 4]
            wv = [wlm(("ln3fm", i)), bfree(bk)]
            for kc in range(8):
                site("tensor", lambda e, r=r, kc=kc, pb=pb: e.matmul(
                    pb[:, 0:256], U1[:, kc * 256 + r * 128: kc * 256 + r * 128 + 128],
                    ab[:, 2048 + kc * 256: 2048 + (kc + 1) * 256],
                    start=(kc == 0), stop=(kc == 7)),
                    waits=wv if kc == 0 else (), inc="st" if kc == 7 else None)
            stv = t("st")
            site("scalar", lambda e, r=r, pb=pb: e.copy(
                Pf[:, r * 256:(r + 1) * 256], pb[:, 0:256]), waits=[("st", stv)], inc="ss")
            bset(bk, "ss")
        pf_s = t("ss")

        for f in range(32):
            bk, pb = BB4[f % 4]
            wv = [("ss", pf_s), bfree(bk)]
            for r in range(2):
                site("tensor", lambda e, f=f, r=r, pb=pb: e.matmul(
                    pb[:, 0:256], V1[:, r * 4096 + f * 128: r * 4096 + (f + 1) * 128],
                    Pf[:, r * 256:(r + 1) * 256], start=(r == 0), stop=(r == 1)),
                    waits=wv if r == 0 else (), inc="st" if r == 1 else None)
            stv = t("st")
            site("scalar", lambda e, f=f, pb=pb: e.activation(
                Hf[:, f * 256:(f + 1) * 256], pb[:, 0:256], AF.Gelu_apprx_tanh,
                bias=B1[:, f:f + 1]), waits=[("st", stv)], inc="ss")
            mark(("hfe", i, f), "ss")
            bset(bk, "ss")

        for r in range(2):
            bk, pb = BB4[r % 4]
            wv = [wlm(("hfe", i, 31)), bfree(bk)]
            for kc in range(32):
                site("tensor", lambda e, r=r, kc=kc, pb=pb: e.matmul(
                    pb[:, 0:256], U2[:, kc * 256 + r * 128: kc * 256 + r * 128 + 128],
                    Hf[:, kc * 256:(kc + 1) * 256], start=(kc == 0), stop=(kc == 31)),
                    waits=wv if kc == 0 else (), inc="st" if kc == 31 else None)
            stv = t("st")
            site("scalar", lambda e, r=r, pb=pb: e.copy(
                Qf[:, r * 256:(r + 1) * 256], pb[:, 0:256]), waits=[("st", stv)], inc="ss")
            bset(bk, "ss")
        qf_s = t("ss")

        for o in range(8):
            bk, pb = BB4[o % 4]
            wv = [("ss", qf_s), bfree(bk)]
            for r in range(2):
                site("tensor", lambda e, o=o, r=r, pb=pb: e.matmul(
                    pb[:, 0:256], V2[:, r * 1024 + o * 128: r * 1024 + (o + 1) * 128],
                    Qf[:, r * 256:(r + 1) * 256], start=(r == 0), stop=(r == 1)),
                    waits=wv if r == 0 else (), inc="st" if r == 1 else None)
            stv = t("st")
            site("scalar", lambda e, o=o, pb=pb: e.activation(
                fo[:, o * 256:(o + 1) * 256], pb[:, 0:256], AF.Identity, bias=B2[:, o:o + 1]),
                waits=[("st", stv)], inc="ss")
            mark(("foe", i, o), "ss")
            bset(bk, "ss")

        stv0 = stv1 = None
        for jj in range(2):
            bk = "t%d" % jj
            wv = [wlm(("foe", i, 7)), bfree(bk)]
            for o in range(8):
                site("tensor", lambda e, jj=jj, o=o: e.transpose(
                    ps_t[jj][:, o * 128:(o + 1) * 128],
                    fo[:, o * 256 + jj * 128: o * 256 + jj * 128 + 128], IDBF),
                    waits=wv if o == 0 else (), inc="st" if o == 7 else None)
            if jj == 0:
                stv0 = t("st")
            else:
                stv1 = t("st")
        site("vector", lambda e: e.tensor_copy(fo_tm[:, 0:1024], ps_t[0][:, 0:1024]),
             waits=[("st", stv0)], inc="sv")
        site("vector", lambda e: e.tensor_copy(fo_tm[:, 1024:2048], ps_t[1][:, 0:1024]),
             waits=[("st", stv1)], inc="sv")
        bset("t0", "sv")
        bset("t1", "sv")

        if i + 1 < NTL:
            emit_ln3(i + 1)
            emit_lnT3(i + 1)

        fos = [fo_tm[:, 0:1024], fo_tm[:, 1024:2048]]
        NRM = (r_tm[:, 0:1024], r_tm[:, 1024:2048])

        def mknorm4(jj, src_):
            def f(e, mean, inv):
                return e.tensor_scalar(NRM[jj][:, 0:1024], src_, mean, inv, ALU.subtract, ALU.mult)
            return f
        ln_pair(fos, [mknorm4(0, fos[0]), mknorm4(1, fos[1])], [("sv", t("sv"))], sb=0)
        for jj in (0, 1):
            site("vector", lambda e, jj=jj, xb=xb: e.tensor_tensor(
                xb[:, jj * 1024:(jj + 1) * 1024], xb[:, jj * 1024:(jj + 1) * 1024],
                NRM[jj][:, 0:1024], ALU.add), inc="sv")
        mark(("ln3u", i), "sv")
        vv = t("sv")
        site("gpsimd", lambda e, i=i, xb=xb: e.dma_start(
            out=d_out[i * TT:(i + 1) * TT, :].rearrange("(j p) d -> p j d", p=128),
            in_=xb.rearrange("p (j d) -> p j d", d=1024)),
            waits=[("sv", vv)], inc="sg%d" % (i % 2))
        emit_x2load(i + 2)

    site("gpsimd", lambda e: e.engine_nop(),
         waits=[("sg0", t("sg0")), ("sg1", t("sg1")), ("sgc", t("sgc"))])

    CONSTW = consts_w
    with nc.Block() as block:
        def mk(name):
            def f(e):
                seen = {}
                if name in ("tensor", "vector", "scalar", "gpsimd"):
                    e.wait_ge(SEM["swc"], CONSTW)
                    seen[id(SEM["swc"])] = CONSTW
                for (eng, fn, ws, inc) in sites:
                    if eng != name:
                        continue
                    for (sh, v) in ws:
                        if seen.get(id(sh), -1) < v:
                            e.wait_ge(sh, v)
                            seen[id(sh)] = v
                    inst = fn(e)
                    if inc is not None:
                        inst.then_inc(inc[0], inc[1])
            return f
        block.sync(mk("sync"))
        block.tensor(mk("tensor"))
        block.vector(mk("vector"))
        block.scalar(mk("scalar"))
        block.gpsimd(mk("gpsimd"))
    return nc


def _host_inputs(inputs):
    f32 = np.float32
    A = np.asarray(inputs["L0_attn_A_log"], f32)
    if not (A == A[0]).all():
        raise RuntimeError("A_log rows differ; general-A fallback not implemented")
    conv_w = np.asarray(inputs["L0_attn_conv_w"], f32)
    ident = np.eye(128, dtype=f32)
    u = np.arange(128)
    return {
        "w_inT": np.ascontiguousarray(np.asarray(inputs["L0_attn_in_proj_w"], f32).T),
        "b_in": np.ascontiguousarray(np.asarray(inputs["L0_attn_in_proj_b"], f32).reshape(32, 128).T),
        "convw": np.ascontiguousarray(conv_w[:, 0, :].reshape(16, 128, 4).transpose(1, 0, 2).reshape(128, 64)),
        "conv_b": np.ascontiguousarray(np.asarray(inputs["L0_attn_conv_b"], f32).reshape(16, 128).T),
        "xprojT": np.ascontiguousarray(np.asarray(inputs["L0_attn_x_proj_w"], f32).T),
        "xproj_b": np.ascontiguousarray(np.asarray(inputs["L0_attn_x_proj_b"], f32).reshape(33, 1)),
        "alog_row": np.ascontiguousarray(A[0:1, :]),
        "dvec": np.ascontiguousarray(np.asarray(inputs["L0_attn_D"], f32).reshape(16, 128).T),
        "w_o_omaj": np.ascontiguousarray(
            np.asarray(inputs["L0_attn_out_proj_w"], f32).reshape(8, 128, 16, 128).transpose(0, 2, 3, 1)),
        "b_out": np.ascontiguousarray(np.asarray(inputs["L0_attn_out_proj_b"], f32).reshape(8, 128).T),
        "gamma_bc": np.ascontiguousarray(np.tile(np.asarray(inputs["L0_attn_ln_gamma"], f32)[None, :], (128, 1))),
        "beta_bc": np.ascontiguousarray(np.tile(np.asarray(inputs["L0_attn_ln_beta"], f32)[None, :], (128, 1))),
        "u1": np.ascontiguousarray(np.asarray(inputs["L0_ffn_U1"], f32)),
        "v1": np.ascontiguousarray(np.asarray(inputs["L0_ffn_V1"], f32)),
        "b1": np.ascontiguousarray(np.asarray(inputs["L0_ffn_b1"], f32).reshape(32, 128).T),
        "u2": np.ascontiguousarray(np.asarray(inputs["L0_ffn_U2"], f32)),
        "v2": np.ascontiguousarray(np.asarray(inputs["L0_ffn_V2"], f32)),
        "b2": np.ascontiguousarray(np.asarray(inputs["L0_ffn_b2"], f32).reshape(8, 128).T),
        "ident32": ident,
        "identbf": ident.astype(BF),
        "tri": np.ascontiguousarray((u[:, None] <= u[None, :]).astype(f32)),
        "tris": np.ascontiguousarray((u[:, None] > u[None, :]).astype(f32)),
        "umask": np.ascontiguousarray((LARGE * (u[:, None] > u[None, :])).astype(f32)),
        "onesrow": np.ones((1, 128), f32),
        "epsv": np.full((128, 1), EPS, f32),
        "onesrowbf": np.ones((1, 128), BF),
    }


def kernel(**inputs):
    x = np.asarray(inputs["L0_attn_x"], np.float32)
    base = _host_inputs(inputs)
    nc = build_program()
    in_maps = [dict(base, x=np.ascontiguousarray(x[b])) for b in range(8)]
    res = run_bass_kernel_spmd(nc, in_maps, core_ids=list(range(8)))
    out = np.stack([np.asarray(res.results[b]["out"]) for b in range(8)], axis=0)
    return out.astype(np.float32)


# version 15
# speedup vs baseline: 1.1845x; 1.0675x over previous
import numpy as np
import ml_dtypes
import concourse.bass as bass
import concourse.mybir as mybir
from concourse.bass_utils import run_bass_kernel_spmd

F32 = mybir.dt.float32
BF16 = mybir.dt.bfloat16
AF = mybir.ActivationFunctionType
ALU = mybir.AluOpType
AXX = mybir.AxisListType.X
BF = ml_dtypes.bfloat16

S, D, DI, NS = 8192, 1024, 2048, 16
TT = 256
NT = S // TT
LARGE = 1.0e6
EPS = 1e-5
RD = 1.0 / D


def build_program(SL=S, dbg=False):
    NTL = SL // TT
    nc = bass.Bass("TRN2", target_bir_lowering=False, detect_race_conditions=False)

    dx = nc.dram_tensor("x", [SL, D], F32, kind="ExternalInput").ap()
    d_wi = nc.dram_tensor("w_inT", [D, 2 * DI], F32, kind="ExternalInput").ap()
    d_bin = nc.dram_tensor("b_in", [128, 32], F32, kind="ExternalInput").ap()
    d_cw = nc.dram_tensor("convw", [128, 64], F32, kind="ExternalInput").ap()
    d_cb = nc.dram_tensor("conv_b", [128, 16], F32, kind="ExternalInput").ap()
    d_xp = nc.dram_tensor("xprojT", [DI, 33], F32, kind="ExternalInput").ap()
    d_xpb = nc.dram_tensor("xproj_b", [33, 1], F32, kind="ExternalInput").ap()
    d_alog = nc.dram_tensor("alog_row", [1, NS], F32, kind="ExternalInput").ap()
    d_dvec = nc.dram_tensor("dvec", [128, 16], F32, kind="ExternalInput").ap()
    d_wo = nc.dram_tensor("w_o_omaj", [8, 16, 128, 128], F32, kind="ExternalInput").ap()
    d_bo = nc.dram_tensor("b_out", [128, 8], F32, kind="ExternalInput").ap()
    d_gb = nc.dram_tensor("gamma_bc", [128, D], F32, kind="ExternalInput").ap()
    d_bb = nc.dram_tensor("beta_bc", [128, D], F32, kind="ExternalInput").ap()
    d_u1 = nc.dram_tensor("u1", [D, 256], F32, kind="ExternalInput").ap()
    d_v1 = nc.dram_tensor("v1", [256, 4096], F32, kind="ExternalInput").ap()
    d_b1 = nc.dram_tensor("b1", [128, 32], F32, kind="ExternalInput").ap()
    d_u2 = nc.dram_tensor("u2", [4096, 256], F32, kind="ExternalInput").ap()
    d_v2 = nc.dram_tensor("v2", [256, D], F32, kind="ExternalInput").ap()
    d_b2 = nc.dram_tensor("b2", [128, 8], F32, kind="ExternalInput").ap()
    d_id32 = nc.dram_tensor("ident32", [128, 128], F32, kind="ExternalInput").ap()
    d_idbf = nc.dram_tensor("identbf", [128, 128], BF16, kind="ExternalInput").ap()
    d_tri = nc.dram_tensor("tri", [128, 128], F32, kind="ExternalInput").ap()
    d_tris = nc.dram_tensor("tris", [128, 128], F32, kind="ExternalInput").ap()
    d_um = nc.dram_tensor("umask", [128, 128], F32, kind="ExternalInput").ap()
    d_ones = nc.dram_tensor("onesrow", [1, 128], F32, kind="ExternalInput").ap()
    d_eps = nc.dram_tensor("epsv", [128, 1], F32, kind="ExternalInput").ap()
    d_onesbf = nc.dram_tensor("onesrowbf", [1, 128], BF16, kind="ExternalInput").ap()
    d_out = nc.dram_tensor("out", [SL, D], F32, kind="ExternalOutput").ap()
    d_x2 = nc.dram_tensor("x2scratch", [SL, D], F32,
                          kind="ExternalOutput" if dbg else "Internal").ap()
    if dbg:
        d_dab = nc.dram_tensor("dbg_ab", [128, 23040], BF16, kind="ExternalOutput").ap()
        d_daf = nc.dram_tensor("dbg_af", [128, 13056], F32, kind="ExternalOutput").ap()
    d_wos = nc.dram_tensor("wo_scr", [8, 16, 128, 128], BF16, kind="Internal").ap()
    d_wbs = nc.dram_tensor("wb_scr", [128, 20480], BF16, kind="Internal").ap()
    d_cds = nc.dram_tensor("cd_scr", [16, 4, 128, 128], BF16, kind="Internal").ap()

    wa = nc.alloc_sbuf_tensor("wa", [128, 33312], BF16)
    Wi = wa[:, 0:32768]
    XP = wa[:, 32768:33312]
    U1 = wa[:, 0:2048]
    V1 = wa[:, 2048:10240]
    U2 = wa[:, 10240:18432]
    V2 = wa[:, 18432:20480]

    wob = nc.alloc_sbuf_tensor("wob", [128, 4096], BF16)
    WoB = [wob[:, 0:2048], wob[:, 2048:4096]]

    ab = nc.alloc_sbuf_tensor("ab", [128, 29312], BF16)
    ln1x_tm2 = [ab[:, 0:2048], ab[:, 27264:29312]]
    ln1x_fm = ab[:, 2048:4096]
    ao = ln1x_fm
    x_main = ab[:, 4096:8240]
    zs = ab[:, 8240:12336]
    xc = ab[:, 12336:16432]
    Xt2 = [ab[:, 16432:18480], ab[:, 22848:24896]]
    Mt2 = [ab[:, 18480:20528], ab[:, 24896:26944]]
    h_bf = ab[0:16, 20528:22576]
    Qt2 = [ab[0:16, 22576:22704], ab[0:16, 26944:27072]]
    Kt2 = [ab[:, 27072:27088], ab[:, 27088:27104]]
    ln3_tm = ab[:, 0:2048]
    ln3_fm = ab[:, 2048:4096]
    Pf = ab[:, 4096:4608]
    Hf = ab[:, 4608:12800]
    Qf = ab[:, 12800:13312]
    fo = ab[:, 13312:15360]
    fo_tm = ab[:, 15360:17408]

    af = nc.alloc_sbuf_tensor("af", [128, 10916], F32)
    x_tm = [af[:, 0:2048], af[:, 2048:4096]]
    r_tm = af[:, 4096:6144]
    wst = [af[:, 6144:6656], af[:, 6656:7168]]
    h_T = af[0:16, 7168:9216]
    ssm = af[:, 9216:9284]
    scr = af[:, 9284:9828]
    stt = af[:, 9828:9892]
    cacc = af[:, 9892:10404]
    sc_y1b = af[:, 10404:10532]
    sc_dT1 = af[:, 10532:10660]
    ssm_fm = af[:, 10660:10916]
    sc_dt = scr[:, 0:2]
    sc_t0 = scr[:, 2:4]
    sc_t1 = scr[:, 4:6]
    sc_ctm = scr[:, 6:8]
    sc_r = scr[:, 8:10]
    sc_dT = scr[:, 16:144]
    sc_E = [scr[:, 144:272], scr[:, 272:400]]
    sc_y1 = scr[:, 400:528]
    sc_k0 = scr[:, 528:544]
    small = nc.alloc_sbuf_tensor("small", [128, 832], F32)
    sc_Cf2 = [small[0:16, 0:128], small[0:16, 128:256]]
    sc_E2s = [small[0:16, 256:384], small[0:16, 384:512]]
    sc_crow2 = [small[0:1, 512:640], small[0:1, 640:768]]
    a_col = small[0:16, 768:769]
    sc_dec2 = [small[0:16, 770:771], small[0:16, 771:772]]
    arow = small[0:1, 772:788]
    epsv = small[:, 800:801]
    nc.const_aps.register(F32, EPS, epsv)

    cf = nc.alloc_sbuf_tensor("cf", [128, 1664], F32)
    ID32 = cf[:, 0:128]
    TRI = cf[:, 128:256]
    TRIS = cf[:, 256:384]
    UM = cf[:, 384:512]
    ONES = cf[0:1, 512:640]
    GB = cf[:, 640:1664]
    cf2 = nc.alloc_sbuf_tensor("cf2", [128, 1248], F32)
    BB = cf2[:, 0:1024]
    B_IN = cf2[:, 1024:1056]
    C_B = cf2[:, 1056:1072]
    B_O = cf2[:, 1072:1080]
    DV = cf2[:, 1080:1096]
    B1 = cf2[:, 1096:1128]
    B2 = cf2[:, 1128:1136]
    XPB = cf2[0:33, 1136:1137]
    ARB = cf2[:, 1140:1156]
    CW = cf2[:, 1156:1220]
    cbf = nc.alloc_sbuf_tensor("cbf", [128, 256], BF16)
    IDBF = cbf[:, 0:128]
    ONESBF = cbf[0:1, 128:256]
    _dw = nc.alloc_sbuf_tensor("dwwin", [128, 4096], BF16)
    DW = [_dw[:, 0:2048], _dw[:, 2048:4096]]
    _cb0 = nc.alloc_sbuf_tensor("cbflat", [1, 4096], BF16)
    cbflat2 = [_cb0[0:1, 0:2048], _cb0[0:1, 2048:4096]]
    sqdump = nc.alloc_sbuf_tensor("sqdump", [128, 1024], BF16)
    ssmbf = nc.alloc_sbuf_tensor("ssmbf", [33, 256], BF16)

    ps_a = nc.alloc_psum_tensor("ps_a", [128, 512], F32)
    ps_b = nc.alloc_psum_tensor("ps_b", [128, 512], F32)
    ps_ab = [ps_a, ps_b]
    ps_t = [nc.alloc_psum_tensor("ps_t1", [128, 1024], BF16),
            nc.alloc_psum_tensor("ps_t2", [128, 1024], BF16)]
    ps_y = [nc.alloc_psum_tensor("ps_y1", [128, 512], F32),
            nc.alloc_psum_tensor("ps_y2", [128, 512], F32)]
    ps_c = [nc.alloc_psum_tensor("ps_c1", [128, 512], F32),
            nc.alloc_psum_tensor("ps_c2", [128, 512], F32)]

    import contextlib
    stx = contextlib.ExitStack()
    SEM, TOT = {}, {}
    for name in ("swc", "sw0", "sw1", "sx0", "sx1", "swo0", "swo1",
                 "sg0", "sg1", "sgw0", "sgw1", "sgc", "sdw0", "sdw1", "st", "sv", "ss"):
        SEM[name] = stx.enter_context(nc.semaphore(name))
        TOT[name] = 0
    DMA_SEMS = ("swc", "sw0", "sw1", "sx0", "sx1", "swo0", "swo1",
                "sg0", "sg1", "sgw0", "sgw1", "sgc", "sdw0", "sdw1")

    sites = []
    lm = {}
    bank = {}

    def site(eng, fn, waits=(), inc=None):
        ws = []
        for w in waits:
            if w is None:
                continue
            sname, v = w
            if v > 0:
                ws.append((SEM[sname], v))
        if inc is not None:
            TOT[inc] += 16 if inc in DMA_SEMS else 1
        sites.append((eng, fn, ws,
                      (SEM[inc], 16 if inc in DMA_SEMS else 1) if inc else None))

    def t(s):
        return TOT[s]

    def mark(key, sem):
        lm[key] = (sem, TOT[sem])

    def wlm(key):
        e = lm.get(key)
        return (e[0], e[1]) if e else None

    def bfree(bk):
        e = bank.get(bk)
        return (e[0], e[1]) if e else None

    def bset(bk, sem):
        bank[bk] = (sem, TOT[sem])

    for (dst, src) in ((ID32, d_id32), (TRI, d_tri), (TRIS, d_tris), (UM, d_um),
                       (ONES, d_ones), (GB, d_gb), (BB, d_bb), (B_IN, d_bin),
                       (C_B, d_cb), (B_O, d_bo), (DV, d_dvec), (B1, d_b1),
                       (B2, d_b2), (XPB, d_xpb), (IDBF, d_idbf), (ONESBF, d_onesbf),
                       (CW, d_cw), (arow, d_alog), (epsv, d_eps)):
        site("sync", lambda e, o=dst, i=src: e.dma_start(out=o, in_=i), inc="swc")
    consts_w = t("swc")

    site("scalar", lambda e: e.activation(arow, arow, AF.Exp),
         waits=[("swc", consts_w)], inc="ss")
    site("vector", lambda e: e.tensor_scalar(arow, arow, -1.0, None, ALU.mult),
         waits=[("ss", t("ss"))], inc="sv")
    av = t("sv")
    site("tensor", lambda e: e.matmul(ps_y[0][:, 0:16], ONES, arow, start=True, stop=True),
         waits=[("sv", av)], inc="st")
    site("tensor", lambda e: e.matmul(ps_y[1][0:16, 0:1], arow, ONES[0:1, 0:1], start=True, stop=True),
         inc="st")
    sv0 = t("st")
    site("vector", lambda e: e.tensor_copy(ARB, ps_y[0][:, 0:16]), waits=[("st", sv0)], inc="sv")
    site("vector", lambda e: e.tensor_copy(a_col, ps_y[1][0:16, 0:1]), inc="sv")
    bset("y0", "sv")
    bset("y1", "sv")
    site("vector", lambda e: e.memset(
        x_main.rearrange("p (c w) -> p c w", w=259)[:, :, 0:3], 0.0), inc="sv")
    site("vector", lambda e: e.memset(h_T, 0.0), inc="sv")
    site("vector", lambda e: e.memset(h_bf, 0.0), inc="sv")
    init_v = t("sv")

    xh = {}
    for _i in (0, 1):
        site("gpsimd", lambda e, i=_i: e.dma_start(
            out=x_tm[i % 2].rearrange("p (j d) -> p j d", d=1024),
            in_=dx[i * TT:(i + 1) * TT, :].rearrange("(j p) d -> p j d", p=128)),
            inc="sx%d" % (_i % 2))
        xh[_i] = t("sx%d" % (_i % 2))

    cseq = [0]

    def loadcast(src_ap, dst_ap, ncols, extra_cast_waits=()):
        j = cseq[0]
        stg = wst[j % 2]
        sm = "sw%d" % (j % 2)

        def _dma(e, o=stg, i=src_ap, n=ncols):
            oo = o[:, 0:n]
            if len(i.ap) == 3:
                oo = oo.rearrange("p (c j) -> p c j", c=i.ap[1][1])
            return e.dma_start(out=oo, in_=i)
        site("sync", _dma, waits=[wlm(("stg", (j - 2) % 2))], inc=sm)
        wv = t(sm)
        site("vector", lambda e, o=dst_ap, i=stg, n=ncols: e.tensor_copy(o, i[:, 0:n]),
             waits=[(sm, wv)] + list(extra_cast_waits), inc="sv")
        mark(("stg", j % 2), "sv")
        cseq[0] += 1

    for q in range(8):
        for kc in range(8):
            loadcast(d_wi[kc * 128:(kc + 1) * 128, q * 512:(q + 1) * 512],
                     Wi[:, kc * 4096 + q * 512: kc * 4096 + (q + 1) * 512], 512)
        mark(("wiq", q), "sv")
    for hh in range(2):
        j = cseq[0]
        stg = wst[j % 2]
        _sm = "sw%d" % (j % 2)
        site("sync", lambda e, o=stg, hh=hh: e.dma_start(
            out=o[:, 0:264].rearrange("p (c j) -> p c j", j=33),
            in_=d_xp.rearrange("(c p) j -> p c j", p=128)[:, hh * 8:(hh + 1) * 8, :]),
            waits=[wlm(("stg", (j - 2) % 2))], inc=_sm)
        wv = t(_sm)
        site("vector", lambda e, i=stg, hh=hh: e.tensor_copy(
            XP.rearrange("p (c j) -> p c j", j=34)[:, hh * 8:(hh + 1) * 8, 0:33],
            i[:, 0:264].rearrange("p (c j) -> p c j", j=33)),
            waits=[(_sm, wv)], inc="sv")
        mark(("stg", j % 2), "sv")
        cseq[0] += 1
    for g in range(4):
        w0 = [("swc", consts_w), wlm(("dwb", g % 2))]
        for ci in range(4):
            cch = 4 * g + ci
            for k in range(4):
                site("vector", lambda e, cch=cch, ci=ci, k=k, g=g: e.tensor_scalar(
                    DW[g % 2][:, (ci * 4 + k) * 128:(ci * 4 + k + 1) * 128],
                    IDBF, CW[:, cch * 4 + k:cch * 4 + k + 1], None, ALU.mult),
                    waits=w0 if (ci == 0 and k == 0) else (), inc="sv")
        vv = t("sv")
        _gm = "sgw%d" % (g % 2)
        site("gpsimd", lambda e, g=g: e.dma_start(
            out=d_cds[4 * g:4 * g + 4].rearrange("c k p j -> p c k j"),
            in_=DW[g % 2].rearrange("p (c k j) -> p c k j", c=4, j=128)),
            waits=[("sv", vv)], inc=_gm)
        mark(("dwb", g % 2), _gm)
    cds_done = (t("sgw0"), t("sgw1"))

    weightsA_v = t("sv")
    for o in range(8):
        for hh in range(4):
            j = cseq[0]
            stg = wst[j % 2]
            _sm = "sw%d" % (j % 2)
            site("sync", lambda e, o_=stg, i=d_wo[o, hh * 4:(hh + 1) * 4].rearrange("c p j -> p c j"):
                 e.dma_start(out=o_[:, 0:512].rearrange("p (c j) -> p c j", j=128), in_=i),
                 waits=[wlm(("stg", (j - 2) % 2))], inc=_sm)
            wv = t(_sm)
            dstb = WoB[j % 2][:, 0:512]
            site("vector", lambda e, o_=dstb, i=stg: e.tensor_copy(o_, i[:, 0:512]),
                 waits=[(_sm, wv), wlm(("wob", j % 2))], inc="sv")
            mark(("stg", j % 2), "sv")
            vv = t("sv")
            _gm = "sgw%d" % (j % 2)
            site("gpsimd", lambda e, o_=d_wos[o, hh * 4:(hh + 1) * 4].rearrange("c p j -> p c j"),
                 i=dstb.rearrange("p (c j) -> p c j", j=128): e.dma_start(out=o_, in_=i),
                 waits=[("sv", vv)], inc=_gm)
            mark(("wob", j % 2), _gm)
            cseq[0] += 1
    jobsB0 = []
    for kc in range(8):
        jobsB0.append((d_u1[kc * 128:(kc + 1) * 128, :], kc * 256, 256))
    for r in range(2):
        for q in range(8):
            jobsB0.append((d_v1[r * 128:(r + 1) * 128, q * 512:(q + 1) * 512],
                           2048 + r * 4096 + q * 512, 512))
    for g in range(16):
        jobsB0.append((d_u2[g * 256:(g + 1) * 256, :].rearrange("(c p) j -> p c j", p=128),
                       10240 + g * 512, 512))
    for r in range(2):
        for q in range(2):
            jobsB0.append((d_v2[r * 128:(r + 1) * 128, q * 512:(q + 1) * 512],
                           18432 + r * 1024 + q * 512, 512))
    for (srcap, dcol, ncols) in jobsB0:
        j = cseq[0]
        stg = wst[j % 2]
        sm = "sw%d" % (j % 2)

        def _dma0(e, o=stg, i=srcap, n=ncols):
            oo = o[:, 0:n]
            if len(i.ap) == 3:
                oo = oo.rearrange("p (c j) -> p c j", c=i.ap[1][1])
            return e.dma_start(out=oo, in_=i)
        site("sync", _dma0, waits=[wlm(("stg", (j - 2) % 2))], inc=sm)
        wv = t(sm)
        dstb = WoB[j % 2][:, 0:ncols]
        site("vector", lambda e, o=dstb, i=stg, n=ncols: e.tensor_copy(o, i[:, 0:n]),
             waits=[(sm, wv), wlm(("wob", j % 2))], inc="sv")
        mark(("stg", j % 2), "sv")
        vv = t("sv")
        _gm = "sgw%d" % (j % 2)
        site("gpsimd", lambda e, o=dcol, n=ncols, i=dstb: e.dma_start(
            out=d_wbs[:, o:o + n], in_=i), waits=[("sv", vv)], inc=_gm)
        mark(("wob", j % 2), _gm)
        cseq[0] += 1
    sgw_tot = (t("sgw0"), t("sgw1"))

    def ln_pair_gen(srcs, norm_fns, wait_pre, sb=0):
        for j in (0, 1):
            b = 8 * (j + sb)
            site("vector", lambda e, j=j, b=b: e.reduce_sum(stt[:, b + 0:b + 1], srcs[j], axis=AXX),
                 waits=wait_pre if j == 0 else (), inc="sv")
        yield
        vv0 = t("sv")
        for j in (0, 1):
            b = 8 * (j + sb)
            site("scalar", lambda e, j=j, b=b: e.activation(
                sqdump[:, 0:1024], srcs[j], AF.Square,
                accum_out=stt[:, b + 1:b + 2]), waits=[("sv", vv0)] if j == 0 else (), inc="ss")
        yield
        for j in (0, 1):
            b = 8 * (j + sb)
            site("vector", lambda e, j=j, b=b: e.tensor_scalar(
                stt[:, b + 2:b + 3], stt[:, b + 0:b + 1], RD, None, ALU.mult), inc="sv")
        for j in (0, 1):
            b = 8 * (j + sb)
            site("vector", lambda e, j=j, b=b: e.tensor_tensor(
                stt[:, b + 5:b + 6], stt[:, b + 2:b + 3], stt[:, b + 2:b + 3], ALU.mult), inc="sv")
        yield
        vvs = {}
        sq_s = t("ss")
        for j in (0, 1):
            b = 8 * (j + sb)
            site("vector", lambda e, j=j, b=b: e.scalar_tensor_tensor(
                stt[:, b + 3:b + 4], stt[:, b + 1:b + 2], RD, stt[:, b + 5:b + 6],
                ALU.mult, ALU.subtract), waits=[("ss", sq_s)] if j == 0 else (), inc="sv")
            vvs[j] = t("sv")
        for j in (0, 1):
            b = 8 * (j + sb)
            site("scalar", lambda e, j=j, b=b: e.activation(
                stt[:, b + 5:b + 6], stt[:, b + 3:b + 4], AF.Sqrt, bias=EPS),
                waits=[("sv", vvs[1])], inc="ss")
        yield
        sv_ = t("ss")
        for j in (0, 1):
            b = 8 * (j + sb)
            site("vector", lambda e, j=j, b=b: e.reciprocal(stt[:, b + 4:b + 5], stt[:, b + 5:b + 6]),
                 waits=[("ss", sv_)] if j == 0 else (), inc="sv")
        yield
        for j in (0, 1):
            b = 8 * (j + sb)
            site("vector", lambda e, j=j, b=b: norm_fns[j](e, stt[:, b + 2:b + 3], stt[:, b + 4:b + 5]), inc="sv")

    def ln_pair(srcs, norm_fns, wait_pre, sb=0):
        for _ in ln_pair_gen(srcs, norm_fns, wait_pre, sb=sb):
            pass

    PUMPQ = []

    def pump(nsteps=1):
        for _ in range(nsteps):
            while PUMPQ:
                try:
                    next(PUMPQ[0])
                    return
                except StopIteration:
                    PUMPQ.pop(0)
            return

    def pump_rest():
        while PUMPQ:
            try:
                next(PUMPQ[0])
            except StopIteration:
                PUMPQ.pop(0)

    def emit_xload(i):
        if i >= NTL:
            return
        xb = x_tm[i % 2]
        _gp = "sg%d" % (i % 2)
        _xp_ = "sx%d" % (i % 2)
        w = [(_gp, 16 * ((i - 2) // 2 + 1))] if i >= 2 else []
        site("gpsimd", lambda e, i=i, xb=xb: e.dma_start(
            out=xb.rearrange("p (j d) -> p j d", d=1024),
            in_=dx[i * TT:(i + 1) * TT, :].rearrange("(j p) d -> p j d", p=128)),
            waits=w, inc=_xp_)
        xh[i] = t(_xp_)

    def ln1_gen(i):
        xb = x_tm[i % 2]
        lt = ln1x_tm2[i % 2]
        srcs = [xb[:, 0:1024], xb[:, 1024:2048]]

        def mknorm(jj, src_):
            def f(e, mean, inv):
                return e.tensor_scalar(lt[:, jj * 1024:(jj + 1) * 1024], src_,
                                       mean, inv, ALU.subtract, ALU.mult)
            return f
        wpre = [("sx%d" % (i % 2), xh[i]), ("sv", init_v),
                wlm(("lnT", i - 2, 0)), wlm(("lnT", i - 2, 1))]
        yield from ln_pair_gen(srcs, [mknorm(0, srcs[0]), mknorm(1, srcs[1])], wpre, sb=2)
        mark(("ln1", i), "sv")

    def emit_lnT(i):
        lt = ln1x_tm2[i % 2]
        for jj in range(2):
            bk = "t%d" % jj
            wv = [wlm(("ln1", i)), bfree(bk), wlm(("aoT", i - 1))]
            for kc in range(8):
                site("tensor", lambda e, jj=jj, kc=kc, lt=lt: e.transpose(
                    ps_t[jj][:, kc * 128:(kc + 1) * 128],
                    lt[:, jj * 1024 + kc * 128: jj * 1024 + (kc + 1) * 128], IDBF),
                    waits=wv if kc == 0 else (), inc="st" if kc == 7 else None)
            mark(("lnT", i, jj), "st")
            stv = t("st")
            site("scalar", lambda e, jj=jj: e.copy(
                ab[:, 2048:4096].rearrange("p (k s) -> p k s", s=256)[:, :, jj * 128:(jj + 1) * 128],
                ps_t[jj][:, 0:1024].rearrange("p (k s) -> p k s", s=128)),
                waits=[("st", stv)], inc="ss")
            bset(bk, "ss")
        mark(("ln1fm", i), "ss")

    def make_epi(i):
        xb = x_tm[i % 2]
        rr = [r_tm[:, 0:1024], r_tm[:, 1024:2048]]

        def mkinp(rj_):
            def f(e, mean, inv):
                return e.tensor_scalar(rj_, rj_, mean, inv, ALU.subtract, ALU.mult)
            return f

        def gen():
            yield from ln_pair_gen(rr, [mkinp(rr[0]), mkinp(rr[1])], [("sv", t("sv"))], sb=0)
            yield
            for jj in (0, 1):
                site("vector", lambda e, jj=jj: e.tensor_tensor(rr[jj], rr[jj], GB, ALU.mult), inc="sv")
            yield
            for jj in (0, 1):
                site("vector", lambda e, jj=jj: e.tensor_tensor(rr[jj], rr[jj], BB, ALU.add), inc="sv")
            yield
            yield from ln_pair_gen(rr, [mkinp(rr[0]), mkinp(rr[1])], [("sv", t("sv"))], sb=0)
            yield
            for jj in (0, 1):
                site("vector", lambda e, jj=jj: e.tensor_tensor(
                    xb[:, jj * 1024:(jj + 1) * 1024], xb[:, jj * 1024:(jj + 1) * 1024],
                    rr[jj], ALU.add), inc="sv")
            mark(("x2", i), "sv")
            yield
            vv = t("sv")
            site("gpsimd", lambda e, i=i: e.dma_start(
                out=d_x2[i * TT:(i + 1) * TT, :].rearrange("(j p) d -> p j d", p=128),
                in_=xb.rearrange("p (j d) -> p j d", d=1024)),
                waits=[("sv", vv)], inc="sg%d" % (i % 2))
            emit_xload(i + 2)
            yield
        return gen()

    for _ in ln1_gen(0):
        pass
    mark(("aoT", -1), "st")
    emit_lnT(0)

    for i in range(NTL):
        xb = x_tm[i % 2]
        if i + 1 < NTL:
            PUMPQ.append(ln1_gen(i + 1))

        def emit_dw_stream(g):
            _dm = "sdw%d" % (g % 2)
            prev = wlm(("dwu", i, g - 2)) if g >= 2 else wlm(("dwu", i - 1, g + 2))
            site("sync", lambda e, g=g: e.dma_start(
                out=DW[g % 2].rearrange("p (c k j) -> p c k j", c=4, j=128),
                in_=d_cds[4 * g:4 * g + 4].rearrange("c k p j -> p c k j")),
                waits=[prev, ("sgw0", cds_done[0]), ("sgw1", cds_done[1])], inc=_dm)
            lm[("dws", i, g)] = (_dm, t(_dm))

        def emit_conv_te(c):
            g = c // 4
            win = DW[g % 2]
            pb = ps_y[c % 2]
            bk = "y%d" % (c % 2)
            wv = [wlm(("xme", i, c)), wlm(("halo", i - 1, c)), wlm(("dws", i, g)), bfree(bk)]
            for k in range(4):
                site("tensor", lambda e, c=c, k=k, win=win, pb=pb: e.matmul(
                    pb[:, 0:256],
                    win[:, ((c % 4) * 4 + k) * 128:((c % 4) * 4 + k + 1) * 128],
                    x_main[:, c * 259 + k: c * 259 + k + 256],
                    start=(k == 0), stop=(k == 3)),
                    waits=wv if k == 0 else (), inc="st" if k == 3 else None)
            if c % 4 == 3:
                mark(("dwu", i, g), "st")
            stv = t("st")
            site("scalar", lambda e, c=c, pb=pb: e.activation(
                xc[:, c * 256:(c + 1) * 256], pb[:, 0:256], AF.Silu, bias=C_B[:, c:c + 1]),
                waits=[("st", stv)], inc="ss")
            mark(("xce", i, c), "ss")
            bset(bk, "ss")
            site("vector", lambda e, c=c: e.tensor_copy(
                x_main[:, c * 259: c * 259 + 3],
                x_main[:, c * 259 + 256: c * 259 + 259]),
                waits=[("st", stv)], inc="sv")
            mark(("halo", i, c), "sv")

        emit_dw_stream(0)
        emit_dw_stream(1)
        for o in range(32):
            bk = "ab%d" % (o % 2)
            wv = [wlm(("ln1fm", i)), wlm(("wiq", o // 4)), bfree(bk)]
            if i > 0 and o < 16:
                wv.append(wlm(("outproj", i - 1)))
            if i > 0 and o >= 16:
                wv.append(wlm(("gate", i - 1)))
            pb = ps_ab[o % 2]
            for kc in range(8):
                site("tensor", lambda e, o=o, kc=kc, pb=pb: e.matmul(
                    pb[:, 0:256],
                    Wi[:, kc * 4096 + o * 128: kc * 4096 + (o + 1) * 128],
                    ab[:, 2048 + kc * 256: 2048 + (kc + 1) * 256],
                    start=(kc == 0), stop=(kc == 7)),
                    waits=wv if kc == 0 else (), inc="st" if kc == 7 else None)
            if o == 31:
                mark(("ipg", i), "st")
            stv = t("st")
            if o < 16:
                c = o
                wv2 = [("st", stv), wlm(("halo", i - 1, c))]
                site("scalar", lambda e, c=c, pb=pb: e.activation(
                    x_main[:, c * 259 + 3: c * 259 + 259], pb[:, 0:256],
                    AF.Identity, bias=B_IN[:, c:c + 1]), waits=wv2, inc="ss")
                mark(("xme", i, c), "ss")
            else:
                c = o - 16
                site("scalar", lambda e, c=c, o=o, pb=pb: e.activation(
                    zs[:, c * 256:(c + 1) * 256], pb[:, 0:256],
                    AF.Silu, bias=B_IN[:, o:o + 1]), waits=[("st", stv)], inc="ss")
            mark(("ipe", i, o), "ss")
            if o == 31:
                mark(("zse", i), "ss")
            bset(bk, "ss")
            if 1 <= o <= 16:
                emit_conv_te(o - 1)
                if o - 1 == 7:
                    emit_dw_stream(2)
                if o - 1 == 11:
                    emit_dw_stream(3)

        for kc in range(16):
            wv = [wlm(("xce", i, kc))]
            if kc == 0:
                wv.append(bfree("ab0"))
                wv.append(("sv", weightsA_v))
            site("tensor", lambda e, kc=kc: e.matmul(
                ps_a[0:33, 0:256], XP[:, kc * 34: kc * 34 + 33],
                xc[:, kc * 256:(kc + 1) * 256],
                start=(kc == 0), stop=(kc == 15)),
                waits=wv, inc="st" if kc == 15 else None)
        stv = t("st")
        site("scalar", lambda e: e.activation(ssm_fm[0:33, 0:256], ps_a[0:33, 0:256],
                                              AF.Identity, bias=XPB), waits=[("st", stv)], inc="ss")
        site("scalar", lambda e: e.activation(ssmbf[0:33, 0:256], ps_a[0:33, 0:256],
                                              AF.Identity, bias=XPB), waits=[wlm(("cbfu",))], inc="ss")
        ssmfm_s = t("ss")
        bset("ab0", "ss")

        dt_v = {}
        for sc in range(2):
            site("tensor", lambda e, sc=sc: e.transpose(
                ps_y[sc][:, 0:33], ssm_fm[0:33, sc * 128:(sc + 1) * 128], ID32[0:33, 0:33]),
                waits=[("ss", ssmfm_s), bfree("y%d" % sc)], inc="st")
            stv = t("st")
            site("vector", lambda e, sc=sc: e.tensor_copy(ssm[:, sc * 34: sc * 34 + 33], ps_y[sc][:, 0:33]),
                 waits=[("st", stv)], inc="sv")
            bset("y%d" % sc, "sv")
            site("scalar", lambda e, sc=sc: e.dma_start(
                out=cbflat2[sc], in_=ssmbf[16:32, sc * 128:(sc + 1) * 128]),
                waits=[("ss", ssmfm_s), wlm(("cbc", sc))], inc="sgc")
            mark(("cfl", sc), "sgc")
            if sc == 1:
                mark(("cbfu",), "sgc")
            vv = t("sv")
            site("scalar", lambda e, sc=sc: e.activation(sc_t0[:, sc:sc + 1], ssm[:, sc * 34 + 32:sc * 34 + 33], AF.Abs),
                 waits=[("sv", vv)], inc="ss")
            site("scalar", lambda e, sc=sc: e.activation(sc_t1[:, sc:sc + 1], ssm[:, sc * 34 + 32:sc * 34 + 33], AF.Relu), inc="ss")
            site("scalar", lambda e, sc=sc: e.activation(sc_t0[:, sc:sc + 1], sc_t0[:, sc:sc + 1], AF.Exp, scale=-1.0),
                 waits=[("ss", t("ss"))], inc="ss")
            site("scalar", lambda e, sc=sc: e.activation(sc_t0[:, sc:sc + 1], sc_t0[:, sc:sc + 1], AF.Ln, bias=1.0),
                 waits=[("ss", t("ss"))], inc="ss")
            sv_ = t("ss")
            site("vector", lambda e, sc=sc: e.tensor_tensor(sc_dt[:, sc:sc + 1], sc_t0[:, sc:sc + 1], sc_t1[:, sc:sc + 1], ALU.add),
                 waits=[("ss", sv_)], inc="sv")
            dt_v[sc] = t("sv")
            pump()

        xt_v = {}
        for sc in range(2):
            for half in range(2):
                bk = "t%d" % half
                for q in range(8):
                    kc = half * 8 + q
                    wv = [wlm(("xce", i, kc))]
                    if q == 0:
                        wv.append(bfree(bk))
                    site("tensor", lambda e, kc=kc, half=half, sc=sc, q=q: e.transpose(
                        ps_t[half][:, q * 128:(q + 1) * 128],
                        xc[:, kc * 256 + sc * 128: kc * 256 + sc * 128 + 128], IDBF),
                        waits=wv, inc="st" if q == 7 else None)
                stv = t("st")
                site("vector", lambda e, half=half, sc=sc: e.tensor_scalar(
                    Xt2[sc][:, half * 1024:(half + 1) * 1024], ps_t[half][:, 0:1024],
                    sc_dt[:, sc:sc + 1], None, ALU.mult),
                    waits=[("st", stv), ("sv", dt_v[sc])], inc="sv")
                bset(bk, "sv")
            xt_v[sc] = t("sv")
            pump()

        dT_v = {}
        kt_v = {}
        dec_s = {}
        for sc in range(2):
            dtap = sc_dt[:, sc:sc + 1]
            site("tensor", lambda e, dtap=dtap: e.matmul(ps_a[:, 0:1], TRI, dtap, start=True, stop=True),
                 waits=[("sv", dt_v[sc]), bfree("ab0")], inc="st")
            site("tensor", lambda e, dtap=dtap: e.matmul(ps_b[0:1, 0:128], dtap, TRI, start=True, stop=True),
                 waits=[bfree("ab1")], inc="st")
            site("tensor", lambda e, dtap=dtap: e.matmul(ps_c[0][:, 0:1], TRIS, dtap, start=True, stop=True),
                 waits=[bfree("c0")], inc="st")
            site("tensor", lambda e, sc=sc: e.transpose(ps_c[1][0:16, 0:128],
                                                        ssm[:, sc * 34 + 16: sc * 34 + 32], ID32),
                 waits=[bfree("c1")], inc="st")
            stv = t("st")
            site("vector", lambda e, sc=sc: e.tensor_copy(sc_ctm[:, sc:sc + 1], ps_a[:, 0:1]),
                 waits=[("st", stv)], inc="sv")
            bset("ab0", "sv")
            site("vector", lambda e, sc=sc: e.tensor_copy(sc_crow2[sc], ps_b[0:1, 0:128]), inc="sv")
            bset("ab1", "sv")
            site("vector", lambda e, sc=sc: e.tensor_copy(sc_r[:, sc:sc + 1], ps_c[0][:, 0:1]), inc="sv")
            bset("c0", "sv")
            site("vector", lambda e, sc=sc: e.tensor_copy(sc_Cf2[sc], ps_c[1][0:16, 0:128]), inc="sv")
            bset("c1", "sv")
            ev1 = t("sv")
            site("tensor", lambda e, sc=sc: e.matmul(ps_a[:, 0:128], ONES, sc_crow2[sc], start=True, stop=True),
                 waits=[("sv", ev1), bfree("ab0")], inc="st")
            site("tensor", lambda e, sc=sc: e.matmul(ps_b[0:16, 0:128], ONES[0:1, 0:16], sc_crow2[sc], start=True, stop=True),
                 waits=[bfree("ab1")], inc="st")
            site("tensor", lambda e, sc=sc: e.matmul(ps_b[0:16, 128:129], ONES[0:1, 0:16], sc_crow2[sc][0:1, 127:128],
                                                     start=True, stop=True), inc="st")
            stv = t("st")
            dTt = sc_dT if sc == 0 else sc_dT1
            site("vector", lambda e, sc=sc, dTt=dTt: e.scalar_tensor_tensor(
                dTt, ps_a[:, 0:128], sc_ctm[:, sc:sc + 1], UM, ALU.subtract, ALU.add),
                waits=[("st", stv)], inc="sv")
            bset("ab0", "sv")
            dT_v[sc] = t("sv")
            site("scalar", lambda e, sc=sc: e.activation(sc_E2s[sc], ps_b[0:16, 0:128], AF.Exp, scale=a_col),
                 waits=[("st", stv)], inc="ss")
            site("scalar", lambda e, sc=sc: e.activation(sc_dec2[sc], ps_b[0:16, 128:129], AF.Exp, scale=a_col), inc="ss")
            dec_s[sc] = t("ss")
            bset("ab1", "ss")
            sv_ = t("ss")
            site("vector", lambda e, sc=sc: e.tensor_tensor(Qt2[sc], sc_E2s[sc], sc_Cf2[sc], ALU.mult),
                 waits=[("ss", sv_)], inc="sv")
            site("vector", lambda e, sc=sc: e.tensor_scalar(sc_k0, ARB, sc_r[:, sc:sc + 1], None, ALU.mult),
                 waits=[("sv", t("sv"))], inc="sv")
            vv = t("sv")
            site("scalar", lambda e: e.activation(sc_k0, sc_k0, AF.Exp), waits=[("sv", vv)], inc="ss")
            sv_ = t("ss")
            site("vector", lambda e, sc=sc: e.tensor_tensor(Kt2[sc], sc_k0, ssm[:, sc * 34: sc * 34 + 16], ALU.mult),
                 waits=[("ss", sv_), ("sv", t("sv"))], inc="sv")
            kt_v[sc] = t("sv")
            pump()

        CBANKS = [("ab0", ps_a), ("ab1", ps_b), ("c0", ps_c[0]), ("c1", ps_c[1])]

        def emit_mblock(sc):
            dTt = sc_dT if sc == 0 else sc_dT1
            for n in range(16):
                bk, pb = CBANKS[n % 4]
                col = (n // 4) * 128
                site("tensor", lambda e, sc=sc, n=n, pb=pb, col=col: e.matmul(
                    pb[:, col:col + 128], ONESBF, cbflat2[sc][0:1, n * 128:(n + 1) * 128],
                    start=True, stop=True),
                    waits=[bfree(bk) if n < 4 else None, wlm(("cfl", sc))], inc="st")
                mark(("cbst", sc, n), "st")
            mark(("cbc", sc), "st")
            for n in range(16):
                ns = sc * 16 + n
                site("scalar", lambda e, sc=sc, n=n, dTt=dTt: e.activation(
                    sc_E[n % 2], dTt, AF.Exp, scale=ARB[:, n:n + 1]),
                    waits=[("sv", dT_v[sc]), wlm(("mteg", i, ns - 2))], inc="ss")
                mark(("exps", sc, n), "ss")

        def emit_mstt(sc, n):
            ns = sc * 16 + n
            bk, pb = CBANKS[n % 4]
            col = (n // 4) * 128
            site("vector", lambda e, sc=sc, n=n, pb=pb, col=col: e.scalar_tensor_tensor(
                Mt2[sc][:, n * 128:(n + 1) * 128], sc_E[n % 2],
                ssm[:, sc * 34 + n: sc * 34 + n + 1], pb[:, col:col + 128], ALU.mult, ALU.mult),
                waits=[wlm(("exps", sc, n)), wlm(("cbst", sc, n))], inc="sv")
            mark(("mteg", i, ns), "sv")
            mark(("mte", i, sc, n), "sv")
            if n >= 12:
                bset(CBANKS[n % 4][0], "sv")

        emit_mblock(0)
        for n in range(16):
            emit_mstt(0, n)
            if n % 4 == 3:
                pump()
        emit_mblock(1)

        def emit_intra_chunk(sc, interleave_next):
            hbf_w = wlm(("hbf",))
            for dc in range(16):
                bk = "y%d" % (dc % 2)
                pb = ps_y[dc % 2]
                for n in range(16):
                    wv = []
                    if n == 0:
                        wv = [hbf_w, bfree(bk), ("sv", xt_v[sc])]
                    wv.append(wlm(("mte", i, sc, n)))
                    site("tensor", lambda e, sc=sc, dc=dc, n=n, pb=pb: e.matmul(
                        pb[:, 0:128], Xt2[sc][:, dc * 128:(dc + 1) * 128],
                        Mt2[sc][:, n * 128:(n + 1) * 128], start=(n == 0), stop=False),
                        waits=wv, inc=None)
                site("tensor", lambda e, sc=sc, dc=dc, pb=pb: e.matmul(
                    pb[:, 0:128], h_bf[0:16, dc * 128:(dc + 1) * 128], Qt2[sc],
                    start=False, stop=True), inc="st")
                if dc == 15:
                    mark(("inter", i, sc), "st")
                stv = t("st")
                y1b = sc_y1 if dc % 2 == 0 else sc_y1b
                site("vector", lambda e, dc=dc, sc=sc, pb=pb, y1b=y1b: e.scalar_tensor_tensor(
                    y1b, xc[:, dc * 256 + sc * 128: dc * 256 + sc * 128 + 128],
                    DV[:, dc:dc + 1], pb[:, 0:128], ALU.mult, ALU.add),
                    waits=[("st", stv), wlm(("zse", i))], inc="sv")
                site("vector", lambda e, dc=dc, sc=sc, y1b=y1b: e.tensor_tensor(
                    x_main[:, dc * 259 + 3 + sc * 128: dc * 259 + 3 + sc * 128 + 128],
                    y1b, zs[:, dc * 256 + sc * 128: dc * 256 + sc * 128 + 128], ALU.mult),
                    waits=[("sv", t("sv"))], inc="sv")
                bset(bk, "sv")
                if interleave_next:
                    emit_mstt(1, dc)
                    if dc % 2 == 1:
                        pump()
                else:
                    pump()
            if sc == 1:
                mark(("gate", i), "sv")

        def emit_state(sc):
            for q in range(4):
                bk = "c%d" % (q % 2)
                pb = ps_c[q % 2]
                site("tensor", lambda e, sc=sc, q=q, pb=pb: e.matmul(
                    pb[0:16, 0:512], Kt2[sc], Xt2[sc][:, q * 512:(q + 1) * 512], start=True, stop=True),
                    waits=[("sv", kt_v[sc]), ("sv", xt_v[sc]), bfree(bk)], inc="st")
                stv = t("st")
                site("vector", lambda e, sc=sc, q=q, pb=pb: e.scalar_tensor_tensor(
                    h_T[0:16, q * 512:(q + 1) * 512], h_T[0:16, q * 512:(q + 1) * 512],
                    sc_dec2[sc], pb[0:16, 0:512], ALU.mult, ALU.add),
                    waits=[("st", stv), ("ss", dec_s[sc])], inc="sv")
                bset(bk, "sv")
            site("vector", lambda e: e.tensor_copy(h_bf, h_T),
                 waits=[wlm(("inter", i, sc)), ("sv", t("sv"))], inc="sv")
            mark(("hbf",), "sv")

        emit_intra_chunk(0, True)
        emit_state(0)
        emit_intra_chunk(1, False)
        emit_state(1)

        for o in range(8):
            _wm = "swo%d" % (o % 2)
            site("sync", lambda e, o=o: e.dma_start(
                out=WoB[o % 2].rearrange("p (c j) -> p c j", j=128),
                in_=d_wos[o].rearrange("c p j -> p c j")),
                waits=[wlm(("opg", i, o - 2)) or wlm(("opg", i - 1, o + 6)),
                       ("sgw0", sgw_tot[0]), ("sgw1", sgw_tot[1])], inc=_wm)
            wo_here = t(_wm)
            bk = "ab%d" % (o % 2)
            pb = ps_ab[o % 2]
            wv = [(_wm, wo_here), wlm(("gate", i)), bfree(bk)]
            for kc in range(16):
                site("tensor", lambda e, o=o, kc=kc, pb=pb: e.matmul(
                    pb[:, 0:256], WoB[o % 2][:, kc * 128:(kc + 1) * 128],
                    x_main[:, kc * 259 + 3: kc * 259 + 259],
                    start=(kc == 0), stop=(kc == 15)),
                    waits=wv if kc == 0 else (), inc="st" if kc == 15 else None)
            mark(("opg", i, o), "st")
            stv = t("st")
            site("scalar", lambda e, o=o, pb=pb: e.activation(
                ao[:, o * 256:(o + 1) * 256], pb[:, 0:256], AF.Identity, bias=B_O[:, o:o + 1]),
                waits=[("st", stv)], inc="ss")
            mark(("aoe", i, o), "ss")
            bset(bk, "ss")
            pump()
        mark(("outproj", i), "st")

        stv0 = stv1 = None
        for jj in range(2):
            bk = "t%d" % jj
            wv = [wlm(("aoe", i, 7)), bfree(bk)]
            for o in range(8):
                site("tensor", lambda e, jj=jj, o=o: e.transpose(
                    ps_t[jj][:, o * 128:(o + 1) * 128],
                    ao[:, o * 256 + jj * 128: o * 256 + jj * 128 + 128], IDBF),
                    waits=wv if o == 0 else (), inc="st" if o == 7 else None)
            if jj == 0:
                stv0 = t("st")
            else:
                mark(("aoT", i), "st")
                stv1 = t("st")

        pump_rest()
        lt = ln1x_tm2[i % 2]
        rr = [r_tm[:, 0:1024], r_tm[:, 1024:2048]]
        site("vector", lambda e, lt=lt, rr=rr: e.tensor_tensor(
            rr[0], ps_t[0][:, 0:1024], lt[:, 0:1024], ALU.add),
            waits=[("st", stv0)], inc="sv")
        site("vector", lambda e, lt=lt, rr=rr: e.tensor_tensor(
            rr[1], ps_t[1][:, 0:1024], lt[:, 1024:2048], ALU.add),
            waits=[("st", stv1)], inc="sv")
        bset("t0", "sv")
        bset("t1", "sv")
        if i + 1 < NTL:
            emit_lnT(i + 1)
        PUMPQ.append(make_epi(i))
    pump_rest()
    lastA_ip = lm[("ipg", NTL - 1)]
    gA = (t("sg0"), t("sg1"))

    for g in range(40):
        _sm = "sw%d" % (g % 2)
        site("sync", lambda e, g=g: e.dma_start(out=wa[:, g * 512:(g + 1) * 512],
                                                in_=d_wbs[:, g * 512:(g + 1) * 512]),
             waits=[("st", lastA_ip[1]), ("sgw0", sgw_tot[0]), ("sgw1", sgw_tot[1])], inc=_sm)
    wbs_w = (t("sw0"), t("sw1"))
    site("vector", lambda e: e.engine_nop(), waits=[("sw0", wbs_w[0]), ("sw1", wbs_w[1])], inc="sv")
    weightsB_v = t("sv")

    xhB = {}

    def emit_x2load(i):
        if i >= NTL:
            return
        xb = x_tm[i % 2]
        _gp = "sg%d" % (i % 2)
        _xp_ = "sx%d" % (i % 2)
        w = [("sg0", gA[0]), ("sg1", gA[1])]
        if i >= 2:
            w.append((_gp, t(_gp)))
        site("gpsimd", lambda e, i=i, xb=xb: e.dma_start(
            out=xb.rearrange("p (j d) -> p j d", d=1024),
            in_=d_x2[i * TT:(i + 1) * TT, :].rearrange("(j p) d -> p j d", p=128)),
            waits=w, inc=_xp_)
        xhB[i] = t(_xp_)

    def emit_ln3(i):
        xb = x_tm[i % 2]
        srcs = [xb[:, 0:1024], xb[:, 1024:2048]]

        def mknorm3(jj, src_):
            def f(e, mean, inv):
                return e.tensor_scalar(ln3_tm[:, jj * 1024:(jj + 1) * 1024], src_,
                                       mean, inv, ALU.subtract, ALU.mult)
            return f
        wpre = [("sx%d" % (i % 2), xhB[i]), ("sv", weightsB_v), wlm(("ln3u", i - 1))]
        ln_pair(srcs, [mknorm3(0, srcs[0]), mknorm3(1, srcs[1])], wpre, sb=2)
        mark(("ln3", i), "sv")

    def emit_lnT3(i):
        for jj in range(2):
            bk = "t%d" % jj
            wv = [wlm(("ln3", i)), bfree(bk)]
            for kc in range(8):
                site("tensor", lambda e, jj=jj, kc=kc: e.transpose(
                    ps_t[jj][:, kc * 128:(kc + 1) * 128],
                    ln3_tm[:, jj * 1024 + kc * 128: jj * 1024 + (kc + 1) * 128], IDBF),
                    waits=wv if kc == 0 else (), inc="st" if kc == 7 else None)
            stv = t("st")
            site("scalar", lambda e, jj=jj: e.copy(
                ab[:, 2048:4096].rearrange("p (k s) -> p k s", s=256)[:, :, jj * 128:(jj + 1) * 128],
                ps_t[jj][:, 0:1024].rearrange("p (k s) -> p k s", s=128)),
                waits=[("st", stv)], inc="ss")
            bset(bk, "ss")
        mark(("ln3fm", i), "ss")

    emit_x2load(0)
    emit_x2load(1)
    emit_ln3(0)
    emit_lnT3(0)

    for i in range(NTL):
        xb = x_tm[i % 2]

        BB4 = [("ab0", ps_a), ("ab1", ps_b), ("y0", ps_y[0]), ("y1", ps_y[1])]

        for r in range(2):
            bk, pb = BB4[r % 4]
            wv = [wlm(("ln3fm", i)), bfree(bk)]
            for kc in range(8):
                site("tensor", lambda e, r=r, kc=kc, pb=pb: e.matmul(
                    pb[:, 0:256], U1[:, kc * 256 + r * 128: kc * 256 + r * 128 + 128],
                    ab[:, 2048 + kc * 256: 2048 + (kc + 1) * 256],
                    start=(kc == 0), stop=(kc == 7)),
                    waits=wv if kc == 0 else (), inc="st" if kc == 7 else None)
            stv = t("st")
            site("scalar", lambda e, r=r, pb=pb: e.copy(
                Pf[:, r * 256:(r + 1) * 256], pb[:, 0:256]), waits=[("st", stv)], inc="ss")
            bset(bk, "ss")
        pf_s = t("ss")

        for f in range(32):
            bk, pb = BB4[f % 4]
            wv = [("ss", pf_s), bfree(bk)]
            for r in range(2):
                site("tensor", lambda e, f=f, r=r, pb=pb: e.matmul(
                    pb[:, 0:256], V1[:, r * 4096 + f * 128: r * 4096 + (f + 1) * 128],
                    Pf[:, r * 256:(r + 1) * 256], start=(r == 0), stop=(r == 1)),
                    waits=wv if r == 0 else (), inc="st" if r == 1 else None)
            stv = t("st")
            site("scalar", lambda e, f=f, pb=pb: e.activation(
                Hf[:, f * 256:(f + 1) * 256], pb[:, 0:256], AF.Gelu_apprx_tanh,
                bias=B1[:, f:f + 1]), waits=[("st", stv)], inc="ss")
            mark(("hfe", i, f), "ss")
            bset(bk, "ss")

        for r in range(2):
            bk, pb = BB4[r % 4]
            wv = [wlm(("hfe", i, 31)), bfree(bk)]
            for kc in range(32):
                site("tensor", lambda e, r=r, kc=kc, pb=pb: e.matmul(
                    pb[:, 0:256], U2[:, kc * 256 + r * 128: kc * 256 + r * 128 + 128],
                    Hf[:, kc * 256:(kc + 1) * 256], start=(kc == 0), stop=(kc == 31)),
                    waits=wv if kc == 0 else (), inc="st" if kc == 31 else None)
            stv = t("st")
            site("scalar", lambda e, r=r, pb=pb: e.copy(
                Qf[:, r * 256:(r + 1) * 256], pb[:, 0:256]), waits=[("st", stv)], inc="ss")
            bset(bk, "ss")
        qf_s = t("ss")

        for o in range(8):
            bk, pb = BB4[o % 4]
            wv = [("ss", qf_s), bfree(bk)]
            for r in range(2):
                site("tensor", lambda e, o=o, r=r, pb=pb: e.matmul(
                    pb[:, 0:256], V2[:, r * 1024 + o * 128: r * 1024 + (o + 1) * 128],
                    Qf[:, r * 256:(r + 1) * 256], start=(r == 0), stop=(r == 1)),
                    waits=wv if r == 0 else (), inc="st" if r == 1 else None)
            stv = t("st")
            site("scalar", lambda e, o=o, pb=pb: e.activation(
                fo[:, o * 256:(o + 1) * 256], pb[:, 0:256], AF.Identity, bias=B2[:, o:o + 1]),
                waits=[("st", stv)], inc="ss")
            mark(("foe", i, o), "ss")
            bset(bk, "ss")

        stv0 = stv1 = None
        for jj in range(2):
            bk = "t%d" % jj
            wv = [wlm(("foe", i, 7)), bfree(bk)]
            for o in range(8):
                site("tensor", lambda e, jj=jj, o=o: e.transpose(
                    ps_t[jj][:, o * 128:(o + 1) * 128],
                    fo[:, o * 256 + jj * 128: o * 256 + jj * 128 + 128], IDBF),
                    waits=wv if o == 0 else (), inc="st" if o == 7 else None)
            if jj == 0:
                stv0 = t("st")
            else:
                stv1 = t("st")
        site("vector", lambda e: e.tensor_copy(fo_tm[:, 0:1024], ps_t[0][:, 0:1024]),
             waits=[("st", stv0)], inc="sv")
        site("vector", lambda e: e.tensor_copy(fo_tm[:, 1024:2048], ps_t[1][:, 0:1024]),
             waits=[("st", stv1)], inc="sv")
        bset("t0", "sv")
        bset("t1", "sv")

        if i + 1 < NTL:
            emit_ln3(i + 1)
            emit_lnT3(i + 1)

        fos = [fo_tm[:, 0:1024], fo_tm[:, 1024:2048]]
        NRM = (r_tm[:, 0:1024], r_tm[:, 1024:2048])

        def mknorm4(jj, src_):
            def f(e, mean, inv):
                return e.tensor_scalar(NRM[jj][:, 0:1024], src_, mean, inv, ALU.subtract, ALU.mult)
            return f
        ln_pair(fos, [mknorm4(0, fos[0]), mknorm4(1, fos[1])], [("sv", t("sv"))], sb=0)
        for jj in (0, 1):
            site("vector", lambda e, jj=jj, xb=xb: e.tensor_tensor(
                xb[:, jj * 1024:(jj + 1) * 1024], xb[:, jj * 1024:(jj + 1) * 1024],
                NRM[jj][:, 0:1024], ALU.add), inc="sv")
        mark(("ln3u", i), "sv")
        vv = t("sv")
        site("gpsimd", lambda e, i=i, xb=xb: e.dma_start(
            out=d_out[i * TT:(i + 1) * TT, :].rearrange("(j p) d -> p j d", p=128),
            in_=xb.rearrange("p (j d) -> p j d", d=1024)),
            waits=[("sv", vv)], inc="sg%d" % (i % 2))
        emit_x2load(i + 2)

    site("gpsimd", lambda e: e.engine_nop(),
         waits=[("sg0", t("sg0")), ("sg1", t("sg1")), ("sgc", t("sgc"))])

    CONSTW = consts_w
    with nc.Block() as block:
        def mk(name):
            def f(e):
                seen = {}
                if name in ("tensor", "vector", "scalar", "gpsimd"):
                    e.wait_ge(SEM["swc"], CONSTW)
                    seen[id(SEM["swc"])] = CONSTW
                for (eng, fn, ws, inc) in sites:
                    if eng != name:
                        continue
                    for (sh, v) in ws:
                        if seen.get(id(sh), -1) < v:
                            e.wait_ge(sh, v)
                            seen[id(sh)] = v
                    inst = fn(e)
                    if inc is not None:
                        inst.then_inc(inc[0], inc[1])
            return f
        block.sync(mk("sync"))
        block.tensor(mk("tensor"))
        block.vector(mk("vector"))
        block.scalar(mk("scalar"))
        block.gpsimd(mk("gpsimd"))
    return nc


def _host_inputs(inputs):
    f32 = np.float32
    A = np.asarray(inputs["L0_attn_A_log"], f32)
    if not (A == A[0]).all():
        raise RuntimeError("A_log rows differ; general-A fallback not implemented")
    conv_w = np.asarray(inputs["L0_attn_conv_w"], f32)
    ident = np.eye(128, dtype=f32)
    u = np.arange(128)
    return {
        "w_inT": np.ascontiguousarray(np.asarray(inputs["L0_attn_in_proj_w"], f32).T),
        "b_in": np.ascontiguousarray(np.asarray(inputs["L0_attn_in_proj_b"], f32).reshape(32, 128).T),
        "convw": np.ascontiguousarray(conv_w[:, 0, :].reshape(16, 128, 4).transpose(1, 0, 2).reshape(128, 64)),
        "conv_b": np.ascontiguousarray(np.asarray(inputs["L0_attn_conv_b"], f32).reshape(16, 128).T),
        "xprojT": np.ascontiguousarray(np.asarray(inputs["L0_attn_x_proj_w"], f32).T),
        "xproj_b": np.ascontiguousarray(np.asarray(inputs["L0_attn_x_proj_b"], f32).reshape(33, 1)),
        "alog_row": np.ascontiguousarray(A[0:1, :]),
        "dvec": np.ascontiguousarray(np.asarray(inputs["L0_attn_D"], f32).reshape(16, 128).T),
        "w_o_omaj": np.ascontiguousarray(
            np.asarray(inputs["L0_attn_out_proj_w"], f32).reshape(8, 128, 16, 128).transpose(0, 2, 3, 1)),
        "b_out": np.ascontiguousarray(np.asarray(inputs["L0_attn_out_proj_b"], f32).reshape(8, 128).T),
        "gamma_bc": np.ascontiguousarray(np.tile(np.asarray(inputs["L0_attn_ln_gamma"], f32)[None, :], (128, 1))),
        "beta_bc": np.ascontiguousarray(np.tile(np.asarray(inputs["L0_attn_ln_beta"], f32)[None, :], (128, 1))),
        "u1": np.ascontiguousarray(np.asarray(inputs["L0_ffn_U1"], f32)),
        "v1": np.ascontiguousarray(np.asarray(inputs["L0_ffn_V1"], f32)),
        "b1": np.ascontiguousarray(np.asarray(inputs["L0_ffn_b1"], f32).reshape(32, 128).T),
        "u2": np.ascontiguousarray(np.asarray(inputs["L0_ffn_U2"], f32)),
        "v2": np.ascontiguousarray(np.asarray(inputs["L0_ffn_V2"], f32)),
        "b2": np.ascontiguousarray(np.asarray(inputs["L0_ffn_b2"], f32).reshape(8, 128).T),
        "ident32": ident,
        "identbf": ident.astype(BF),
        "tri": np.ascontiguousarray((u[:, None] <= u[None, :]).astype(f32)),
        "tris": np.ascontiguousarray((u[:, None] > u[None, :]).astype(f32)),
        "umask": np.ascontiguousarray((LARGE * (u[:, None] > u[None, :])).astype(f32)),
        "onesrow": np.ones((1, 128), f32),
        "epsv": np.full((128, 1), EPS, f32),
        "onesrowbf": np.ones((1, 128), BF),
    }


def kernel(**inputs):
    x = np.asarray(inputs["L0_attn_x"], np.float32)
    base = _host_inputs(inputs)
    nc = build_program()
    in_maps = [dict(base, x=np.ascontiguousarray(x[b])) for b in range(8)]
    res = run_bass_kernel_spmd(nc, in_maps, core_ids=list(range(8)))
    out = np.stack([np.asarray(res.results[b]["out"]) for b in range(8)], axis=0)
    return out.astype(np.float32)
